# version 09
# baseline (speedup 1.0000x reference)
import numpy as np
import ml_dtypes
import concourse.bass as bass
import concourse.mybir as mybir
from concourse.bass_utils import run_bass_kernel_spmd

F32 = mybir.dt.float32
BF16 = mybir.dt.bfloat16
AF = mybir.ActivationFunctionType
ALU = mybir.AluOpType
AX = mybir.AxisListType

D = 1024
NCH = 8
N = 2048
SUP = 256
NSUP = N // SUP
HB = 16
HD = 64
HC = 8
WIN = 512
NW = N // WIN
TW = WIN // 128
HID = 2730
FFI = 1536
NEG = -1.0e30
OS_SIZES = [128] * (HID // 128) + ([HID % 128] if HID % 128 else [])
NOS = len(OS_SIZES)

def _os_ranges():
    o0, out = 0, []
    for w in OS_SIZES:
        out.append((o0, w))
        o0 += w
    return out

OSR = _os_ranges()

PVIDX = {}
_pc = 0
def _pvadd(name, n):
    global _pc
    PVIDX[name] = _pc
    _pc += n
_pvadd("eps", 1)
_pvadd("b_g1", 1)
_pvadd("b_g2", 8)
_pvadd("gamma", 8)
_pvadd("beta", 8)
for _l in range(2):
    _pvadd(f"ln_g{_l}", 8)
    _pvadd(f"ln_b{_l}", 8)
    _pvadd(f"b2_{_l}", 8)
for _l in range(2):
    _pvadd(f"b1a_{_l}", NOS)
    _pvadd(f"b1g_{_l}", NOS)
PV = _pc


def _bf(x):
    return np.ascontiguousarray(np.asarray(x)).astype(ml_dtypes.bfloat16)


def _host_constants():
    half_idx = np.arange(0, HD, 2, dtype=np.float64)
    inv_freq = 1.0 / 10000.0 ** (half_idx / HD)
    t = np.arange(N, dtype=np.float64)
    freqs = t[:, None] * inv_freq[None, :]
    freqs = np.concatenate([freqs, freqs], axis=-1)
    base = (half_idx + 0.4 * HD) / (1.4 * HD)
    power = (t - N // 2) / float(WIN // 2)
    scale = base[None, :] ** power[:, None]
    scale = np.concatenate([scale, scale], axis=-1)
    cs, sn = np.cos(freqs), np.sin(freqs)
    t64 = [cs * scale * 8.0, sn * scale * 8.0, cs / scale, sn / scale]

    def wtile(a):
        return np.ascontiguousarray(
            a.reshape(NW, TW, 128, HD).transpose(0, 2, 1, 3).reshape(NW, 128, TW * HD))

    hi = [a.astype(np.float32) for a in t64]
    lo = [(a - h.astype(np.float64)).astype(np.float32) for a, h in zip(t64, hi)]
    tabs = np.stack([wtile(h) for h in hi])
    tabslo = np.stack([wtile(x) for x in lo])
    ii = np.arange(128)
    ml = np.where(ii[None, :] < ii[:, None], NEG, 0.0).astype(np.float32)
    mh = np.where(ii[None, :] > ii[:, None], NEG, 0.0).astype(np.float32)
    triu_keep = (ii[:, None] <= ii[None, :]).astype(np.float32)
    maskb = np.concatenate([triu_keep, np.ones((128, 128), np.float32), triu_keep], 1)
    eye = np.eye(128, dtype=np.float32)
    ones = np.ones((128, 128), np.float32)
    return dict(tabs=tabs, tabslo=tabslo, ml=_bf(ml), mh=_bf(mh), maskb=_bf(maskb),
                eye_f=eye, eye_b=_bf(eye), ones_f=ones, ones_b=_bf(ones))


def _pack_pvec(ins):
    cols = np.zeros((128, PV), np.float32)
    def put(name, k, v):
        cols[: v.shape[0], PVIDX[name] + k] = v
    put("eps", 0, np.full((128,), 1e-5, np.float32))
    put("b_g1", 0, np.asarray(ins["L0_attn_b_g1"], np.float32))
    for c in range(NCH):
        put("b_g2", c, np.asarray(ins["L0_attn_b_g2"], np.float32)[c * 128:(c + 1) * 128])
        put("gamma", c, np.asarray(ins["L0_attn_gamma"], np.float32)[c * 128:(c + 1) * 128])
        put("beta", c, np.asarray(ins["L0_attn_beta"], np.float32)[c * 128:(c + 1) * 128])
    for l in range(2):
        for c in range(NCH):
            put(f"ln_g{l}", c, np.asarray(ins["L0_ffn_ln_g"], np.float32)[l, c * 128:(c + 1) * 128])
            put(f"ln_b{l}", c, np.asarray(ins["L0_ffn_ln_b"], np.float32)[l, c * 128:(c + 1) * 128])
            put(f"b2_{l}", c, np.asarray(ins["L0_ffn_b2"], np.float32)[l, c * 128:(c + 1) * 128])
        b1 = np.asarray(ins["L0_ffn_b1"], np.float32)[l]
        for k, (o0, ow) in enumerate(OSR):
            put(f"b1a_{l}", k, b1[o0:o0 + ow])
            put(f"b1g_{l}", k, b1[HID + o0:HID + o0 + ow])
    return cols



class Buf:
    __slots__ = ("w", "rs", "name")

    def __init__(self, name=""):
        self.w = {}
        self.rs = {}
        self.name = name

    def toks_w(self):
        return list(self.w.items())

    def toks_rw(self):
        out = dict(self.w)
        for s, v in self.rs.items():
            out[s] = max(out.get(s, 0), v)
        return list(out.items())

    def add_read(self, tok):
        s, v = tok
        self.rs[s] = max(self.rs.get(s, 0), v)

    def set_write(self, tok):
        s, v = tok
        self.w = {s: v}
        self.rs = {}


class KB:
    ENGS = ("T", "V", "S", "Q", "G")

    def __init__(self):
        self.nc = bass.Bass("TRN2", target_bir_lowering=False)
        self.prog = {e: [] for e in self.ENGS}
        self.cnt = {"T": 0, "V": 0, "S": 0, "Q": 0, "G": 0}
        self.wm = {e: {} for e in self.ENGS}
        self.n_wait = 0
        self.n_inst = 0

    def build_resources(self):
        import contextlib
        self.stack = contextlib.ExitStack()
        ent = self.stack.enter_context
        self.sem = {k: ent(self.nc.semaphore(f"sem_{k}"))
                    for k in ("T", "V", "S", "Q", "G")}
        return ent

    def dma_sem(self, name):
        if name not in self.sem:
            self.sem[name] = self.stack.enter_context(
                self.nc.semaphore(f"sem_{name}"))
            self.cnt[name] = 0
        return self.sem[name]

    def op(self, eng, fn, reads=(), writes=(), deps=(), dma=False):
        need = {}
        for b in reads:
            for s, v in b.toks_w():
                need[s] = max(need.get(s, 0), v)
        for b in writes:
            for s, v in b.toks_rw():
                need[s] = max(need.get(s, 0), v)
        for s, v in deps:
            need[s] = max(need.get(s, 0), v)
        wmm = self.wm[eng]
        waits = []
        for s, v in sorted(need.items()):
            if wmm.get(s, 0) < v:
                waits.append((s, v))
                wmm[s] = v
        if dma:
            kind = dma
            self.dma_sem(kind)
            inc = 16
        else:
            kind, inc = eng, 1
        self.cnt[kind] += inc
        tok = (kind, self.cnt[kind])
        self.n_wait += len(waits)
        self.n_inst += 1
        self.prog[eng].append((waits, fn, kind, inc))
        for b in reads:
            b.add_read(tok)
        for b in writes:
            b.set_write(tok)
        return tok

    def snapshot(self):
        return [(k, v) for k, v in self.cnt.items() if v > 0]

    def barrier(self):
        toks = self.snapshot()
        for e in self.ENGS:
            self.op(e, lambda en: en.nop(), deps=toks)

    def finalize_blocks(self):
        sem, prog, cnt = self.sem, self.prog, self.cnt
        engmap = {"T": "tensor", "V": "vector", "S": "scalar", "Q": "sync",
                  "G": "gpsimd"}
        with self.nc.Block() as block:
            for e in self.ENGS:
                def mk(e):
                    def body(eng):
                        for waits, fn, kind, inc in prog[e]:
                            for s, v in waits:
                                eng.wait_ge(sem[s], v)
                            fn(eng).then_inc(sem[kind], inc)
                        if e == "Q":
                            for k, v in cnt.items():
                                if v and k != "Q":
                                    eng.wait_ge(sem[k], v)
                    return body
                getattr(block, engmap[e])(mk(e))



def build_program(debug=False):
    kb = KB()
    nc = kb.nc
    ent = kb.build_resources()
    tap_t = {}

    def din(name, shape, dt=F32):
        return nc.dram_tensor(name, list(shape), dt, kind="ExternalInput").ap()

    xT = din("xT", [D, N])
    wqkvT = din("wqkvT", [D, 3 * D])
    wg1T = din("wg1T", [D, HD])
    wg2T = din("wg2T", [HD, D])
    woutT = din("woutT", [D, D])
    fqkv = din("fqkv", [2, D, FFI])
    fwo = din("fwo", [2, 512, D])
    fw1 = din("fw1", [2, D, 2 * HID])
    fw2 = din("fw2", [2, HID, D])
    qscaleD = din("qscale", [2, 4, 128, HD])
    pvecD = din("pvec", [128, PV])
    tabsD = din("tabs", [4, NW, 128, TW * HD])
    tabsloD = din("tabslo", [4, NW, 128, TW * HD])
    mlD = din("ml", [128, 128], BF16)
    mhD = din("mh", [128, 128], BF16)
    maskbD = din("maskb", [128, 384], BF16)
    eyefD = din("eye_f", [128, 128])
    eyebD = din("eye_b", [128, 128], BF16)
    onesfD = din("ones_f", [128, 128])
    onesbD = din("ones_b", [128, 128], BF16)

    outT = nc.dram_tensor("outT", [D, N], F32, kind="ExternalOutput").ap()
    x2s = nc.dram_tensor("x2s", [D, N], F32).ap()

    sF = ent(nc.sbuf_tensor("sF", [128, NCH, N + 1], F32))

    fplan, bplan = {}, {}
    _fc = [0]
    def fslot(name, cols):
        fplan[name] = (_fc[0], cols)
        _fc[0] += cols
    fslot("STAT", 1024)
    fslot("SMALL", 128)
    fslot("GATE", 2048)
    fslot("SCR", 1536)
    fslot("WST", 2048)
    fslot("TAB", 1024)
    fslot("CONST", 448)
    fslot("YBND", 32)
    fslot("S32", 512)
    AFt = ent(nc.sbuf_tensor("AFa", [128, _fc[0]], F32))

    _bc = [0]
    def bslot(name, cols):
        bplan[name] = (_bc[0], cols)
        _bc[0] += cols
    bslot("YBF", 2064)
    bslot("QF", 2048)
    bslot("KF", 4096)
    bslot("VT", 4096)
    bslot("KST", 2048)
    bslot("ATM", 1536)
    bslot("OF", 2048)
    bslot("SBF", 512)
    bslot("WA", 24576)
    bslot("WB", 8192)
    bslot("WG", 1536)
    bslot("CONB", 896)
    ABt = ent(nc.sbuf_tensor("ABa", [128, _bc[0]], BF16))

    total_bytes = (NCH * (N + 1)) * 4 + _fc[0] * 4 + _bc[0] * 2
    assert total_bytes <= 209500, total_bytes

    PA = ent(nc.psum_tensor("PA", [128, 1024], F32))
    PC = ent(nc.psum_tensor("PC", [128, 512], F32))
    PD = ent(nc.psum_tensor("PD", [128, 512], F32))
    PE = ent(nc.psum_tensor("PE", [128, 512], F32))
    PB = ent(nc.psum_tensor("PB", [128, 512], F32))
    PS = ent(nc.psum_tensor("PS", [128, 512], F32))
    PT = ent(nc.psum_tensor("PT", [128, 1024], BF16))

    def fv(name, cols=None, off=0):
        o, c = fplan[name]
        if cols is None:
            cols = c - off
        assert off >= 0 and off + cols <= c, (name, off, cols)
        return AFt[:, o + off:o + off + cols]

    def bv(name, cols=None, off=0):
        o, c = bplan[name]
        if cols is None:
            cols = c - off
        assert off >= 0 and off + cols <= c, (name, off, cols)
        return ABt[:, o + off:o + off + cols]

    bufs = {}
    def B(name):
        if name not in bufs:
            bufs[name] = Buf(name)
        return bufs[name]

    SFB = [B(f"sF{s}") for s in range(NSUP)]

    def sf_bufs(c0, c1):
        lo = max(0, (c0 - 1)) // SUP
        hi = min(NSUP - 1, max(0, (c1 - 2)) // SUP)
        return [SFB[i] for i in range(lo, hi + 1)]

    PAB0, PAB1 = B("PA0"), B("PA1")
    PCB, PDB, PEB, PBB, PSB, PTB = (B(x) for x in ("PCp", "PDp", "PEp", "PBp", "PSp", "PTp"))

    op = kb.op

    def mm(pbuf, out_ap, lhsT_ap, rhs_ap, rbufs, start, stop, deps=()):
        return op("T", lambda e, o=out_ap, l=lhsT_ap, r=rhs_ap, st=start, sp=stop:
                  e.matmul(o, l, r, start=st, stop=sp),
                  reads=list(rbufs), writes=[pbuf], deps=deps)

    def trans(pbuf, out_ap, in_ap, ident_ap, rbufs, start=True, stop=True):
        return op("T", lambda e, o=out_ap, i=in_ap, d=ident_ap, st=start, sp=stop:
                  e.matmul(o, i, d, start=st, stop=sp, is_transpose=True),
                  reads=list(rbufs), writes=[pbuf])

    def vop(fn, reads=(), writes=(), deps=()):
        return op("V", fn, reads=reads, writes=writes, deps=deps)

    def sop(fn, reads=(), writes=(), deps=()):
        return op("S", fn, reads=reads, writes=writes, deps=deps)

    def dmaQ(sem, out_ap, in_ap, reads=(), writes=()):
        return op("Q", lambda e, o=out_ap, i=in_ap: e.dma_start(out=o, in_=i),
                  reads=reads, writes=writes, dma=sem)

    def dmaG(sem, out_ap, in_ap, reads=(), writes=()):
        return op("G", lambda e, o=out_ap, i=in_ap: e.dma_start(out=o, in_=i),
                  reads=reads, writes=writes, dma=sem)

    def jwarm(n):
        for _ in range(n):
            op("T", lambda e: e.matmul(PA[:, 512:1024], eye_b, warm_rhs,
                                       start=True, stop=True))

    CONSTF, CONSTB, INB = B("CONSTF"), B("CONSTB"), B("INCHAIN")
    eye_f = fv("CONST", 128, 0)
    ones_f = fv("CONST", 128, 128)
    pvecS = fv("CONST", PV, 256)
    eye_b = bv("CONB", 128, 0)
    ones_b = bv("CONB", 128, 128)
    ml_b = bv("CONB", 128, 256)
    mh_b = bv("CONB", 128, 384)
    maskb_b = bv("CONB", 384, 512)
    dmaG("IN", eye_f, eyefD, writes=[CONSTF, INB])
    dmaG("IN", ones_f, onesfD, writes=[CONSTF, INB])
    dmaG("IN", pvecS, pvecD, writes=[CONSTF, INB])
    dmaG("IN", eye_b, eyebD, writes=[CONSTB, INB])
    dmaG("IN", ones_b, onesbD, writes=[CONSTB, INB])
    dmaG("IN", ml_b, mlD, writes=[CONSTB, INB])
    dmaG("IN", mh_b, mhD, writes=[CONSTB, INB])
    dmaG("IN", maskb_b, maskbD, writes=[CONSTB, INB])

    def pvcol(name, k=0, rows=128):
        return pvecS[0:rows, PVIDX[name] + k: PVIDX[name] + k + 1]

    def tap(name, ap, rbufs, rows=128):
        if not debug:
            return
        shp = [rows] + [int(d) for d in ap.shape[1:]]
        t = nc.dram_tensor("tap_" + name, shp, ap.dtype, kind="ExternalOutput").ap()
        tap_t[name] = shp
        dmaG("TAPS", t, ap, reads=list(rbufs))

    def tap_sf(name):
        if not debug:
            return
        t = nc.dram_tensor("tap_" + name, [D, N], F32, kind="ExternalOutput").ap()
        tap_t[name] = (D, N)
        for c in range(NCH):
            dmaG("TAPS", t[c * 128:(c + 1) * 128, :], sF[:, c, 1:N + 1],
                 reads=list(SFB))

    warm_rhs = bv("CONB", 512, 256)
    for i in range(24):
        op("T", lambda e: e.matmul(PB[:, 0:512], eye_b, warm_rhs,
                                   start=True, stop=True),
           reads=[CONSTB], writes=[PBB])

    for c in range(NCH):
        dmaG("IN", sF[:, c, 1:N + 1], xT[c * 128:(c + 1) * 128, :],
             writes=list(SFB) + [INB])
    vop(lambda e: e.memset(sF[:, :, 0:1], 0.0), writes=[SFB[0]])

    WST = [B("WST0"), B("WST1")]
    _wi = [0]

    def cast_w(dst_ap, dst_buf, src_ap, rows, cols, eng="S"):
        done = 0
        while done < cols:
            w = min(1024, cols - done)
            i = _wi[0] % 2
            _wi[0] += 1
            stg = fv("WST", w, 1024 * i)[0:rows, :]
            dmaQ(f"WST{i}", stg, src_ap[:, done:done + w], writes=[WST[i]])
            dpc = dst_ap[0:rows, done:done + w]
            if eng == "S":
                sop(lambda e, d=dpc, g=stg: e.copy(d, g),
                    reads=[WST[i]], writes=[dst_buf])
            elif eng == "G":
                op("G", lambda e, d=dpc, g=stg: e.tensor_copy(d, g),
                   reads=[WST[i]], writes=[dst_buf])
            else:
                vop(lambda e, d=dpc, g=stg: e.tensor_copy(d, g),
                    reads=[WST[i]], writes=[dst_buf])
            done += w

    rs_row = fv("STAT", 512, 0)
    c0_row = fv("STAT", 512, 512)
    nmld_row = fv("STAT", 1024, 0)
    STATB = B("STATB")
    SCRB = B("SCRB")
    GATEB = B("GATEB")
    SMALLB = B("SMALLB")
    sq_scr = fv("SCR", 1024, 512)

    def stat_finish(width):
        jwarm(6)
        mu = sq_scr[0:1, 0:width]
        sop(lambda e, w=width: e.activation(mu[0:1, 0:w], PE[0:1, 0:w], AF.Copy,
                                            scale=1.0 / D),
            reads=[PEB], writes=[SCRB])
        vop(lambda e, w=width: e.scalar_tensor_tensor(c0_row[0:1, 0:w], mu[0:1, 0:w],
                                                      1.0, mu[0:1, 0:w],
                                                      ALU.mult, ALU.mult),
            reads=[SCRB], writes=[STATB])
        vop(lambda e, w=width: e.scalar_tensor_tensor(rs_row[0:1, 0:w], PB[0:1, 0:w],
                                                      1.0 / D, c0_row[0:1, 0:w],
                                                      ALU.mult, ALU.subtract),
            reads=[PBB, STATB], writes=[STATB])
        sop(lambda e, w=width: e.activation(rs_row[0:1, 0:w], rs_row[0:1, 0:w],
                                            AF.Ln, bias=pvcol("eps", rows=1)),
            reads=[STATB, CONSTF], writes=[STATB])
        sop(lambda e, w=width: e.activation(rs_row[0:1, 0:w], rs_row[0:1, 0:w],
                                            AF.Exp, scale=-0.5),
            reads=[STATB], writes=[STATB])
        vop(lambda e, w=width: e.scalar_tensor_tensor(c0_row[0:1, 0:w], mu[0:1, 0:w],
                                                      -1.0, rs_row[0:1, 0:w],
                                                      ALU.mult, ALU.mult),
            reads=[SCRB, STATB], writes=[STATB])

    def ln_stats_sf(c0, width, shifted=False):
        for c in range(NCH):
            off = -1 if (shifted and c >= 4) else 0
            ap = sF[:, c, c0 + off: c0 + off + width]
            rb = sf_bufs(c0 + off, c0 + off + width)
            mm(PEB, PE[0:1, 0:width], ones_f[:, 0:1], ap, [CONSTF] + rb,
               start=(c == 0), stop=(c == NCH - 1))
        for c in range(NCH):
            off = -1 if (shifted and c >= 4) else 0
            ap = sF[:, c, c0 + off: c0 + off + width]
            rb = sf_bufs(c0 + off, c0 + off + width)
            sop(lambda e, a=ap, w=width: e.activation(sq_scr[:, 0:w], a, AF.Square),
                reads=rb, writes=[SCRB])
            if c < 2:
                jwarm(1)
            mm(PBB, PB[0:1, 0:width], ones_f[:, 0:1], sq_scr[:, 0:width],
               [CONSTF, SCRB], start=(c == 0), stop=(c == NCH - 1))
        stat_finish(width)

    def bcast_rows(width):
        mm(PCB, PC[:, 0:width], ones_f[0:1, 0:128], rs_row[0:1, 0:width],
           [CONSTF, STATB], start=True, stop=True)
        mm(PDB, PD[:, 0:width], ones_f[0:1, 0:128], c0_row[0:1, 0:width],
           [CONSTF, STATB], start=True, stop=True)

    WAB, WBB, WGB, SBFB = B("WAB"), B("WBB"), B("WGB"), B("SBFB")
    wqkv_b = bv("WA", 24576).rearrange("p (c o) -> p c o", c=NCH)
    wout_b = bv("WB", 8192).rearrange("p (c o) -> p c o", c=NCH)
    wg1_b = bv("WG", 512, 0).rearrange("p (c o) -> p c o", c=NCH)
    wg2_b = bv("WG", 1024, 512)

    for c in range(NCH):
        cast_w(wqkv_b[:, c, :], WAB, wqkvT[c * 128:(c + 1) * 128, :], 128, 3 * D,
               eng="G")
        cast_w(wout_b[:, c, :], WBB, woutT[c * 128:(c + 1) * 128, :], 128, D,
               eng="G")
        cast_w(wg1_b[:, c, :], WGB, wg1T[c * 128:(c + 1) * 128, :], 128, HD)
    cast_w(wg2_b, WGB, wg2T, 64, D)

    sbf = bv("SBF", 512)
    s32 = fv("S32", 512)
    S32B = B("S32B")
    vop(lambda e: e.memset(sbf[:], 0.0), writes=[SBFB])
    op("T", lambda e: e.matmul(PA[:, 512:1024], eye_b, warm_rhs,
                               start=True, stop=True), reads=[INB, CONSTB])
    jwarm(16)

    h1 = bv("YBF", 2048).rearrange("p (c t) -> p c t", c=NCH)
    H1B = B("H1B")
    qf = bv("QF", 2048).rearrange("p (c t) -> p c t", c=NCH)
    kf = bv("KF", 2048).rearrange("p (c t) -> p c t", c=NCH)
    vt = bv("VT", 2048).rearrange("p (t o) -> p t o", t=2)
    kst = bv("KST", 2048).rearrange("p (t o) -> p t o", t=2)
    QFB, KFB, VTB, KSTB = B("QFB"), B("KFB"), B("VTB"), B("KSTB")
    atm = bv("ATM", 768).rearrange("p (i c) -> p i c", i=2)
    ATMB = [B("ATM0"), B("ATM1")]
    g1f = bv("ATM", 256, 768)
    G1B = B("G1B")
    gate = fv("GATE", 2048).rearrange("p (c t) -> p c t", c=NCH)
    oln = bv("OF", 2048).rearrange("p (c t) -> p c t", c=NCH)
    OLNB = B("OLNB")
    scr = fv("SCR", 1024, 0)

    ps_open = [False]

    for s in range(NSUP):
        c0 = 1 + s * SUP
        sfb = [SFB[s]]

        ln_stats_sf(c0, SUP)
        bcast_rows(SUP)
        jwarm(3)
        for c in range(NCH):
            vop(lambda e, c=c, c0=c0: e.tensor_tensor(scr[:, 0:SUP],
                                                      sF[:, c, c0:c0 + SUP],
                                                      PC[:, 0:SUP], ALU.mult),
                reads=sfb + [PCB], writes=[SCRB])
            vop(lambda e, c=c: e.tensor_tensor(h1[:, c, :], scr[:, 0:SUP],
                                               PD[:, 0:SUP], ALU.add),
                reads=[SCRB, PDB], writes=[H1B])

        if s == 0:
            tap("h1", h1, [B("H1B")])
        for oc in range(16):
            pb, pbuf = (PC, PCB) if oc % 2 == 0 else (PD, PDB)
            for kc in range(NCH):
                mm(pbuf, pb[:, 0:SUP], wqkv_b[:, kc, oc * 128:(oc + 1) * 128],
                   h1[:, kc, :], [WAB, H1B], start=(kc == 0), stop=(kc == NCH - 1))
            dst, dbuf = (qf, QFB) if oc < 8 else (kf, KFB)
            sop(lambda e, d=dst[:, oc % 8, :], p=pb: e.activation(d, p[:, 0:SUP],
                                                                  AF.Silu),
                reads=[pbuf], writes=[dbuf])
        for tt in range(2):
            pb, pbuf = (PC, PCB) if tt == 0 else (PD, PDB)
            for half in range(2):
                for kc in range(NCH):
                    mm(pbuf, pb[:, 0:512], h1[:, kc, tt * 128:(tt + 1) * 128],
                       wqkv_b[:, kc, 2048 + half * 512: 2048 + (half + 1) * 512],
                       [WAB, H1B], start=(kc == 0), stop=(kc == NCH - 1))
                sop(lambda e, t=tt, h=half, p=pb: e.activation(
                        vt[:, t, h * 512:(h + 1) * 512], p[:, 0:512], AF.Copy),
                    reads=[pbuf], writes=[VTB])
        for tt in range(2):
            for c in range(NCH):
                trans(PTB, PT[:, c * 128:(c + 1) * 128],
                      kf[:, c, tt * 128:(tt + 1) * 128], eye_b, [KFB, CONSTB],
                      start=(c == 0), stop=(c == NCH - 1))
            vop(lambda e, t=tt: e.tensor_copy(kst[:, t, :], PT[:]),
                reads=[PTB], writes=[KSTB])

        if s == 0:
            tap("qf", qf, [QFB])
            tap("kf", kf, [KFB])
            tap("vt", vt, [VTB])
            tap("kst", kst, [KSTB])
        for kc in range(NCH):
            mm(PEB, PE[0:64, 0:SUP], wg1_b[:, kc, :], h1[:, kc, :],
               [WGB, H1B], start=(kc == 0), stop=(kc == NCH - 1))
        sop(lambda e: e.activation(g1f[0:64, :], PE[0:64, 0:SUP], AF.Identity,
                                   bias=pvcol("b_g1", rows=64)),
            reads=[PEB, CONSTF], writes=[G1B])
        for dc in range(NCH):
            pb, pbuf = (PC, PCB) if dc % 2 == 0 else (PD, PDB)
            mm(pbuf, pb[:, 0:SUP], wg2_b[0:64, dc * 128:(dc + 1) * 128],
               g1f[0:64, :], [WGB, G1B], start=True, stop=True)
            sop(lambda e, d=dc, p=pb: e.activation(gate[:, d, :], p[:, 0:SUP],
                                                   AF.Sigmoid, bias=pvcol("b_g2", d)),
                reads=[pbuf, CONSTF], writes=[GATEB])

        if s == 0:
            tap("gate", gate, [GATEB])
        for pr in range(8):
            opb, opbuf = (PE, PEB) if pr % 2 == 0 else (PB, PBB)
            for hh in range(2):
                h = pr * 2 + hh
                jwarm(1)
                hc, hp = h // 2, (h % 2) * 64
                ab = ATMB[h % 2]
                mm(PAB0, PA[:, 0:256], kf[hp:hp + 64, hc, 0:128],
                   qf[hp:hp + 64, hc, 0:256], [KFB, QFB], start=True, stop=False)
                mm(PAB0, PA[:, 256:384], kf[hp:hp + 64, hc, 128:256],
                   qf[hp:hp + 64, hc, 128:256], [KFB, QFB], start=False, stop=True)
                vop(lambda e, i=h % 2: e.tensor_tensor(atm[:, i, :], PA[:, 0:384],
                                                       maskb_b, ALU.mult),
                    reads=[PAB0, CONSTB], writes=[ab])
                if s == 0 and h == 0:
                    tap("atm_h0", atm[:, 0, :], [ab])
                first_o = [True]
                def omm(lhs, rhs, rb, stop):
                    st = first_o[0]
                    first_o[0] = False
                    mm(opbuf, lhs[0], lhs[1], rhs, rb, start=st, stop=stop)
                if s > 0:
                    omm((opb[hp:hp + 64, 0:256],
                         sbf[hp:hp + 64, hc * 64:(hc + 1) * 64]),
                        qf[hp:hp + 64, hc, 0:256], [SBFB, QFB], False)
                omm((opb[hp:hp + 64, 0:256], vt[:, 0, h * 64:(h + 1) * 64]),
                    atm[:, h % 2, 0:256], [VTB, ab], False)
                omm((opb[hp:hp + 64, 128:256], vt[:, 1, h * 64:(h + 1) * 64]),
                    atm[:, h % 2, 256:384], [VTB, ab], stop=True)
                mm(PSB, PS[hp:hp + 64, hc * 64:(hc + 1) * 64],
                   kst[:, 0, h * 64:(h + 1) * 64], vt[:, 0, h * 64:(h + 1) * 64],
                   [KSTB, VTB], start=True, stop=False)
                mm(PSB, PS[hp:hp + 64, hc * 64:(hc + 1) * 64],
                   kst[:, 1, h * 64:(h + 1) * 64], vt[:, 1, h * 64:(h + 1) * 64],
                   [KSTB, VTB], start=False, stop=True)
            vop(lambda e, pr=pr, p=opb: e.tensor_tensor(gate[:, pr, :],
                                                        p[:, 0:256],
                                                        gate[:, pr, :], ALU.mult),
                reads=[opbuf, GATEB], writes=[GATEB])
        if s == 0:
            vop(lambda e: e.tensor_copy(s32[:], PS[:]), reads=[PSB], writes=[S32B])
        else:
            vop(lambda e: e.tensor_tensor(s32[:], PS[:], s32[:], ALU.add),
                reads=[PSB, S32B], writes=[S32B])
        if s < NSUP - 1:
            sop(lambda e: e.copy(sbf[:], s32[:]), reads=[S32B], writes=[SBFB])

        if s <= 1:
            tap(f"og{s}", gate, [GATEB])
            tap(f"sbf{s}", sbf, [SBFB])
        for c in range(NCH):
            mm(PEB, PE[0:1, 0:SUP], ones_f[:, 0:1], gate[:, c, :],
               [CONSTF, GATEB], start=(c == 0), stop=(c == NCH - 1))
        for c in range(NCH):
            sop(lambda e, c=c: e.activation(sq_scr[:, 0:SUP], gate[:, c, :],
                                            AF.Square),
                reads=[GATEB], writes=[SCRB])
            if c < 2:
                jwarm(1)
            mm(PBB, PB[0:1, 0:SUP], ones_f[:, 0:1], sq_scr[:, 0:SUP],
               [CONSTF, SCRB], start=(c == 0), stop=(c == NCH - 1))
        stat_finish(SUP)
        bcast_rows(SUP)
        jwarm(3)
        for c in range(NCH):
            vop(lambda e, c=c: e.tensor_tensor(scr[:, 0:SUP], gate[:, c, :],
                                               PC[:, 0:SUP], ALU.mult),
                reads=[GATEB, PCB], writes=[SCRB])
            vop(lambda e: e.tensor_tensor(scr[:, 256:256 + SUP], scr[:, 0:SUP],
                                          PD[:, 0:SUP], ALU.add),
                reads=[PDB, SCRB], writes=[SCRB])
            sop(lambda e, c=c: e.activation(oln[:, c, :], scr[:, 256:256 + SUP],
                                            AF.Identity, bias=pvcol("beta", c),
                                            scale=pvcol("gamma", c)),
                reads=[SCRB, CONSTF], writes=[OLNB])

        if s == 0:
            tap("oln", oln, [OLNB])
        jwarm(3)
        for dc in range(NCH):
            pb, pbuf = (PC, PCB) if dc % 2 == 0 else (PD, PDB)
            for kc in range(NCH):
                mm(pbuf, pb[:, 0:SUP], wout_b[:, kc, dc * 128:(dc + 1) * 128],
                   oln[:, kc, :], [WBB, OLNB], start=(kc == 0), stop=(kc == NCH - 1))
            vop(lambda e, d=dc, p=pb, c0=c0: e.tensor_tensor(
                    sF[:, d, c0:c0 + SUP], p[:, 0:SUP], sF[:, d, c0:c0 + SUP],
                    ALU.add),
                reads=[pbuf] + sfb, writes=sfb)
            dmaG("SPILL", x2s[dc * 128:(dc + 1) * 128, s * SUP:(s + 1) * SUP],
                 sF[:, dc, c0:c0 + SUP], reads=sfb, writes=[B("X2SB")])

    kb.barrier()
    tap_sf("x2")

    for g in range(4):
        c0 = 1 + g * 512
        gb = sf_bufs(c0, c0 + 512)
        ln_stats_sf(c0, 512)
        bcast_rows(512)
        for c in range(NCH):
            vop(lambda e, c=c, c0=c0: e.tensor_tensor(scr[:, 0:512],
                                                      sF[:, c, c0:c0 + 512],
                                                      PC[:, 0:512], ALU.mult),
                reads=gb + [PCB], writes=[SCRB])
            vop(lambda e, c=c, c0=c0: e.tensor_tensor(sF[:, c, c0:c0 + 512],
                                                      scr[:, 0:512],
                                                      PD[:, 0:512], ALU.add),
                reads=[SCRB, PDB], writes=gb)

    kb.barrier()
    tap_sf("y0")

    YBND = B("YBNDB")
    ybnd = fv("YBND", 32).rearrange("p (c k) -> p c k", c=NCH)

    fqkv_b = bv("WA", 12288).rearrange("p (c o) -> p c o", c=NCH)
    fwo_b = bv("WA", 4096, 12288).rearrange("p (c o) -> p c o", c=4)
    FQB, FOB = B("FQB"), B("FOB")
    w1v = [bv("WA", 2048, i * 2048).rearrange("p (c o) -> p c o", c=NCH)
           for i in range(2)]
    W1B = [B("W1B0"), B("W1B1")]
    w2v = [bv("WB", 1024, i * 1024) for i in range(8)]
    W2B = [B(f"W2B{i}") for i in range(8)]

    qfc = bv("QF", 2048).rearrange("p (b t) -> p b t", b=4)
    kfc = bv("KF", 4096).rearrange("p (w b t) -> p w b t", w=2, b=4)
    vtc = bv("VT", 4096).rearrange("p (w b t) -> p w b t", w=2, b=4)
    QFCB = B("QFCB")
    KFCB = [B("KFC0"), B("KFC1")]
    VTCB = [B("VTC0"), B("VTC1")]
    pa_junk = bv("KST", 2048)
    PAJB = B("PAJB")
    ptile8 = bv("ATM", 1536).rearrange("p (i c) -> p i c", i=12)
    PTL8 = [B(f"PTL{i}") for i in range(12)]
    _pabk = [B("PABK0"), B("PABK1")]
    PASL = [_pabk[i // 4] for i in range(8)]
    bcastB = fv("GATE", 2048, 0)
    BCB = None
    ofc = bv("OF", 2048).rearrange("p (b t) -> p b t", b=4)
    OFCB = B("OFCB")
    ybf = bv("YBF", 2064).rearrange("p (i c k) -> p i c k", i=2, c=NCH)
    YBFB = [B("YBF0"), B("YBF1")]
    tabw = fv("TAB", 1024).rearrange("p (i k) -> p i k", i=4)
    TABB = B("TABB")
    qscale_t = fv("SCR", 256, 1024).rearrange("p (i k) -> p i k", i=4)
    QSB = B("QSB")
    scrA = fv("GATE", 512, 0)
    scrB2 = fv("GATE", 512, 512)
    scrC = fv("GATE", 512, 1024)
    scrD = fv("SCR", 512, 0)
    def nm_v(par):
        return fv("SMALL", 8, par * 48 + 0)
    def dsP_v(par):
        return fv("SMALL", 8, par * 48 + 8)
    def dsC_v(par):
        return fv("SMALL", 8, par * 48 + 16)
    def lnd_v(par):
        return fv("SMALL", 8, par * 48 + 24)
    def nmld_v(par):
        return fv("SMALL", 8, par * 48 + 32)
    def tmp1_v(par):
        return fv("SMALL", 8, par * 48 + 40)
    rn_t = fv("SMALL", 8, 96)
    X2R = [B("X2R0"), B("X2R1")]

    hg_regions = []
    for i in range(20):
        hg_regions.append(bv("WA", 1024, 4096 + i * 1024))
    hg_regions.append(bv("QF", 1024, 0))
    hg_regions.append(bv("QF", 1024, 1024))
    HGB = B("HGB")
    ln4f = ABt[:, bplan["KF"][0]: bplan["KF"][0] + 8192].rearrange(
        "p (c t) -> p c t", c=NCH)
    LN4B = B("LN4B")
    gel = fv("GATE", 1024, 0)

    def rotary_qk(l, w, tt, which, pt, ptb, kfw, kb_cur):
        csx = tabw[:, 2 * which, tt * 64:(tt + 1) * 64]
        snx = tabw[:, 2 * which + 1, tt * 64:(tt + 1) * 64]
        sop(lambda e, p=pt: e.activation(scrA[:, 0:512], p[:, 0:512], AF.Square),
            reads=[ptb], writes=[GATEB])
        vop(lambda e: e.reduce_sum(rn_t[:, 0:8],
                                   scrA[:, 0:512].rearrange("p (h d) -> p h d", h=HC),
                                   AX.X),
            reads=[GATEB], writes=[SMALLB])
        sop(lambda e: e.activation(rn_t[:, 0:8], rn_t[:, 0:8], AF.Sqrt),
            reads=[SMALLB], writes=[SMALLB])
        vop(lambda e: e.tensor_scalar_max(rn_t[:, 0:8], rn_t[:, 0:8], 1e-12),
            reads=[SMALLB], writes=[SMALLB])
        vop(lambda e: e.reciprocal(rn_t[:, 0:8], rn_t[:, 0:8]),
            reads=[SMALLB], writes=[SMALLB])
        vop(lambda e, p=pt: e.tensor_tensor(
                scrB2[:, 0:512].rearrange("p (h d) -> p h d", h=HC),
                p[:, 0:512].rearrange("p (h d) -> p h d", h=HC),
                rn_t[:, 0:8].unsqueeze(2).broadcast_to([128, HC, HD]), ALU.mult),
            reads=[ptb, SMALLB], writes=[GATEB])
        vop(lambda e, c=csx: e.tensor_tensor(
                scrA[:, 0:512].rearrange("p (h d) -> p h d", h=HC),
                scrB2[:, 0:512].rearrange("p (h d) -> p h d", h=HC),
                c.unsqueeze(1).broadcast_to([128, HC, HD]), ALU.mult),
            reads=[GATEB, TABB], writes=[GATEB])
        vop(lambda e, sx=snx: e.tensor_tensor(
                scrC[:, 0:512].rearrange("p (h d) -> p h d", h=HC)[:, :, 0:32],
                scrB2[:, 0:512].rearrange("p (h d) -> p h d", h=HC)[:, :, 32:64],
                sx[:, 0:32].unsqueeze(1).broadcast_to([128, HC, 32]), ALU.mult),
            reads=[GATEB, TABB], writes=[GATEB])
        vop(lambda e, sx=snx: e.tensor_tensor(
                scrC[:, 0:512].rearrange("p (h d) -> p h d", h=HC)[:, :, 32:64],
                scrB2[:, 0:512].rearrange("p (h d) -> p h d", h=HC)[:, :, 0:32],
                sx[:, 32:64].unsqueeze(1).broadcast_to([128, HC, 32]), ALU.mult),
            reads=[GATEB, TABB], writes=[GATEB])
        vop(lambda e: e.tensor_sub(
                scrD[:, 0:512].rearrange("p (h d) -> p h d", h=HC)[:, :, 0:32],
                scrA[:, 0:512].rearrange("p (h d) -> p h d", h=HC)[:, :, 0:32],
                scrC[:, 0:512].rearrange("p (h d) -> p h d", h=HC)[:, :, 0:32]),
            reads=[GATEB], writes=[SCRB])
        vop(lambda e: e.tensor_add(
                scrD[:, 0:512].rearrange("p (h d) -> p h d", h=HC)[:, :, 32:64],
                scrA[:, 0:512].rearrange("p (h d) -> p h d", h=HC)[:, :, 32:64],
                scrC[:, 0:512].rearrange("p (h d) -> p h d", h=HC)[:, :, 32:64]),
            reads=[GATEB], writes=[SCRB])
        sop(lambda e: e.copy(pa_junk[:, 0:512], scrD[:, 0:512]),
            reads=[SCRB], writes=[PAJB])
        for blk in range(4):
            trans(PTB, PT[:, blk * 128:(blk + 1) * 128],
                  pa_junk[:, blk * 128:(blk + 1) * 128], eye_b, [PAJB, CONSTB],
                  start=(blk == 0), stop=(blk == 3))
        if which == 0:
            vop(lambda e, tt=tt: e.tensor_copy(
                    qfc[:, :, tt * 128:(tt + 1) * 128],
                    PT[:, 0:512].rearrange("p (b c) -> p b c", b=4)),
                reads=[PTB], writes=[QFCB])
        else:
            vop(lambda e, tt=tt: e.tensor_copy(
                    kfw[:, :, tt * 128:(tt + 1) * 128],
                    PT[:, 0:512].rearrange("p (b c) -> p b c", b=4)),
                reads=[PTB], writes=[kb_cur])

    def stage_c_attn(l):
        for c in range(NCH):
            cast_w(fqkv_b[:, c, :], FQB, fqkv[l, c * 128:(c + 1) * 128, :], 128, FFI,
                   eng="G")
        for c in range(4):
            cast_w(fwo_b[:, c, :], FOB, fwo[l, c * 128:(c + 1) * 128, :], 128, D,
                   eng="G")
        for i in range(4):
            dmaG("QS", qscale_t[:, i, :], qscaleD[l, i], writes=[QSB])
        for w in range(1, NW):
            vop(lambda e, w=w: e.tensor_copy(ybnd[:, :, w:w + 1],
                                             sF[:, :, w * WIN:w * WIN + 1]),
                reads=sf_bufs(w * WIN, w * WIN + 1), writes=[YBND])
        jwarm(12)

        for w in range(NW):
            wsl = w % 2
            kfw, kfp = kfc[:, wsl], kfc[:, 1 - wsl]
            vtw, vtp = vtc[:, wsl], vtc[:, 1 - wsl]
            kb_cur, kb_prev = KFCB[wsl], KFCB[1 - wsl]
            vb_cur, vb_prev = VTCB[wsl], VTCB[1 - wsl]
            wsf = sf_bufs(1 + w * WIN, 1 + (w + 1) * WIN)

            dmaQ("TAB", tabw[:, :, :], tabsD[:, w].transpose([1, 0, 2]), writes=[TABB])
            dmaQ("TABL", fv("SCR", 1024, 0).rearrange("p (i k) -> p i k", i=4),
                 tabsloD[:, w].transpose([1, 0, 2]), writes=[SCRB])
            vop(lambda e: e.tensor_tensor(tabw[:, :, :], tabw[:, :, :],
                                          fv("SCR", 1024, 0).rearrange(
                                              "p (i k) -> p i k", i=4), ALU.add),
                reads=[SCRB, TABB], writes=[TABB])
            vop(lambda e: e.tensor_tensor(
                    tabw.rearrange("p i (a b) -> p i a b", a=TW),
                    tabw.rearrange("p i (a b) -> p i a b", a=TW),
                    qscale_t[:, :, :].unsqueeze(2).broadcast_to([128, 4, TW, HD]),
                    ALU.mult),
                reads=[QSB, TABB], writes=[TABB])

            for tt in range(TW):
                t0g = 1 + w * WIN + tt * 128
                ti = tt % 2
                if w > 0 and tt == 0:
                    sop(lambda e, ti=ti, t0g=t0g: e.copy(ybf[:, ti, :, 1:129],
                                                         sF[:, :, t0g:t0g + 128]),
                        reads=wsf, writes=[YBFB[ti]])
                    sop(lambda e, ti=ti, w=w: e.copy(ybf[:, ti, :, 0:1],
                                                     ybnd[:, :, w:w + 1]),
                        reads=[YBND], writes=[YBFB[ti]])
                else:
                    sop(lambda e, ti=ti, t0g=t0g: e.copy(
                            ybf[:, ti, :, 0:129], sF[:, :, t0g - 1:t0g + 128]),
                        reads=sf_bufs(t0g - 1, t0g + 128), writes=[YBFB[ti]])
                if tt % 2 == 0:
                    qp, qpb = PC, PCB
                    kp, kpb = PD, PDB
                    vp, vpb = PE[:, 0:512], PEB
                else:
                    qp, qpb = PA[:, 0:512], PASL[0]
                    kp, kpb = PA[:, 512:1024], PASL[4]
                    vp, vpb = PB[:, 0:512], PBB
                for kc in range(NCH):
                    lhs = ybf[:, ti, kc, 1:129] if kc < 4 else ybf[:, ti, kc, 0:128]
                    mm(qpb, qp[:, 0:512], lhs, fqkv_b[:, kc, 0:512],
                       [YBFB[ti], FQB], start=(kc == 0), stop=(kc == NCH - 1))
                    mm(kpb, kp[:, 0:512], lhs, fqkv_b[:, kc, 512:1024],
                       [YBFB[ti], FQB], start=(kc == 0), stop=(kc == NCH - 1))
                    mm(vpb, vp[:, 0:512], lhs, fqkv_b[:, kc, 1024:1536],
                       [YBFB[ti], FQB], start=(kc == 0), stop=(kc == NCH - 1))
                sop(lambda e, tt=tt, vv=vtw, vp=vp: e.activation(vv[:, tt, :],
                                                                 vp[:, 0:512], AF.Copy),
                    reads=[vpb], writes=[vb_cur])
                rotary_qk(l, w, tt, 0, qp, qpb, kfw, kb_cur)
                rotary_qk(l, w, tt, 1, kp, kpb, kfw, kb_cur)

            slot_ctr = [0]

            def slot_ap(sl):
                if sl < 8:
                    return PA[:, sl * 128:(sl + 1) * 128], PASL[sl]
                return PC[:, (sl - 8) * 128:(sl - 7) * 128], PCB

            def kts_of(tt):
                return ([("p", j) for j in range(tt, TW)] if w > 0 else []) \
                    + [("c", j) for j in range(0, tt + 1)]

            def pass1(tt):
                par = tt % 2
                nm, dsP, dsC = nm_v(par), dsP_v(par), dsC_v(par)
                tmp1 = tmp1_v(par)
                wp = (TW - tt) * 128
                wc = (tt + 1) * 128
                for h in range(HC):
                    hp, hpr = (h % 2) * 64, h // 2
                    ppv = PD[:, 0:wp]
                    pcu = PE[:, 0:wc]
                    if w > 0:
                        op("T", lambda e, o=ppv, l1=qfc[hp:hp + 64, hpr, tt * 128:(tt + 1) * 128],
                           r=kfp[hp:hp + 64, hpr, tt * 128:512]:
                           e.matmul(o, l1, r, start=True, stop=True),
                           reads=[QFCB, kb_prev], writes=[PDB])
                        vop(lambda e, o=ppv: e.tensor_tensor(o[:, 0:128], o[:, 0:128],
                                                             ml_b, ALU.add),
                            reads=[CONSTB], writes=[PDB])
                    op("T", lambda e, o=pcu, l1=qfc[hp:hp + 64, hpr, tt * 128:(tt + 1) * 128],
                       r=kfw[hp:hp + 64, hpr, 0:wc]:
                       e.matmul(o, l1, r, start=True, stop=True),
                       reads=[QFCB, kb_cur], writes=[PEB])
                    vop(lambda e, o=pcu, wc=wc: e.tensor_tensor(
                            o[:, wc - 128:wc], o[:, wc - 128:wc], mh_b, ALU.add),
                        reads=[CONSTB], writes=[PEB])
                    if w > 0:
                        vop(lambda e, o=ppv: e.reduce_max(tmp1[:, 0:1], o, AX.X,
                                                          negate=True),
                            reads=[PDB], writes=[SMALLB])
                        vop(lambda e, o=pcu: e.reduce_max(tmp1[:, 1:2], o, AX.X,
                                                          negate=True),
                            reads=[PEB], writes=[SMALLB])
                        vop(lambda e, h=h: e.tensor_tensor(nm[:, h:h + 1],
                                                           tmp1[:, 0:1],
                                                           tmp1[:, 1:2], ALU.min),
                            reads=[SMALLB], writes=[SMALLB])
                        sop(lambda e, o=ppv, wp=wp, h=h: e.activation(
                                pa_junk[:, 0:wp], o, AF.Exp,
                                bias=nm[:, h:h + 1], accum_out=dsP[:, h:h + 1]),
                            reads=[PDB, SMALLB], writes=[PAJB, SMALLB])
                    else:
                        vop(lambda e, o=pcu, h=h: e.reduce_max(
                                nm[:, h:h + 1], o, AX.X, negate=True),
                            reads=[PEB], writes=[SMALLB])
                    sop(lambda e, o=pcu, wc=wc, h=h: e.activation(
                            pa_junk[:, 512:512 + wc], o, AF.Exp,
                            bias=nm[:, h:h + 1], accum_out=dsC[:, h:h + 1]),
                        reads=[PEB, SMALLB], writes=[PAJB, SMALLB])

            def nmphase(tt):
                par = tt % 2
                nm, dsP, dsC, lnd, nmld = (nm_v(par), dsP_v(par), dsC_v(par),
                                           lnd_v(par), nmld_v(par))
                bcB = bcastB[:, par * 1024:(par + 1) * 1024]
                if w > 0:
                    vop(lambda e: e.tensor_add(dsC[:, 0:8], dsC[:, 0:8], dsP[:, 0:8]),
                        reads=[SMALLB], writes=[SMALLB])
                sop(lambda e: e.activation(lnd[:, 0:8], dsC[:, 0:8], AF.Ln),
                    reads=[SMALLB], writes=[SMALLB])
                vop(lambda e: e.tensor_sub(nmld[:, 0:8], nm[:, 0:8], lnd[:, 0:8]),
                    reads=[SMALLB], writes=[SMALLB])
                for grp in range(2):
                    for hh in range(4):
                        h = grp * 4 + hh
                        trans(PBB, PB[0:1, hh * 128:(hh + 1) * 128],
                              nmld[:, h:h + 1], eye_f, [SMALLB, CONSTF],
                              start=(hh == 0), stop=(hh == 3))
                    vop(lambda e, g=grp: e.tensor_copy(
                            nmld_row[0:1, g * 512:(g + 1) * 512], PB[0:1, 0:512]),
                        reads=[PBB], writes=[STATB])
                for half in range(2):
                    mm(PBB, PB[:, 0:512], ones_f[0:1, 0:128],
                       nmld_row[0:1, half * 512:(half + 1) * 512],
                       [STATB, CONSTF], start=True, stop=True)
                    sop(lambda e, half=half, b=bcB: e.copy(
                            b[:, half * 512:(half + 1) * 512], PB[:, 0:512]),
                        reads=[PBB], writes=[GATEB])

            def pass2(tt):
                par = tt % 2
                bcB = bcastB[:, par * 1024:(par + 1) * 1024]
                nkts = (TW - tt if w > 0 else 0) + (tt + 1)

                def emit_at(h, sls):
                    hp, hpr = (h % 2) * 64, h // 2
                    for idx, (wk, j) in enumerate(kts_of(tt)):
                        kfx, kbx = (kfp, kb_prev) if wk == "p" else (kfw, kb_cur)
                        sl = slot_ctr[0] % 12
                        slot_ctr[0] += 1
                        sls.append((sl, wk, j))
                        pap, pbf = slot_ap(sl)
                        op("T", lambda e, o=pap, l1=kfx[hp:hp + 64, hpr, j * 128:(j + 1) * 128],
                           r=qfc[hp:hp + 64, hpr, tt * 128:(tt + 1) * 128]:
                           e.matmul(o, l1, r, start=True, stop=True),
                           reads=[kbx, QFCB], writes=[pbf])

                def slot_runs(sls):
                    runs = []
                    for sl, wk, j in sls:
                        if runs and runs[-1][0] + runs[-1][1] == sl and sl % 4 != 0:
                            runs[-1][1] += 1
                        else:
                            runs.append([sl, 1])
                    return runs

                def emit_rest(h, sls):
                    hp = (h % 2) * 64
                    oc0 = (h // 2) * 128
                    for s0, ln_ in slot_runs(sls):
                        pbf = slot_ap(s0)[1]
                        wide = PA if s0 < 8 else PC
                        off = s0 * 128 if s0 < 8 else (s0 - 8) * 128
                        vop(lambda e, w_=wide, off=off, ln_=ln_, h=h, b=bcB:
                            e.tensor_tensor(
                                w_[:, off:off + ln_ * 128].rearrange(
                                    "p (a b) -> p a b", a=ln_),
                                w_[:, off:off + ln_ * 128].rearrange(
                                    "p (a b) -> p a b", a=ln_),
                                b[:, h * 128:(h + 1) * 128].unsqueeze(1)
                                .broadcast_to([128, ln_, 128]), ALU.add),
                            reads=[GATEB], writes=[pbf])
                    for sl, wk, j in sls:
                        pap, pbf = slot_ap(sl)
                        if wk == "p" and j == tt:
                            vop(lambda e, o=pap: e.tensor_tensor(o, o, mh_b, ALU.add),
                                reads=[CONSTB], writes=[pbf])
                        if wk == "c" and j == tt:
                            vop(lambda e, o=pap: e.tensor_tensor(o, o, ml_b, ALU.add),
                                reads=[CONSTB], writes=[pbf])
                    for s0, ln_ in slot_runs(sls):
                        pbf = slot_ap(s0)[1]
                        wide = PA if s0 < 8 else PC
                        off = s0 * 128 if s0 < 8 else (s0 - 8) * 128
                        pp = s0
                        sop(lambda e, w_=wide, off=off, ln_=ln_, pp=pp: e.activation(
                                ptile8[:, pp:pp + ln_, :].rearrange("p a b -> p (a b)"),
                                w_[:, off:off + ln_ * 128], AF.Exp),
                            reads=[pbf],
                            writes=[PTL8[s0 + k] for k in range(ln_)])
                    for idx, (sl, wk, j) in enumerate(sls):
                        vtx, vbx = (vtp, vb_prev) if wk == "p" else (vtw, vb_cur)
                        op("T", lambda e, pp=sl, l1=vtx[:, j, h * 64:(h + 1) * 64],
                           hp=hp, oc0=oc0, st=(idx == 0), sp=(idx == nkts - 1):
                           e.matmul(PS[hp:hp + 64, oc0:oc0 + 128], l1,
                                    ptile8[:, pp, :], start=st, stop=sp),
                           reads=[vbx, PTL8[sl]], writes=[PSB])

                prev = None
                for h in range(HC):
                    sls = []
                    emit_at(h, sls)
                    if prev is not None:
                        emit_rest(prev[0], prev[1])
                    prev = (h, sls)
                emit_rest(prev[0], prev[1])
                for pr in range(4):
                    vop(lambda e, pr=pr, tt=tt: e.tensor_copy(
                            ofc[:, pr, tt * 128:(tt + 1) * 128],
                            PS[:, pr * 128:(pr + 1) * 128]),
                        reads=[PSB], writes=[OFCB])

            pass1(0)
            nmphase(0)
            for tt in range(TW):
                if tt + 1 < TW:
                    pass1(tt + 1)
                pass2(tt)
                if tt + 1 < TW:
                    nmphase(tt + 1)

            c0 = 1 + w * WIN
            for dc in range(NCH):
                pb, pbuf = (PC, PCB) if dc % 2 == 0 else (PD, PDB)
                for kc in range(4):
                    mm(pbuf, pb[:, 0:WIN], fwo_b[:, kc, dc * 128:(dc + 1) * 128],
                       ofc[:, kc, :], [FOB, OFCB], start=(kc == 0), stop=(kc == 3))
                vop(lambda e, d=dc, p=pb, c0=c0: e.tensor_tensor(
                        sF[:, d, c0:c0 + WIN], p[:, 0:WIN],
                        sF[:, d, c0:c0 + WIN], ALU.add),
                    reads=[pbuf] + wsf, writes=wsf)
            if l == 0:
                tap(f"vt0_w{w}", vtc[:, 0], [VTCB[0]])
                tap(f"kf0_w{w}", kfc[:, 0], [KFCB[0]])

    fw1r = [fw1[l].rearrange("(c p) o -> p c o", p=128) for l in range(2)]

    def load_w1(l, osn, slot):
        o0, ow = OSR[osn]
        for part, off in ((0, o0), (1, HID + o0)):
            i = _wi[0] % 2
            _wi[0] += 1
            stg = fv("WST", 1024, 1024 * i).rearrange(
                "p (c o) -> p c o", c=NCH)[:, :, 0:ow]
            dmaQ(f"WST{i}", stg, fw1r[l][:, :, off:off + ow], writes=[WST[i]])
            op("G", lambda e, s=slot, ow=ow, g=stg, part=part: e.tensor_copy(
                   w1v[s][:, :, part * 128:part * 128 + ow], g),
               reads=[WST[i]], writes=[W1B[slot]])

    def load_w2(l, osn, slot):
        o0, ow = OSR[osn]
        i = _wi[0] % 2
        _wi[0] += 1
        stg = fv("WST", 1024, 1024 * i)[0:ow, :]
        dmaQ(f"WST{i}", stg, fw2[l, o0:o0 + ow, :], writes=[WST[i]])
        op("G", lambda e, s=slot, ow=ow, g=stg: e.tensor_copy(w2v[s][0:ow, :], g),
           reads=[WST[i]], writes=[W2B[slot]])

    def stage_c_mlp(l, last):
        vop(lambda e: e.tensor_copy(ybnd[:, :, 0:1], sF[:, :, N // 2:N // 2 + 1]),
            reads=sf_bufs(N // 2, N // 2 + 1), writes=[YBND])
        for half in range(2):
            hb0 = 1 + half * (N // 2)
            if half == 1:
                vop(lambda e: e.tensor_copy(ybnd[:, :, 1:2],
                                            sF[:, :, N // 2:N // 2 + 1]),
                    reads=sf_bufs(N // 2, N // 2 + 1), writes=[YBND])
                vop(lambda e: e.tensor_copy(sF[:, :, N // 2:N // 2 + 1],
                                            ybnd[:, :, 0:1]),
                    reads=[YBND], writes=sf_bufs(N // 2, N // 2 + 1))
            for g in range(2):
                g0 = hb0 + g * 512
                ln_stats_sf(g0, 512, shifted=True)
                bcast_rows(512)
                for c in range(NCH):
                    jwarm(2)
                    off = -1 if c >= 4 else 0
                    ap = sF[:, c, g0 + off: g0 + off + 512]
                    rb = sf_bufs(g0 + off, g0 + off + 512)
                    vop(lambda e, a=ap: e.tensor_tensor(scr[:, 0:512], a,
                                                        PC[:, 0:512], ALU.mult),
                        reads=rb + [PCB], writes=[SCRB])
                    vop(lambda e: e.tensor_tensor(scr[:, 512:1024], scr[:, 0:512],
                                                  PD[:, 0:512], ALU.add),
                        reads=[SCRB, PDB], writes=[SCRB])
                    sop(lambda e, c=c, g=g: e.activation(
                            ln4f[:, c, g * 512:(g + 1) * 512], scr[:, 512:1024],
                            AF.Identity, bias=pvcol(f"ln_b{l}", c),
                            scale=pvcol(f"ln_g{l}", c)),
                        reads=[SCRB, CONSTF], writes=[LN4B])
            if half == 1:
                vop(lambda e: e.tensor_copy(sF[:, :, N // 2:N // 2 + 1],
                                            ybnd[:, :, 1:2]),
                    reads=[YBND], writes=sf_bufs(N // 2, N // 2 + 1))

            load_w1(l, 0, 0)
            for osn in range(NOS):
                slot = osn % 2
                if osn + 1 < NOS:
                    load_w1(l, osn + 1, 1 - slot)
                o0, ow = OSR[osn]
                if osn == 0:
                    jwarm(4)
                for tg in range(2):
                    for kc in range(NCH):
                        mm(PCB, PC[0:ow, 0:512], w1v[slot][:, kc, 0:ow],
                           ln4f[:, kc, tg * 512:(tg + 1) * 512], [W1B[slot], LN4B],
                           start=(kc == 0), stop=(kc == NCH - 1))
                        mm(PDB, PD[0:ow, 0:512], w1v[slot][:, kc, 128:128 + ow],
                           ln4f[:, kc, tg * 512:(tg + 1) * 512], [W1B[slot], LN4B],
                           start=(kc == 0), stop=(kc == NCH - 1))
                    gslot = gel[0:ow, tg * 512:(tg + 1) * 512]
                    sop(lambda e, g=gslot, ow=ow, osn=osn: e.activation(
                            g, PD[0:ow, 0:512], AF.Gelu,
                            bias=pvcol(f"b1g_{l}", osn, rows=ow)),
                        reads=[PDB, CONSTF], writes=[GATEB])
                    vop(lambda e, g=gslot, ow=ow, osn=osn, tg=tg:
                        e.scalar_tensor_tensor(
                            hg_regions[osn][0:ow, tg * 512:(tg + 1) * 512],
                            PC[0:ow, 0:512], pvcol(f"b1a_{l}", osn, rows=ow),
                            g, ALU.add, ALU.mult),
                        reads=[PCB, GATEB, CONSTF], writes=[HGB])

            groups = [list(range(g, min(g + 4, NOS))) for g in range(0, NOS, 4)]
            for gi, grp in enumerate(groups):
                for k, osn in enumerate(grp):
                    load_w2(l, osn, (gi % 2) * 4 + k)
                jwarm(2)
                for tg in range(2):
                    g0 = hb0 + tg * 512
                    gsf = sf_bufs(g0, g0 + 512)
                    for dc in range(NCH):
                        pb, pbuf = (PC, PCB) if dc % 2 == 0 else (PD, PDB)
                        for k, osn in enumerate(grp):
                            o0, ow = OSR[osn]
                            sl = (gi % 2) * 4 + k
                            mm(pbuf, pb[:, 0:512],
                               w2v[sl][0:ow, dc * 128:(dc + 1) * 128],
                               hg_regions[osn][0:ow, tg * 512:(tg + 1) * 512],
                               [W2B[sl], HGB], start=(k == 0),
                               stop=(k == len(grp) - 1))
                        if gi == 0:
                            vop(lambda e, d=dc, p=pb, g0=g0: e.scalar_tensor_tensor(
                                    sF[:, d, g0:g0 + 512], p[:, 0:512],
                                    pvcol(f"b2_{l}", d), sF[:, d, g0:g0 + 512],
                                    ALU.add, ALU.add),
                                reads=[pbuf, CONSTF] + gsf, writes=gsf)
                        else:
                            vop(lambda e, d=dc, p=pb, g0=g0: e.tensor_tensor(
                                    sF[:, d, g0:g0 + 512], p[:, 0:512],
                                    sF[:, d, g0:g0 + 512], ALU.add),
                                reads=[pbuf] + gsf, writes=gsf)
                        if last and gi == len(groups) - 1:
                            xsl = dc % 2
                            dmaG(f"X2R{xsl}", fv("TAB", 512, xsl * 512),
                                 x2s[dc * 128:(dc + 1) * 128, g0 - 1:g0 - 1 + 512],
                                 reads=[B("X2SB")], writes=[X2R[xsl]])
                            vop(lambda e, d=dc, g0=g0, xsl=xsl: e.tensor_tensor(
                                    sF[:, d, g0:g0 + 512], sF[:, d, g0:g0 + 512],
                                    fv("TAB", 512, xsl * 512), ALU.add),
                                reads=[X2R[xsl]] + gsf, writes=gsf)
                            dmaG("OUT", outT[dc * 128:(dc + 1) * 128, g0 - 1:g0 - 1 + 512],
                                 sF[:, dc, g0:g0 + 512], reads=gsf)

    for l in range(2):
        stage_c_attn(l)
        kb.barrier()
        if l == 0:
            tap_sf("ya0")
            tap("qfc0", qfc, [QFCB])
            tap("kfc0", kfc, [KFCB[0], KFCB[1]])
            tap("vtc0", vtc, [VTCB[0], VTCB[1]])
            tap("ofc0", ofc, [OFCB])
        stage_c_mlp(l, last=(l == 1))
        kb.barrier()
        if l == 0:
            tap_sf("ym0")

    kb.finalize_blocks()
    kb.tap_t = tap_t
    return kb


_BUILD_CACHE = {}

def _get_kb():
    if "kb" not in _BUILD_CACHE:
        _BUILD_CACHE["kb"] = build_program()
    return _BUILD_CACHE["kb"]


def prep_common(inputs):
    consts = _host_constants()
    pv = _pack_pvec(inputs)
    qs = np.asarray(inputs["L0_ffn_q_scale"], np.float32)
    ks = np.asarray(inputs["L0_ffn_k_scale"], np.float32)
    qsc = np.zeros((2, 4, 128, HD), np.float32)
    for l in range(2):
        qsc[l, 0] = np.broadcast_to(qs[l], (128, HD))
        qsc[l, 1] = np.broadcast_to(np.concatenate([qs[l, 32:], qs[l, :32]]), (128, HD))
        qsc[l, 2] = np.broadcast_to(ks[l], (128, HD))
        qsc[l, 3] = np.broadcast_to(np.concatenate([ks[l, 32:], ks[l, :32]]), (128, HD))

    return dict(
        wqkvT=np.ascontiguousarray(np.asarray(inputs["L0_attn_w_qkv"], np.float32).T),
        wg1T=np.ascontiguousarray(np.asarray(inputs["L0_attn_w_g1"], np.float32).T),
        wg2T=np.ascontiguousarray(np.asarray(inputs["L0_attn_w_g2"], np.float32).T),
        woutT=np.ascontiguousarray(np.asarray(inputs["L0_attn_w_out"], np.float32).T),
        fqkv=np.ascontiguousarray(np.asarray(inputs["L0_ffn_wqkv"], np.float32)),
        fwo=np.ascontiguousarray(np.asarray(inputs["L0_ffn_wo"], np.float32)),
        fw1=np.ascontiguousarray(np.asarray(inputs["L0_ffn_w1"], np.float32)),
        fw2=np.ascontiguousarray(np.asarray(inputs["L0_ffn_w2"], np.float32)),
        qscale=qsc, pvec=pv,
        tabs=consts["tabs"], tabslo=consts["tabslo"], ml=consts["ml"], mh=consts["mh"],
        maskb=consts["maskb"], eye_f=consts["eye_f"], eye_b=consts["eye_b"],
        ones_f=consts["ones_f"], ones_b=consts["ones_b"],
    )


def kernel(**inputs):
    common = prep_common(inputs)
    x = np.asarray(inputs["L0_attn_x"], np.float32)
    kb = _get_kb()
    in_maps = []
    for b in range(8):
        m = dict(common)
        m["xT"] = np.ascontiguousarray(x[b].T)
        in_maps.append(m)
    res = run_bass_kernel_spmd(kb.nc, in_maps, core_ids=list(range(8)))
    out = np.stack([np.ascontiguousarray(r["outT"].T) for r in res.results])
    return out.astype(np.float32)


# version 10
# speedup vs baseline: 1.0147x; 1.0147x over previous
import numpy as np
import ml_dtypes
import concourse.bass as bass
import concourse.mybir as mybir
from concourse.bass_utils import run_bass_kernel_spmd

F32 = mybir.dt.float32
BF16 = mybir.dt.bfloat16
AF = mybir.ActivationFunctionType
ALU = mybir.AluOpType
AX = mybir.AxisListType

D = 1024
NCH = 8
N = 2048
SUP = 256
NSUP = N // SUP
HB = 16
HD = 64
HC = 8
WIN = 512
NW = N // WIN
TW = WIN // 128
HID = 2730
FFI = 1536
NEG = -1.0e30
OS_SIZES = [128] * (HID // 128) + ([HID % 128] if HID % 128 else [])
NOS = len(OS_SIZES)

def _os_ranges():
    o0, out = 0, []
    for w in OS_SIZES:
        out.append((o0, w))
        o0 += w
    return out

OSR = _os_ranges()

PVIDX = {}
_pc = 0
def _pvadd(name, n):
    global _pc
    PVIDX[name] = _pc
    _pc += n
_pvadd("eps", 1)
_pvadd("b_g1", 1)
_pvadd("b_g2", 8)
_pvadd("gamma", 8)
_pvadd("beta", 8)
for _l in range(2):
    _pvadd(f"ln_g{_l}", 8)
    _pvadd(f"ln_b{_l}", 8)
    _pvadd(f"b2_{_l}", 8)
for _l in range(2):
    _pvadd(f"b1a_{_l}", NOS)
    _pvadd(f"b1g_{_l}", NOS)
PV = _pc


def _bf(x):
    return np.ascontiguousarray(np.asarray(x)).astype(ml_dtypes.bfloat16)


def _host_constants():
    half_idx = np.arange(0, HD, 2, dtype=np.float64)
    inv_freq = 1.0 / 10000.0 ** (half_idx / HD)
    t = np.arange(N, dtype=np.float64)
    freqs = t[:, None] * inv_freq[None, :]
    freqs = np.concatenate([freqs, freqs], axis=-1)
    base = (half_idx + 0.4 * HD) / (1.4 * HD)
    power = (t - N // 2) / float(WIN // 2)
    scale = base[None, :] ** power[:, None]
    scale = np.concatenate([scale, scale], axis=-1)
    cs, sn = np.cos(freqs), np.sin(freqs)
    t64 = [cs * scale * 8.0, sn * scale * 8.0, cs / scale, sn / scale]

    def wtile(a):
        return np.ascontiguousarray(
            a.reshape(NW, TW, 128, HD).transpose(0, 2, 1, 3).reshape(NW, 128, TW * HD))

    hi = [a.astype(np.float32) for a in t64]
    lo = [(a - h.astype(np.float64)).astype(np.float32) for a, h in zip(t64, hi)]
    tabs = np.stack([wtile(h) for h in hi])
    tabslo = np.stack([wtile(x) for x in lo])
    ii = np.arange(128)
    ml = np.where(ii[None, :] < ii[:, None], NEG, 0.0).astype(np.float32)
    mh = np.where(ii[None, :] > ii[:, None], NEG, 0.0).astype(np.float32)
    triu_keep = (ii[:, None] <= ii[None, :]).astype(np.float32)
    maskb = np.concatenate([triu_keep, np.ones((128, 128), np.float32), triu_keep], 1)
    eye = np.eye(128, dtype=np.float32)
    ones = np.ones((128, 128), np.float32)
    return dict(tabs=tabs, tabslo=tabslo, ml=_bf(ml), mh=_bf(mh), maskb=_bf(maskb),
                eye_f=eye, eye_b=_bf(eye), ones_f=ones, ones_b=_bf(ones))


def _pack_pvec(ins):
    cols = np.zeros((128, PV), np.float32)
    def put(name, k, v):
        cols[: v.shape[0], PVIDX[name] + k] = v
    put("eps", 0, np.full((128,), 1e-5, np.float32))
    put("b_g1", 0, np.asarray(ins["L0_attn_b_g1"], np.float32))
    for c in range(NCH):
        put("b_g2", c, np.asarray(ins["L0_attn_b_g2"], np.float32)[c * 128:(c + 1) * 128])
        put("gamma", c, np.asarray(ins["L0_attn_gamma"], np.float32)[c * 128:(c + 1) * 128])
        put("beta", c, np.asarray(ins["L0_attn_beta"], np.float32)[c * 128:(c + 1) * 128])
    for l in range(2):
        for c in range(NCH):
            put(f"ln_g{l}", c, np.asarray(ins["L0_ffn_ln_g"], np.float32)[l, c * 128:(c + 1) * 128])
            put(f"ln_b{l}", c, np.asarray(ins["L0_ffn_ln_b"], np.float32)[l, c * 128:(c + 1) * 128])
            put(f"b2_{l}", c, np.asarray(ins["L0_ffn_b2"], np.float32)[l, c * 128:(c + 1) * 128])
        b1 = np.asarray(ins["L0_ffn_b1"], np.float32)[l]
        for k, (o0, ow) in enumerate(OSR):
            put(f"b1a_{l}", k, b1[o0:o0 + ow])
            put(f"b1g_{l}", k, b1[HID + o0:HID + o0 + ow])
    return cols



class Buf:
    __slots__ = ("w", "rs", "name")

    def __init__(self, name=""):
        self.w = {}
        self.rs = {}
        self.name = name

    def toks_w(self):
        return list(self.w.items())

    def toks_rw(self):
        out = dict(self.w)
        for s, v in self.rs.items():
            out[s] = max(out.get(s, 0), v)
        return list(out.items())

    def add_read(self, tok):
        s, v = tok
        self.rs[s] = max(self.rs.get(s, 0), v)

    def set_write(self, tok):
        s, v = tok
        self.w = {s: v}
        self.rs = {}


class KB:
    ENGS = ("T", "V", "S", "Q", "G")

    def __init__(self):
        self.nc = bass.Bass("TRN2", target_bir_lowering=False)
        self.prog = {e: [] for e in self.ENGS}
        self.cnt = {"T": 0, "V": 0, "S": 0, "Q": 0, "G": 0}
        self.wm = {e: {} for e in self.ENGS}
        self.n_wait = 0
        self.n_inst = 0

    def build_resources(self):
        import contextlib
        self.stack = contextlib.ExitStack()
        ent = self.stack.enter_context
        self.sem = {k: ent(self.nc.semaphore(f"sem_{k}"))
                    for k in ("T", "V", "S", "Q", "G")}
        return ent

    def dma_sem(self, name):
        if name not in self.sem:
            self.sem[name] = self.stack.enter_context(
                self.nc.semaphore(f"sem_{name}"))
            self.cnt[name] = 0
        return self.sem[name]

    def op(self, eng, fn, reads=(), writes=(), deps=(), dma=False):
        need = {}
        for b in reads:
            for s, v in b.toks_w():
                need[s] = max(need.get(s, 0), v)
        for b in writes:
            for s, v in b.toks_rw():
                need[s] = max(need.get(s, 0), v)
        for s, v in deps:
            need[s] = max(need.get(s, 0), v)
        wmm = self.wm[eng]
        waits = []
        for s, v in sorted(need.items()):
            if wmm.get(s, 0) < v:
                waits.append((s, v))
                wmm[s] = v
        if dma:
            kind = dma
            self.dma_sem(kind)
            inc = 16
        else:
            kind, inc = eng, 1
        self.cnt[kind] += inc
        tok = (kind, self.cnt[kind])
        self.n_wait += len(waits)
        self.n_inst += 1
        self.prog[eng].append((waits, fn, kind, inc))
        for b in reads:
            b.add_read(tok)
        for b in writes:
            b.set_write(tok)
        return tok

    def snapshot(self):
        return [(k, v) for k, v in self.cnt.items() if v > 0]

    def barrier(self):
        toks = self.snapshot()
        for e in self.ENGS:
            self.op(e, lambda en: en.nop(), deps=toks)

    def finalize_blocks(self):
        sem, prog, cnt = self.sem, self.prog, self.cnt
        engmap = {"T": "tensor", "V": "vector", "S": "scalar", "Q": "sync",
                  "G": "gpsimd"}
        with self.nc.Block() as block:
            for e in self.ENGS:
                def mk(e):
                    def body(eng):
                        for waits, fn, kind, inc in prog[e]:
                            for s, v in waits:
                                eng.wait_ge(sem[s], v)
                            fn(eng).then_inc(sem[kind], inc)
                        if e == "Q":
                            for k, v in cnt.items():
                                if v and k != "Q":
                                    eng.wait_ge(sem[k], v)
                    return body
                getattr(block, engmap[e])(mk(e))



def build_program(debug=False):
    kb = KB()
    nc = kb.nc
    ent = kb.build_resources()
    tap_t = {}

    def din(name, shape, dt=F32):
        return nc.dram_tensor(name, list(shape), dt, kind="ExternalInput").ap()

    xT = din("xT", [D, N])
    wqkvT = din("wqkvT", [D, 3 * D])
    wg1T = din("wg1T", [D, HD])
    wg2T = din("wg2T", [HD, D])
    woutT = din("woutT", [D, D])
    fqkv = din("fqkv", [2, D, FFI])
    fwo = din("fwo", [2, 512, D])
    fw1 = din("fw1", [2, D, 2 * HID])
    fw2 = din("fw2", [2, HID, D])
    qscaleD = din("qscale", [2, 4, 128, HD])
    pvecD = din("pvec", [128, PV])
    tabsD = din("tabs", [4, NW, 128, TW * HD])
    tabsloD = din("tabslo", [4, NW, 128, TW * HD])
    mlD = din("ml", [128, 128], BF16)
    mhD = din("mh", [128, 128], BF16)
    maskbD = din("maskb", [128, 384], BF16)
    eyefD = din("eye_f", [128, 128])
    eyebD = din("eye_b", [128, 128], BF16)
    onesfD = din("ones_f", [128, 128])
    onesbD = din("ones_b", [128, 128], BF16)

    outT = nc.dram_tensor("outT", [D, N], F32, kind="ExternalOutput").ap()
    x2s = nc.dram_tensor("x2s", [D, N], F32).ap()

    sF = ent(nc.sbuf_tensor("sF", [128, NCH, N + 1], F32))

    fplan, bplan = {}, {}
    _fc = [0]
    def fslot(name, cols):
        fplan[name] = (_fc[0], cols)
        _fc[0] += cols
    fslot("STAT", 1024)
    fslot("SMALL", 128)
    fslot("GATE", 2048)
    fslot("SCR", 1536)
    fslot("WST", 2048)
    fslot("TAB", 1024)
    fslot("CONST", 448)
    fslot("YBND", 32)
    fslot("S32", 512)
    AFt = ent(nc.sbuf_tensor("AFa", [128, _fc[0]], F32))

    _bc = [0]
    def bslot(name, cols):
        bplan[name] = (_bc[0], cols)
        _bc[0] += cols
    bslot("YBF", 2064)
    bslot("QF", 2048)
    bslot("KF", 4096)
    bslot("VT", 4096)
    bslot("KST", 2048)
    bslot("ATM", 1536)
    bslot("OF", 2048)
    bslot("SBF", 512)
    bslot("WA", 24576)
    bslot("WB", 8192)
    bslot("WG", 1536)
    bslot("CONB", 896)
    ABt = ent(nc.sbuf_tensor("ABa", [128, _bc[0]], BF16))

    total_bytes = (NCH * (N + 1)) * 4 + _fc[0] * 4 + _bc[0] * 2
    assert total_bytes <= 209500, total_bytes

    PA = ent(nc.psum_tensor("PA", [128, 1024], F32))
    PC = ent(nc.psum_tensor("PC", [128, 512], F32))
    PD = ent(nc.psum_tensor("PD", [128, 512], F32))
    PE = ent(nc.psum_tensor("PE", [128, 512], F32))
    PB = ent(nc.psum_tensor("PB", [128, 512], F32))
    PS = ent(nc.psum_tensor("PS", [128, 512], F32))
    PT = ent(nc.psum_tensor("PT", [128, 1024], BF16))

    def fv(name, cols=None, off=0):
        o, c = fplan[name]
        if cols is None:
            cols = c - off
        assert off >= 0 and off + cols <= c, (name, off, cols)
        return AFt[:, o + off:o + off + cols]

    def bv(name, cols=None, off=0):
        o, c = bplan[name]
        if cols is None:
            cols = c - off
        assert off >= 0 and off + cols <= c, (name, off, cols)
        return ABt[:, o + off:o + off + cols]

    bufs = {}
    def B(name):
        if name not in bufs:
            bufs[name] = Buf(name)
        return bufs[name]

    SFB = [B(f"sF{s}") for s in range(NSUP)]

    def sf_bufs(c0, c1):
        lo = max(0, (c0 - 1)) // SUP
        hi = min(NSUP - 1, max(0, (c1 - 2)) // SUP)
        return [SFB[i] for i in range(lo, hi + 1)]

    PAB0, PAB1 = B("PA0"), B("PA1")
    PCB, PDB, PEB, PBB, PSB, PTB = (B(x) for x in ("PCp", "PDp", "PEp", "PBp", "PSp", "PTp"))

    op = kb.op

    def mm(pbuf, out_ap, lhsT_ap, rhs_ap, rbufs, start, stop, deps=()):
        return op("T", lambda e, o=out_ap, l=lhsT_ap, r=rhs_ap, st=start, sp=stop:
                  e.matmul(o, l, r, start=st, stop=sp),
                  reads=list(rbufs), writes=[pbuf], deps=deps)

    def trans(pbuf, out_ap, in_ap, ident_ap, rbufs, start=True, stop=True):
        return op("T", lambda e, o=out_ap, i=in_ap, d=ident_ap, st=start, sp=stop:
                  e.matmul(o, i, d, start=st, stop=sp, is_transpose=True),
                  reads=list(rbufs), writes=[pbuf])

    def vop(fn, reads=(), writes=(), deps=()):
        return op("V", fn, reads=reads, writes=writes, deps=deps)

    def sop(fn, reads=(), writes=(), deps=()):
        return op("S", fn, reads=reads, writes=writes, deps=deps)

    def dmaQ(sem, out_ap, in_ap, reads=(), writes=()):
        return op("Q", lambda e, o=out_ap, i=in_ap: e.dma_start(out=o, in_=i),
                  reads=reads, writes=writes, dma=sem)

    def dmaG(sem, out_ap, in_ap, reads=(), writes=()):
        return op("G", lambda e, o=out_ap, i=in_ap: e.dma_start(out=o, in_=i),
                  reads=reads, writes=writes, dma=sem)

    def jwarm(n):
        for _ in range(n):
            op("T", lambda e: e.matmul(PA[:, 512:1024], eye_b, warm_rhs,
                                       start=True, stop=True))

    CONSTF, CONSTB, INB = B("CONSTF"), B("CONSTB"), B("INCHAIN")
    eye_f = fv("CONST", 128, 0)
    ones_f = fv("CONST", 128, 128)
    pvecS = fv("CONST", PV, 256)
    eye_b = bv("CONB", 128, 0)
    ones_b = bv("CONB", 128, 128)
    ml_b = bv("CONB", 128, 256)
    mh_b = bv("CONB", 128, 384)
    maskb_b = bv("CONB", 384, 512)
    dmaG("IN", eye_f, eyefD, writes=[CONSTF, INB])
    dmaG("IN", ones_f, onesfD, writes=[CONSTF, INB])
    dmaG("IN", pvecS, pvecD, writes=[CONSTF, INB])
    dmaG("IN", eye_b, eyebD, writes=[CONSTB, INB])
    dmaG("IN", ones_b, onesbD, writes=[CONSTB, INB])
    dmaG("IN", ml_b, mlD, writes=[CONSTB, INB])
    dmaG("IN", mh_b, mhD, writes=[CONSTB, INB])
    dmaG("IN", maskb_b, maskbD, writes=[CONSTB, INB])

    def pvcol(name, k=0, rows=128):
        return pvecS[0:rows, PVIDX[name] + k: PVIDX[name] + k + 1]

    def tap(name, ap, rbufs, rows=128):
        if not debug:
            return
        shp = [rows] + [int(d) for d in ap.shape[1:]]
        t = nc.dram_tensor("tap_" + name, shp, ap.dtype, kind="ExternalOutput").ap()
        tap_t[name] = shp
        dmaG("TAPS", t, ap, reads=list(rbufs))

    def tap_sf(name):
        if not debug:
            return
        t = nc.dram_tensor("tap_" + name, [D, N], F32, kind="ExternalOutput").ap()
        tap_t[name] = (D, N)
        for c in range(NCH):
            dmaG("TAPS", t[c * 128:(c + 1) * 128, :], sF[:, c, 1:N + 1],
                 reads=list(SFB))

    warm_rhs = bv("CONB", 512, 256)
    for i in range(24):
        op("T", lambda e: e.matmul(PB[:, 0:512], eye_b, warm_rhs,
                                   start=True, stop=True),
           reads=[CONSTB], writes=[PBB])

    for c in range(NCH):
        dmafn = dmaG if c % 2 else dmaQ
        dmafn("IN", sF[:, c, 1:N + 1], xT[c * 128:(c + 1) * 128, :],
              writes=list(SFB) + [INB])
    vop(lambda e: e.memset(sF[:, :, 0:1], 0.0), writes=[SFB[0]])

    WST = [B("WST0"), B("WST1")]
    _wi = [0]
    _rr = [0]
    RR_ENGS = ("G", "V", "S")

    def cast_w(dst_ap, dst_buf, src_ap, rows, cols, eng="S", extra_w=()):
        done = 0
        while done < cols:
            w = min(1024, cols - done)
            i = _wi[0] % 2
            _wi[0] += 1
            stg = fv("WST", w, 1024 * i)[0:rows, :]
            dmaQ(f"WST{i}", stg, src_ap[:, done:done + w], writes=[WST[i]])
            dpc = dst_ap[0:rows, done:done + w]
            ee = eng
            if ee == "RR":
                ee = RR_ENGS[_rr[0] % 3]
                _rr[0] += 1
            wl = [dst_buf] + list(extra_w)
            if ee == "S":
                sop(lambda e, d=dpc, g=stg: e.copy(d, g),
                    reads=[WST[i]], writes=wl)
            elif ee == "G":
                op("G", lambda e, d=dpc, g=stg: e.tensor_copy(d, g),
                   reads=[WST[i]], writes=wl)
            else:
                vop(lambda e, d=dpc, g=stg: e.tensor_copy(d, g),
                    reads=[WST[i]], writes=wl)
            done += w

    fqkv_b = bv("WA", 12288).rearrange("p (c o) -> p c o", c=NCH)
    fwo_b = bv("WA", 4096, 12288).rearrange("p (c o) -> p c o", c=4)
    FQC = [B(f"FQC{c}") for c in range(NCH)]
    FOC = [B(f"FOC{c}") for c in range(4)]

    def emit_attn_cast(l, extra=()):
        for c in range(NCH):
            cast_w(fqkv_b[:, c, :], FQC[c], fqkv[l, c * 128:(c + 1) * 128, :],
                   128, FFI, eng="RR", extra_w=extra)
        for c in range(4):
            cast_w(fwo_b[:, c, :], FOC[c], fwo[l, c * 128:(c + 1) * 128, :],
                   128, D, eng="RR", extra_w=extra)

    rs_row = fv("STAT", 512, 0)
    c0_row = fv("STAT", 512, 512)
    nmld_row = fv("STAT", 1024, 0)
    STATB = B("STATB")
    SCRB = B("SCRB")
    GATEB = B("GATEB")
    SMALLB = B("SMALLB")
    sq_scr = fv("SCR", 1024, 512)

    def stat_finish(width):
        jwarm(6)
        mu = sq_scr[0:1, 0:width]
        sop(lambda e, w=width: e.activation(mu[0:1, 0:w], PE[0:1, 0:w], AF.Copy,
                                            scale=1.0 / D),
            reads=[PEB], writes=[SCRB])
        vop(lambda e, w=width: e.scalar_tensor_tensor(c0_row[0:1, 0:w], mu[0:1, 0:w],
                                                      1.0, mu[0:1, 0:w],
                                                      ALU.mult, ALU.mult),
            reads=[SCRB], writes=[STATB])
        vop(lambda e, w=width: e.scalar_tensor_tensor(rs_row[0:1, 0:w], PB[0:1, 0:w],
                                                      1.0 / D, c0_row[0:1, 0:w],
                                                      ALU.mult, ALU.subtract),
            reads=[PBB, STATB], writes=[STATB])
        sop(lambda e, w=width: e.activation(rs_row[0:1, 0:w], rs_row[0:1, 0:w],
                                            AF.Ln, bias=pvcol("eps", rows=1)),
            reads=[STATB, CONSTF], writes=[STATB])
        sop(lambda e, w=width: e.activation(rs_row[0:1, 0:w], rs_row[0:1, 0:w],
                                            AF.Exp, scale=-0.5),
            reads=[STATB], writes=[STATB])
        vop(lambda e, w=width: e.scalar_tensor_tensor(c0_row[0:1, 0:w], mu[0:1, 0:w],
                                                      -1.0, rs_row[0:1, 0:w],
                                                      ALU.mult, ALU.mult),
            reads=[SCRB, STATB], writes=[STATB])

    def ln_stats_sf(c0, width, shifted=False):
        for c in range(NCH):
            off = -1 if (shifted and c >= 4) else 0
            ap = sF[:, c, c0 + off: c0 + off + width]
            rb = sf_bufs(c0 + off, c0 + off + width)
            mm(PEB, PE[0:1, 0:width], ones_f[:, 0:1], ap, [CONSTF] + rb,
               start=(c == 0), stop=(c == NCH - 1))
        for c in range(NCH):
            off = -1 if (shifted and c >= 4) else 0
            ap = sF[:, c, c0 + off: c0 + off + width]
            rb = sf_bufs(c0 + off, c0 + off + width)
            sop(lambda e, a=ap, w=width: e.activation(sq_scr[:, 0:w], a, AF.Square),
                reads=rb, writes=[SCRB])
            if c < 2:
                jwarm(1)
            mm(PBB, PB[0:1, 0:width], ones_f[:, 0:1], sq_scr[:, 0:width],
               [CONSTF, SCRB], start=(c == 0), stop=(c == NCH - 1))
        stat_finish(width)

    def bcast_rows(width):
        mm(PCB, PC[:, 0:width], ones_f[0:1, 0:128], rs_row[0:1, 0:width],
           [CONSTF, STATB], start=True, stop=True)
        mm(PDB, PD[:, 0:width], ones_f[0:1, 0:128], c0_row[0:1, 0:width],
           [CONSTF, STATB], start=True, stop=True)

    WAB, WBB, WGB, SBFB = B("WAB"), B("WBB"), B("WGB"), B("SBFB")
    wqkv_b = bv("WA", 24576).rearrange("p (c o) -> p c o", c=NCH)
    wout_b = bv("WB", 8192).rearrange("p (c o) -> p c o", c=NCH)
    wg1_b = bv("WG", 512, 0).rearrange("p (c o) -> p c o", c=NCH)
    wg2_b = bv("WG", 1024, 512)

    for c in range(NCH):
        cast_w(wqkv_b[:, c, :], WAB, wqkvT[c * 128:(c + 1) * 128, :], 128, 3 * D,
               eng="RR")
        cast_w(wout_b[:, c, :], WBB, woutT[c * 128:(c + 1) * 128, :], 128, D,
               eng="RR")
        cast_w(wg1_b[:, c, :], WGB, wg1T[c * 128:(c + 1) * 128, :], 128, HD,
               eng="RR")
    cast_w(wg2_b, WGB, wg2T, 64, D, eng="RR")

    sbf = bv("SBF", 512)
    s32 = fv("S32", 512)
    S32B = B("S32B")
    vop(lambda e: e.memset(sbf[:], 0.0), writes=[SBFB])
    op("T", lambda e: e.matmul(PA[:, 512:1024], eye_b, warm_rhs,
                               start=True, stop=True), reads=[INB, CONSTB])
    jwarm(16)

    h1 = bv("YBF", 2048).rearrange("p (c t) -> p c t", c=NCH)
    H1C = [B(f"H1C{c}") for c in range(NCH)]
    qf = bv("QF", 2048).rearrange("p (c t) -> p c t", c=NCH)
    kf = bv("KF", 2048).rearrange("p (c t) -> p c t", c=NCH)
    vt = bv("VT", 2048).rearrange("p (t o) -> p t o", t=2)
    kst = bv("KST", 2048).rearrange("p (t o) -> p t o", t=2)
    QFB, KFB, VTB, KSTB = B("QFB"), B("KFB"), B("VTB"), B("KSTB")
    atm = bv("ATM", 768).rearrange("p (i c) -> p i c", i=2)
    ATMB = [B("ATM0"), B("ATM1")]
    g1f = bv("ATM", 256, 768)
    G1B = B("G1B")
    gate = fv("GATE", 2048).rearrange("p (c t) -> p c t", c=NCH)
    oln = bv("OF", 2048).rearrange("p (c t) -> p c t", c=NCH)
    OLNC = [B(f"OLNC{c}") for c in range(NCH)]
    scr = fv("SCR", 1024, 0)

    ps_open = [False]

    for s in range(NSUP):
        c0 = 1 + s * SUP
        sfb = [SFB[s]]

        ln_stats_sf(c0, SUP)
        bcast_rows(SUP)
        jwarm(3)
        for c in range(NCH):
            vop(lambda e, c=c, c0=c0: e.tensor_tensor(scr[:, 0:SUP],
                                                      sF[:, c, c0:c0 + SUP],
                                                      PC[:, 0:SUP], ALU.mult),
                reads=sfb + [PCB], writes=[SCRB])
            vop(lambda e, c=c: e.tensor_tensor(h1[:, c, :], scr[:, 0:SUP],
                                               PD[:, 0:SUP], ALU.add),
                reads=[SCRB, PDB], writes=[H1C[c]])

        if s == 0:
            tap("h1", h1, list(H1C))
        for oc in range(16):
            pb, pbuf = (PC, PCB) if oc % 2 == 0 else (PD, PDB)
            for kc in range(NCH):
                mm(pbuf, pb[:, 0:SUP], wqkv_b[:, kc, oc * 128:(oc + 1) * 128],
                   h1[:, kc, :], [WAB, H1C[kc]], start=(kc == 0), stop=(kc == NCH - 1))
            dst, dbuf = (qf, QFB) if oc < 8 else (kf, KFB)
            sop(lambda e, d=dst[:, oc % 8, :], p=pb: e.activation(d, p[:, 0:SUP],
                                                                  AF.Silu),
                reads=[pbuf], writes=[dbuf])
        for tt in range(2):
            pb, pbuf = (PC, PCB) if tt == 0 else (PD, PDB)
            for half in range(2):
                for kc in range(NCH):
                    mm(pbuf, pb[:, 0:512], h1[:, kc, tt * 128:(tt + 1) * 128],
                       wqkv_b[:, kc, 2048 + half * 512: 2048 + (half + 1) * 512],
                       [WAB, H1C[kc]], start=(kc == 0), stop=(kc == NCH - 1))
                sop(lambda e, t=tt, h=half, p=pb: e.activation(
                        vt[:, t, h * 512:(h + 1) * 512], p[:, 0:512], AF.Copy),
                    reads=[pbuf], writes=[VTB])
        for tt in range(2):
            for c in range(NCH):
                trans(PTB, PT[:, c * 128:(c + 1) * 128],
                      kf[:, c, tt * 128:(tt + 1) * 128], eye_b, [KFB, CONSTB],
                      start=(c == 0), stop=(c == NCH - 1))
            vop(lambda e, t=tt: e.tensor_copy(kst[:, t, :], PT[:]),
                reads=[PTB], writes=[KSTB])

        if s == 0:
            tap("qf", qf, [QFB])
            tap("kf", kf, [KFB])
            tap("vt", vt, [VTB])
            tap("kst", kst, [KSTB])
        for kc in range(NCH):
            mm(PEB, PE[0:64, 0:SUP], wg1_b[:, kc, :], h1[:, kc, :],
               [WGB, H1C[kc]], start=(kc == 0), stop=(kc == NCH - 1))
        sop(lambda e: e.activation(g1f[0:64, :], PE[0:64, 0:SUP], AF.Identity,
                                   bias=pvcol("b_g1", rows=64)),
            reads=[PEB, CONSTF], writes=[G1B])
        for dc in range(NCH):
            pb, pbuf = (PC, PCB) if dc % 2 == 0 else (PD, PDB)
            mm(pbuf, pb[:, 0:SUP], wg2_b[0:64, dc * 128:(dc + 1) * 128],
               g1f[0:64, :], [WGB, G1B], start=True, stop=True)
            sop(lambda e, d=dc, p=pb: e.activation(gate[:, d, :], p[:, 0:SUP],
                                                   AF.Sigmoid, bias=pvcol("b_g2", d)),
                reads=[pbuf, CONSTF], writes=[GATEB])

        if s == 0:
            tap("gate", gate, [GATEB])
        for pr in range(8):
            opb, opbuf = (PE, PEB) if pr % 2 == 0 else (PB, PBB)
            for hh in range(2):
                h = pr * 2 + hh
                jwarm(1)
                hc, hp = h // 2, (h % 2) * 64
                ab = ATMB[h % 2]
                mm(PAB0, PA[:, 0:256], kf[hp:hp + 64, hc, 0:128],
                   qf[hp:hp + 64, hc, 0:256], [KFB, QFB], start=True, stop=False)
                mm(PAB0, PA[:, 256:384], kf[hp:hp + 64, hc, 128:256],
                   qf[hp:hp + 64, hc, 128:256], [KFB, QFB], start=False, stop=True)
                vop(lambda e, i=h % 2: e.tensor_tensor(atm[:, i, :], PA[:, 0:384],
                                                       maskb_b, ALU.mult),
                    reads=[PAB0, CONSTB], writes=[ab])
                if s == 0 and h == 0:
                    tap("atm_h0", atm[:, 0, :], [ab])
                first_o = [True]
                def omm(lhs, rhs, rb, stop):
                    st = first_o[0]
                    first_o[0] = False
                    mm(opbuf, lhs[0], lhs[1], rhs, rb, start=st, stop=stop)
                if s > 0:
                    omm((opb[hp:hp + 64, 0:256],
                         sbf[hp:hp + 64, hc * 64:(hc + 1) * 64]),
                        qf[hp:hp + 64, hc, 0:256], [SBFB, QFB], False)
                omm((opb[hp:hp + 64, 0:256], vt[:, 0, h * 64:(h + 1) * 64]),
                    atm[:, h % 2, 0:256], [VTB, ab], False)
                omm((opb[hp:hp + 64, 128:256], vt[:, 1, h * 64:(h + 1) * 64]),
                    atm[:, h % 2, 256:384], [VTB, ab], stop=True)
                mm(PSB, PS[hp:hp + 64, hc * 64:(hc + 1) * 64],
                   kst[:, 0, h * 64:(h + 1) * 64], vt[:, 0, h * 64:(h + 1) * 64],
                   [KSTB, VTB], start=True, stop=False)
                mm(PSB, PS[hp:hp + 64, hc * 64:(hc + 1) * 64],
                   kst[:, 1, h * 64:(h + 1) * 64], vt[:, 1, h * 64:(h + 1) * 64],
                   [KSTB, VTB], start=False, stop=True)
            vop(lambda e, pr=pr, p=opb: e.tensor_tensor(gate[:, pr, :],
                                                        p[:, 0:256],
                                                        gate[:, pr, :], ALU.mult),
                reads=[opbuf, GATEB], writes=[GATEB])
        if s == 0:
            vop(lambda e: e.tensor_copy(s32[:], PS[:]), reads=[PSB], writes=[S32B])
        else:
            vop(lambda e: e.tensor_tensor(s32[:], PS[:], s32[:], ALU.add),
                reads=[PSB, S32B], writes=[S32B])
        if s < NSUP - 1:
            sop(lambda e: e.copy(sbf[:], s32[:]), reads=[S32B], writes=[SBFB])

        if s <= 1:
            tap(f"og{s}", gate, [GATEB])
            tap(f"sbf{s}", sbf, [SBFB])
        for c in range(NCH):
            mm(PEB, PE[0:1, 0:SUP], ones_f[:, 0:1], gate[:, c, :],
               [CONSTF, GATEB], start=(c == 0), stop=(c == NCH - 1))
        for c in range(NCH):
            sop(lambda e, c=c: e.activation(sq_scr[:, 0:SUP], gate[:, c, :],
                                            AF.Square),
                reads=[GATEB], writes=[SCRB])
            if c < 2:
                jwarm(1)
            mm(PBB, PB[0:1, 0:SUP], ones_f[:, 0:1], sq_scr[:, 0:SUP],
               [CONSTF, SCRB], start=(c == 0), stop=(c == NCH - 1))
        stat_finish(SUP)
        bcast_rows(SUP)
        jwarm(3)
        for c in range(NCH):
            vop(lambda e, c=c: e.tensor_tensor(scr[:, 0:SUP], gate[:, c, :],
                                               PC[:, 0:SUP], ALU.mult),
                reads=[GATEB, PCB], writes=[SCRB])
            vop(lambda e: e.tensor_tensor(scr[:, 256:256 + SUP], scr[:, 0:SUP],
                                          PD[:, 0:SUP], ALU.add),
                reads=[PDB, SCRB], writes=[SCRB])
            sop(lambda e, c=c: e.activation(oln[:, c, :], scr[:, 256:256 + SUP],
                                            AF.Identity, bias=pvcol("beta", c),
                                            scale=pvcol("gamma", c)),
                reads=[SCRB, CONSTF], writes=[OLNC[c]])

        if s == 0:
            tap("oln", oln, list(OLNC))
        jwarm(3)
        for dc in range(NCH):
            pb, pbuf = (PC, PCB) if dc % 2 == 0 else (PD, PDB)
            for kc in range(NCH):
                mm(pbuf, pb[:, 0:SUP], wout_b[:, kc, dc * 128:(dc + 1) * 128],
                   oln[:, kc, :], [WBB, OLNC[kc]], start=(kc == 0), stop=(kc == NCH - 1))
            vop(lambda e, d=dc, p=pb, c0=c0: e.tensor_tensor(
                    sF[:, d, c0:c0 + SUP], p[:, 0:SUP], sF[:, d, c0:c0 + SUP],
                    ALU.add),
                reads=[pbuf] + sfb, writes=sfb)
            dmaG("SPILL", x2s[dc * 128:(dc + 1) * 128, s * SUP:(s + 1) * SUP],
                 sF[:, dc, c0:c0 + SUP], reads=sfb, writes=[B("X2SB")])

    emit_attn_cast(0, extra=(WAB,))

    kb.barrier()
    tap_sf("x2")

    for g in range(4):
        c0 = 1 + g * 512
        gb = sf_bufs(c0, c0 + 512)
        ln_stats_sf(c0, 512)
        bcast_rows(512)
        for c in range(NCH):
            vop(lambda e, c=c, c0=c0: e.tensor_tensor(scr[:, 0:512],
                                                      sF[:, c, c0:c0 + 512],
                                                      PC[:, 0:512], ALU.mult),
                reads=gb + [PCB], writes=[SCRB])
            vop(lambda e, c=c, c0=c0: e.tensor_tensor(sF[:, c, c0:c0 + 512],
                                                      scr[:, 0:512],
                                                      PD[:, 0:512], ALU.add),
                reads=[SCRB, PDB], writes=gb)

    kb.barrier()
    tap_sf("y0")

    YBND = B("YBNDB")
    ybnd = fv("YBND", 32).rearrange("p (c k) -> p c k", c=NCH)

    w1v = [bv("WA", 2048, i * 2048).rearrange("p (c o) -> p c o", c=NCH)
           for i in range(2)]
    W1B = [B("W1B0"), B("W1B1")]
    w2v = [bv("WB", 1024, i * 1024) for i in range(8)]
    W2B = [B(f"W2B{i}") for i in range(8)]

    qfc = bv("QF", 2048).rearrange("p (b t) -> p b t", b=4)
    kfc = bv("KF", 4096).rearrange("p (w b t) -> p w b t", w=2, b=4)
    vtc = bv("VT", 4096).rearrange("p (w b t) -> p w b t", w=2, b=4)
    QFCB = B("QFCB")
    KFCB = [B("KFC0"), B("KFC1")]
    VTCB = [B("VTC0"), B("VTC1")]
    pa_junk = bv("KST", 2048)
    PAJB = B("PAJB")
    ptile8 = bv("ATM", 1536).rearrange("p (i c) -> p i c", i=12)
    PTL8 = [B(f"PTL{i}") for i in range(12)]
    _pabk = [B("PABK0"), B("PABK1")]
    PASL = [_pabk[i // 4] for i in range(8)]
    bcastB = fv("GATE", 2048, 0)
    BCB = None
    ofc = bv("OF", 2048).rearrange("p (b t) -> p b t", b=4)
    OFCB = B("OFCB")
    ybf = bv("YBF", 2064).rearrange("p (i c k) -> p i c k", i=2, c=NCH)
    YBFB = [B("YBF0"), B("YBF1")]
    tabw = fv("TAB", 1024).rearrange("p (i k) -> p i k", i=4)
    TABB = B("TABB")
    qscale_t = fv("SCR", 256, 1024).rearrange("p (i k) -> p i k", i=4)
    QSB = B("QSB")
    scrA = fv("GATE", 512, 0)
    scrB2 = fv("GATE", 512, 512)
    scrC = fv("GATE", 512, 1024)
    scrD = fv("SCR", 512, 0)
    def nm_v(par):
        return fv("SMALL", 8, par * 48 + 0)
    def dsP_v(par):
        return fv("SMALL", 8, par * 48 + 8)
    def dsC_v(par):
        return fv("SMALL", 8, par * 48 + 16)
    def lnd_v(par):
        return fv("SMALL", 8, par * 48 + 24)
    def nmld_v(par):
        return fv("SMALL", 8, par * 48 + 32)
    def tmp1_v(par):
        return fv("SMALL", 8, par * 48 + 40)
    rn_t = fv("SMALL", 8, 96)
    rn_k = fv("SMALL", 8, 104)
    RNB = [B("RNQ"), B("RNK")]
    WKB = B("WKB")
    PAJQ, PAJK = B("PAJQ"), B("PAJK")
    X2R = [B("X2R0"), B("X2R1")]

    hg_regions = []
    for i in range(20):
        hg_regions.append(bv("WA", 1024, 4096 + i * 1024))
    hg_regions.append(bv("QF", 1024, 0))
    hg_regions.append(bv("QF", 1024, 1024))
    HGB = B("HGB")
    ln4f = ABt[:, bplan["KF"][0]: bplan["KF"][0] + 8192].rearrange(
        "p (c t) -> p c t", c=NCH)
    LN4C = [B(f"LN4C{i}") for i in range(2 * NCH)]
    gel = fv("GATE", 1024, 0)

    def rotary_pair(l, w, tt, qp, qpb, kp, kpb, kfw, kb_cur):
        t1 = [fv("GATE", 512, 0), fv("WST", 512, 0)]
        Ax = [fv("GATE", 512, 512), fv("WST", 512, 512)]
        Cx = [fv("GATE", 512, 1024), fv("WST", 512, 1024)]
        Dx = [fv("SCR", 512, 0), fv("WST", 512, 1536)]
        DBf = [SCRB, WKB]
        GBf = [GATEB, WKB]
        rn = [rn_t, rn_k]
        ps = [(qp, qpb), (kp, kpb)]
        pj = [pa_junk[:, 1024:1536], pa_junk[:, 1536:2048]]
        PJb = [PAJQ, PAJK]
        cs = [tabw[:, 0, tt * 64:(tt + 1) * 64], tabw[:, 2, tt * 64:(tt + 1) * 64]]
        sn = [tabw[:, 1, tt * 64:(tt + 1) * 64], tabw[:, 3, tt * 64:(tt + 1) * 64]]
        h3 = lambda a: a.rearrange("p (h d) -> p h d", h=HC)
        for i in (0, 1):
            sop(lambda e, i=i: e.activation(Ax[i], ps[i][0][:, 0:512], AF.Square),
                reads=[ps[i][1]], writes=[GBf[i]])
        for i in (0, 1):
            vop(lambda e, i=i: e.reduce_sum(rn[i][:, 0:8], h3(Ax[i]), AX.X),
                reads=[GBf[i]], writes=[RNB[i]])
        for i in (0, 1):
            sop(lambda e, i=i: e.activation(rn[i][:, 0:8], rn[i][:, 0:8], AF.Sqrt),
                reads=[RNB[i]], writes=[RNB[i]])
        for i in (0, 1):
            vop(lambda e, i=i: e.tensor_scalar_max(rn[i][:, 0:8], rn[i][:, 0:8],
                                                   1e-12),
                reads=[RNB[i]], writes=[RNB[i]])
            vop(lambda e, i=i: e.reciprocal(rn[i][:, 0:8], rn[i][:, 0:8]),
                reads=[RNB[i]], writes=[RNB[i]])
        for i in (0, 1):
            vop(lambda e, i=i: e.tensor_tensor(
                    h3(t1[i]), h3(ps[i][0][:, 0:512]),
                    rn[i][:, 0:8].unsqueeze(2).broadcast_to([128, HC, HD]),
                    ALU.mult),
                reads=[ps[i][1], RNB[i]], writes=[GBf[i]])
        for i in (0, 1):
            vop(lambda e, i=i: e.tensor_tensor(
                    h3(Ax[i]), h3(t1[i]),
                    cs[i].unsqueeze(1).broadcast_to([128, HC, HD]), ALU.mult),
                reads=[GBf[i], TABB], writes=[GBf[i]])
        for i in (0, 1):
            vop(lambda e, i=i: e.tensor_tensor(
                    h3(Cx[i])[:, :, 0:32], h3(t1[i])[:, :, 32:64],
                    sn[i][:, 0:32].unsqueeze(1).broadcast_to([128, HC, 32]),
                    ALU.mult),
                reads=[GBf[i], TABB], writes=[GBf[i]])
            vop(lambda e, i=i: e.tensor_tensor(
                    h3(Cx[i])[:, :, 32:64], h3(t1[i])[:, :, 0:32],
                    sn[i][:, 32:64].unsqueeze(1).broadcast_to([128, HC, 32]),
                    ALU.mult),
                reads=[GBf[i], TABB], writes=[GBf[i]])
        for i in (0, 1):
            vop(lambda e, i=i: e.tensor_sub(
                    h3(Dx[i])[:, :, 0:32], h3(Ax[i])[:, :, 0:32],
                    h3(Cx[i])[:, :, 0:32]),
                reads=[GBf[i]], writes=[DBf[i]])
            vop(lambda e, i=i: e.tensor_add(
                    h3(Dx[i])[:, :, 32:64], h3(Ax[i])[:, :, 32:64],
                    h3(Cx[i])[:, :, 32:64]),
                reads=[GBf[i]], writes=[DBf[i]])
        for i in (0, 1):
            sop(lambda e, i=i: e.copy(pj[i], Dx[i]),
                reads=[DBf[i]], writes=[PJb[i]])
        for i in (0, 1):
            for blk in range(4):
                trans(PTB, PT[:, i * 512 + blk * 128: i * 512 + (blk + 1) * 128],
                      pj[i][:, blk * 128:(blk + 1) * 128], eye_b,
                      [PJb[i], CONSTB],
                      start=(i == 0 and blk == 0), stop=(i == 1 and blk == 3))
        vop(lambda e, tt=tt: e.tensor_copy(
                qfc[:, :, tt * 128:(tt + 1) * 128],
                PT[:, 0:512].rearrange("p (b c) -> p b c", b=4)),
            reads=[PTB], writes=[QFCB])
        vop(lambda e, tt=tt, kfw=kfw: e.tensor_copy(
                kfw[:, :, tt * 128:(tt + 1) * 128],
                PT[:, 512:1024].rearrange("p (b c) -> p b c", b=4)),
            reads=[PTB], writes=[kb_cur])

    def stage_c_attn(l):
        if l > 0:
            emit_attn_cast(l)
        op("V", lambda e: e.nop(), writes=[WST[0], WST[1], WKB])
        for i in range(4):
            dmaG("QS", qscale_t[:, i, :], qscaleD[l, i], writes=[QSB])
        for w in range(1, NW):
            vop(lambda e, w=w: e.tensor_copy(ybnd[:, :, w:w + 1],
                                             sF[:, :, w * WIN:w * WIN + 1]),
                reads=sf_bufs(w * WIN, w * WIN + 1), writes=[YBND])
        jwarm(12)

        for w in range(NW):
            wsl = w % 2
            kfw, kfp = kfc[:, wsl], kfc[:, 1 - wsl]
            vtw, vtp = vtc[:, wsl], vtc[:, 1 - wsl]
            kb_cur, kb_prev = KFCB[wsl], KFCB[1 - wsl]
            vb_cur, vb_prev = VTCB[wsl], VTCB[1 - wsl]
            wsf = sf_bufs(1 + w * WIN, 1 + (w + 1) * WIN)

            dmaQ("TAB", tabw[:, :, :], tabsD[:, w].transpose([1, 0, 2]), writes=[TABB])
            dmaQ("TABL", fv("SCR", 1024, 0).rearrange("p (i k) -> p i k", i=4),
                 tabsloD[:, w].transpose([1, 0, 2]), writes=[SCRB])
            vop(lambda e: e.tensor_tensor(tabw[:, :, :], tabw[:, :, :],
                                          fv("SCR", 1024, 0).rearrange(
                                              "p (i k) -> p i k", i=4), ALU.add),
                reads=[SCRB, TABB], writes=[TABB])
            vop(lambda e: e.tensor_tensor(
                    tabw.rearrange("p i (a b) -> p i a b", a=TW),
                    tabw.rearrange("p i (a b) -> p i a b", a=TW),
                    qscale_t[:, :, :].unsqueeze(2).broadcast_to([128, 4, TW, HD]),
                    ALU.mult),
                reads=[QSB, TABB], writes=[TABB])

            def emit_ybf(tt):
                t0g = 1 + w * WIN + tt * 128
                ti = tt % 2
                if w > 0 and tt == 0:
                    op("G", lambda e, ti=ti, t0g=t0g: e.tensor_copy(
                           ybf[:, ti, :, 1:129], sF[:, :, t0g:t0g + 128]),
                       reads=wsf, writes=[YBFB[ti]])
                    op("G", lambda e, ti=ti, w=w: e.tensor_copy(
                           ybf[:, ti, :, 0:1], ybnd[:, :, w:w + 1]),
                       reads=[YBND], writes=[YBFB[ti]])
                else:
                    op("G", lambda e, ti=ti, t0g=t0g: e.tensor_copy(
                           ybf[:, ti, :, 0:129], sF[:, :, t0g - 1:t0g + 128]),
                       reads=sf_bufs(t0g - 1, t0g + 128), writes=[YBFB[ti]])

            emit_ybf(0)
            for tt in range(TW):
                ti = tt % 2
                if tt % 2 == 0:
                    qp, qpb = PC, PCB
                    kp, kpb = PD, PDB
                    vp, vpb = PE[:, 0:512], PEB
                else:
                    qp, qpb = PA[:, 0:512], PASL[0]
                    kp, kpb = PA[:, 512:1024], PASL[4]
                    vp, vpb = PB[:, 0:512], PBB
                for kc in range(NCH):
                    lhs = ybf[:, ti, kc, 1:129] if kc < 4 else ybf[:, ti, kc, 0:128]
                    mm(qpb, qp[:, 0:512], lhs, fqkv_b[:, kc, 0:512],
                       [YBFB[ti], FQC[kc]], start=(kc == 0), stop=(kc == NCH - 1))
                    mm(kpb, kp[:, 0:512], lhs, fqkv_b[:, kc, 512:1024],
                       [YBFB[ti], FQC[kc]], start=(kc == 0), stop=(kc == NCH - 1))
                    mm(vpb, vp[:, 0:512], lhs, fqkv_b[:, kc, 1024:1536],
                       [YBFB[ti], FQC[kc]], start=(kc == 0), stop=(kc == NCH - 1))
                if tt + 1 < TW:
                    emit_ybf(tt + 1)
                sop(lambda e, tt=tt, vv=vtw, vp=vp: e.activation(vv[:, tt, :],
                                                                 vp[:, 0:512], AF.Copy),
                    reads=[vpb], writes=[vb_cur])
                rotary_pair(l, w, tt, qp, qpb, kp, kpb, kfw, kb_cur)

            slot_ctr = [0]

            def slot_ap(sl):
                if sl < 8:
                    return PA[:, sl * 128:(sl + 1) * 128], PASL[sl]
                return PC[:, (sl - 8) * 128:(sl - 7) * 128], PCB

            def kts_of(tt):
                return ([("p", j) for j in range(tt, TW)] if w > 0 else []) \
                    + [("c", j) for j in range(0, tt + 1)]

            def pass1(tt):
                par = tt % 2
                nm, dsP, dsC = nm_v(par), dsP_v(par), dsC_v(par)
                tmp1 = tmp1_v(par)
                wp = (TW - tt) * 128
                wc = (tt + 1) * 128
                for h in range(HC):
                    hp, hpr = (h % 2) * 64, h // 2
                    ppv = PD[:, 0:wp]
                    pcu = PE[:, 0:wc]
                    if w > 0:
                        op("T", lambda e, o=ppv, l1=qfc[hp:hp + 64, hpr, tt * 128:(tt + 1) * 128],
                           r=kfp[hp:hp + 64, hpr, tt * 128:512]:
                           e.matmul(o, l1, r, start=True, stop=True),
                           reads=[QFCB, kb_prev], writes=[PDB])
                        vop(lambda e, o=ppv: e.tensor_tensor(o[:, 0:128], o[:, 0:128],
                                                             ml_b, ALU.add),
                            reads=[CONSTB], writes=[PDB])
                    op("T", lambda e, o=pcu, l1=qfc[hp:hp + 64, hpr, tt * 128:(tt + 1) * 128],
                       r=kfw[hp:hp + 64, hpr, 0:wc]:
                       e.matmul(o, l1, r, start=True, stop=True),
                       reads=[QFCB, kb_cur], writes=[PEB])
                    vop(lambda e, o=pcu, wc=wc: e.tensor_tensor(
                            o[:, wc - 128:wc], o[:, wc - 128:wc], mh_b, ALU.add),
                        reads=[CONSTB], writes=[PEB])
                    if w > 0:
                        vop(lambda e, o=ppv: e.reduce_max(tmp1[:, 0:1], o, AX.X,
                                                          negate=True),
                            reads=[PDB], writes=[SMALLB])
                        vop(lambda e, o=pcu: e.reduce_max(tmp1[:, 1:2], o, AX.X,
                                                          negate=True),
                            reads=[PEB], writes=[SMALLB])
                        vop(lambda e, h=h: e.tensor_tensor(nm[:, h:h + 1],
                                                           tmp1[:, 0:1],
                                                           tmp1[:, 1:2], ALU.min),
                            reads=[SMALLB], writes=[SMALLB])
                        sop(lambda e, o=ppv, wp=wp, h=h: e.activation(
                                pa_junk[:, 0:wp], o, AF.Exp,
                                bias=nm[:, h:h + 1], accum_out=dsP[:, h:h + 1]),
                            reads=[PDB, SMALLB], writes=[PAJB, SMALLB])
                    else:
                        vop(lambda e, o=pcu, h=h: e.reduce_max(
                                nm[:, h:h + 1], o, AX.X, negate=True),
                            reads=[PEB], writes=[SMALLB])
                    sop(lambda e, o=pcu, wc=wc, h=h: e.activation(
                            pa_junk[:, 512:512 + wc], o, AF.Exp,
                            bias=nm[:, h:h + 1], accum_out=dsC[:, h:h + 1]),
                        reads=[PEB, SMALLB], writes=[PAJB, SMALLB])

            def nmphase(tt):
                par = tt % 2
                nm, dsP, dsC, lnd, nmld = (nm_v(par), dsP_v(par), dsC_v(par),
                                           lnd_v(par), nmld_v(par))
                bcB = bcastB[:, par * 1024:(par + 1) * 1024]
                if w > 0:
                    vop(lambda e: e.tensor_add(dsC[:, 0:8], dsC[:, 0:8], dsP[:, 0:8]),
                        reads=[SMALLB], writes=[SMALLB])
                sop(lambda e: e.activation(lnd[:, 0:8], dsC[:, 0:8], AF.Ln),
                    reads=[SMALLB], writes=[SMALLB])
                vop(lambda e: e.tensor_sub(nmld[:, 0:8], nm[:, 0:8], lnd[:, 0:8]),
                    reads=[SMALLB], writes=[SMALLB])
                for grp in range(2):
                    for hh in range(4):
                        h = grp * 4 + hh
                        trans(PBB, PB[0:1, hh * 128:(hh + 1) * 128],
                              nmld[:, h:h + 1], eye_f, [SMALLB, CONSTF],
                              start=(hh == 0), stop=(hh == 3))
                    vop(lambda e, g=grp: e.tensor_copy(
                            nmld_row[0:1, g * 512:(g + 1) * 512], PB[0:1, 0:512]),
                        reads=[PBB], writes=[STATB])
                for half in range(2):
                    mm(PBB, PB[:, 0:512], ones_f[0:1, 0:128],
                       nmld_row[0:1, half * 512:(half + 1) * 512],
                       [STATB, CONSTF], start=True, stop=True)
                    sop(lambda e, half=half, b=bcB: e.copy(
                            b[:, half * 512:(half + 1) * 512], PB[:, 0:512]),
                        reads=[PBB], writes=[GATEB])

            def pass2(tt):
                par = tt % 2
                bcB = bcastB[:, par * 1024:(par + 1) * 1024]
                nkts = (TW - tt if w > 0 else 0) + (tt + 1)

                def emit_at(h, sls):
                    hp, hpr = (h % 2) * 64, h // 2
                    for idx, (wk, j) in enumerate(kts_of(tt)):
                        kfx, kbx = (kfp, kb_prev) if wk == "p" else (kfw, kb_cur)
                        sl = slot_ctr[0] % 12
                        slot_ctr[0] += 1
                        sls.append((sl, wk, j))
                        pap, pbf = slot_ap(sl)
                        op("T", lambda e, o=pap, l1=kfx[hp:hp + 64, hpr, j * 128:(j + 1) * 128],
                           r=qfc[hp:hp + 64, hpr, tt * 128:(tt + 1) * 128]:
                           e.matmul(o, l1, r, start=True, stop=True),
                           reads=[kbx, QFCB], writes=[pbf])

                def slot_runs(sls):
                    runs = []
                    for sl, wk, j in sls:
                        if runs and runs[-1][0] + runs[-1][1] == sl and sl % 4 != 0:
                            runs[-1][1] += 1
                        else:
                            runs.append([sl, 1])
                    return runs

                def emit_rest(h, sls):
                    hp = (h % 2) * 64
                    oc0 = (h // 2) * 128
                    for s0, ln_ in slot_runs(sls):
                        pbf = slot_ap(s0)[1]
                        wide = PA if s0 < 8 else PC
                        off = s0 * 128 if s0 < 8 else (s0 - 8) * 128
                        vop(lambda e, w_=wide, off=off, ln_=ln_, h=h, b=bcB:
                            e.tensor_tensor(
                                w_[:, off:off + ln_ * 128].rearrange(
                                    "p (a b) -> p a b", a=ln_),
                                w_[:, off:off + ln_ * 128].rearrange(
                                    "p (a b) -> p a b", a=ln_),
                                b[:, h * 128:(h + 1) * 128].unsqueeze(1)
                                .broadcast_to([128, ln_, 128]), ALU.add),
                            reads=[GATEB], writes=[pbf])
                    for sl, wk, j in sls:
                        pap, pbf = slot_ap(sl)
                        if wk == "p" and j == tt:
                            vop(lambda e, o=pap: e.tensor_tensor(o, o, mh_b, ALU.add),
                                reads=[CONSTB], writes=[pbf])
                        if wk == "c" and j == tt:
                            vop(lambda e, o=pap: e.tensor_tensor(o, o, ml_b, ALU.add),
                                reads=[CONSTB], writes=[pbf])
                    for s0, ln_ in slot_runs(sls):
                        pbf = slot_ap(s0)[1]
                        wide = PA if s0 < 8 else PC
                        off = s0 * 128 if s0 < 8 else (s0 - 8) * 128
                        pp = s0
                        sop(lambda e, w_=wide, off=off, ln_=ln_, pp=pp: e.activation(
                                ptile8[:, pp:pp + ln_, :].rearrange("p a b -> p (a b)"),
                                w_[:, off:off + ln_ * 128], AF.Exp),
                            reads=[pbf],
                            writes=[PTL8[s0 + k] for k in range(ln_)])
                    for idx, (sl, wk, j) in enumerate(sls):
                        vtx, vbx = (vtp, vb_prev) if wk == "p" else (vtw, vb_cur)
                        op("T", lambda e, pp=sl, l1=vtx[:, j, h * 64:(h + 1) * 64],
                           hp=hp, oc0=oc0, st=(idx == 0), sp=(idx == nkts - 1):
                           e.matmul(PS[hp:hp + 64, oc0:oc0 + 128], l1,
                                    ptile8[:, pp, :], start=st, stop=sp),
                           reads=[vbx, PTL8[sl]], writes=[PSB])

                prev = None
                for h in range(HC):
                    sls = []
                    emit_at(h, sls)
                    if prev is not None:
                        emit_rest(prev[0], prev[1])
                    prev = (h, sls)
                emit_rest(prev[0], prev[1])
                for pr in range(4):
                    vop(lambda e, pr=pr, tt=tt: e.tensor_copy(
                            ofc[:, pr, tt * 128:(tt + 1) * 128],
                            PS[:, pr * 128:(pr + 1) * 128]),
                        reads=[PSB], writes=[OFCB])

            pass1(0)
            nmphase(0)
            for tt in range(TW):
                if tt + 1 < TW:
                    pass1(tt + 1)
                pass2(tt)
                if tt + 1 < TW:
                    nmphase(tt + 1)

            c0 = 1 + w * WIN
            for dc in range(NCH):
                pb, pbuf = (PC, PCB) if dc % 2 == 0 else (PD, PDB)
                for kc in range(4):
                    mm(pbuf, pb[:, 0:WIN], fwo_b[:, kc, dc * 128:(dc + 1) * 128],
                       ofc[:, kc, :], [FOC[kc], OFCB], start=(kc == 0), stop=(kc == 3))
                vop(lambda e, d=dc, p=pb, c0=c0: e.tensor_tensor(
                        sF[:, d, c0:c0 + WIN], p[:, 0:WIN],
                        sF[:, d, c0:c0 + WIN], ALU.add),
                    reads=[pbuf] + wsf, writes=wsf)
            if l == 0:
                tap(f"vt0_w{w}", vtc[:, 0], [VTCB[0]])
                tap(f"kf0_w{w}", kfc[:, 0], [KFCB[0]])

    fw1r = [fw1[l].rearrange("(c p) o -> p c o", p=128) for l in range(2)]

    def load_w1(l, osn, slot):
        o0, ow = OSR[osn]
        for part, off in ((0, o0), (1, HID + o0)):
            i = _wi[0] % 2
            _wi[0] += 1
            stg = fv("WST", 1024, 1024 * i).rearrange(
                "p (c o) -> p c o", c=NCH)[:, :, 0:ow]
            dmaQ(f"WST{i}", stg, fw1r[l][:, :, off:off + ow], writes=[WST[i]])
            eng = "G" if part == 0 else "V"
            op(eng, lambda e, s=slot, ow=ow, g=stg, part=part: e.tensor_copy(
                   w1v[s][:, :, part * 128:part * 128 + ow], g),
               reads=[WST[i]], writes=[W1B[slot]])

    def load_w2(l, osn, slot):
        o0, ow = OSR[osn]
        i = _wi[0] % 2
        _wi[0] += 1
        stg = fv("WST", 1024, 1024 * i)[0:ow, :]
        dmaQ(f"WST{i}", stg, fw2[l, o0:o0 + ow, :], writes=[WST[i]])
        eng = "G" if slot % 2 == 0 else "V"
        op(eng, lambda e, s=slot, ow=ow, g=stg: e.tensor_copy(w2v[s][0:ow, :], g),
           reads=[WST[i]], writes=[W2B[slot]])

    def stage_c_mlp(l, last):
        vop(lambda e: e.tensor_copy(ybnd[:, :, 0:1], sF[:, :, N // 2:N // 2 + 1]),
            reads=sf_bufs(N // 2, N // 2 + 1), writes=[YBND])
        for half in range(2):
            hb0 = 1 + half * (N // 2)
            load_w1(l, 0, 0)
            load_w1(l, 1, 1)
            if half == 1:
                vop(lambda e: e.tensor_copy(ybnd[:, :, 1:2],
                                            sF[:, :, N // 2:N // 2 + 1]),
                    reads=sf_bufs(N // 2, N // 2 + 1), writes=[YBND])
                vop(lambda e: e.tensor_copy(sF[:, :, N // 2:N // 2 + 1],
                                            ybnd[:, :, 0:1]),
                    reads=[YBND], writes=sf_bufs(N // 2, N // 2 + 1))
            for g in range(2):
                g0 = hb0 + g * 512
                ln_stats_sf(g0, 512, shifted=True)
                bcast_rows(512)
                for c in range(NCH):
                    jwarm(2)
                    off = -1 if c >= 4 else 0
                    ap = sF[:, c, g0 + off: g0 + off + 512]
                    rb = sf_bufs(g0 + off, g0 + off + 512)
                    vop(lambda e, a=ap: e.tensor_tensor(scr[:, 0:512], a,
                                                        PC[:, 0:512], ALU.mult),
                        reads=rb + [PCB], writes=[SCRB])
                    vop(lambda e: e.tensor_tensor(scr[:, 512:1024], scr[:, 0:512],
                                                  PD[:, 0:512], ALU.add),
                        reads=[SCRB, PDB], writes=[SCRB])
                    sop(lambda e, c=c, g=g: e.activation(
                            ln4f[:, c, g * 512:(g + 1) * 512], scr[:, 512:1024],
                            AF.Identity, bias=pvcol(f"ln_b{l}", c),
                            scale=pvcol(f"ln_g{l}", c)),
                        reads=[SCRB, CONSTF], writes=[LN4C[g * NCH + c]])
            if half == 1:
                vop(lambda e: e.tensor_copy(sF[:, :, N // 2:N // 2 + 1],
                                            ybnd[:, :, 1:2]),
                    reads=[YBND], writes=sf_bufs(N // 2, N // 2 + 1))

            for osn in range(NOS):
                slot = osn % 2
                o0, ow = OSR[osn]
                if osn == 0:
                    jwarm(4)
                for tg in range(2):
                    for kc in range(NCH):
                        mm(PCB, PC[0:ow, 0:512], w1v[slot][:, kc, 0:ow],
                           ln4f[:, kc, tg * 512:(tg + 1) * 512],
                           [W1B[slot], LN4C[tg * NCH + kc]],
                           start=(kc == 0), stop=(kc == NCH - 1))
                        mm(PDB, PD[0:ow, 0:512], w1v[slot][:, kc, 128:128 + ow],
                           ln4f[:, kc, tg * 512:(tg + 1) * 512],
                           [W1B[slot], LN4C[tg * NCH + kc]],
                           start=(kc == 0), stop=(kc == NCH - 1))
                    gslot = gel[0:ow, tg * 512:(tg + 1) * 512]
                    sop(lambda e, g=gslot, ow=ow, osn=osn: e.activation(
                            g, PD[0:ow, 0:512], AF.Gelu,
                            bias=pvcol(f"b1g_{l}", osn, rows=ow)),
                        reads=[PDB, CONSTF], writes=[GATEB])
                    vop(lambda e, g=gslot, ow=ow, osn=osn, tg=tg:
                        e.scalar_tensor_tensor(
                            hg_regions[osn][0:ow, tg * 512:(tg + 1) * 512],
                            PC[0:ow, 0:512], pvcol(f"b1a_{l}", osn, rows=ow),
                            g, ALU.add, ALU.mult),
                        reads=[PCB, GATEB, CONSTF], writes=[HGB])
                if osn + 2 < NOS:
                    load_w1(l, osn + 2, slot)

            groups = [list(range(g, min(g + 4, NOS))) for g in range(0, NOS, 4)]
            for k, osn in enumerate(groups[0]):
                load_w2(l, osn, k)
            for gi, grp in enumerate(groups):
                jwarm(2)
                for tg in range(2):
                    g0 = hb0 + tg * 512
                    gsf = sf_bufs(g0, g0 + 512)
                    for dc in range(NCH):
                        pb, pbuf = (PC, PCB) if dc % 2 == 0 else (PD, PDB)
                        for k, osn in enumerate(grp):
                            o0, ow = OSR[osn]
                            sl = (gi % 2) * 4 + k
                            mm(pbuf, pb[:, 0:512],
                               w2v[sl][0:ow, dc * 128:(dc + 1) * 128],
                               hg_regions[osn][0:ow, tg * 512:(tg + 1) * 512],
                               [W2B[sl], HGB], start=(k == 0),
                               stop=(k == len(grp) - 1))
                        if gi == 0:
                            vop(lambda e, d=dc, p=pb, g0=g0: e.scalar_tensor_tensor(
                                    sF[:, d, g0:g0 + 512], p[:, 0:512],
                                    pvcol(f"b2_{l}", d), sF[:, d, g0:g0 + 512],
                                    ALU.add, ALU.add),
                                reads=[pbuf, CONSTF] + gsf, writes=gsf)
                        else:
                            vop(lambda e, d=dc, p=pb, g0=g0: e.tensor_tensor(
                                    sF[:, d, g0:g0 + 512], p[:, 0:512],
                                    sF[:, d, g0:g0 + 512], ALU.add),
                                reads=[pbuf] + gsf, writes=gsf)
                        if last and gi == len(groups) - 1:
                            xsl = dc % 2
                            dmaG(f"X2R{xsl}", fv("TAB", 512, xsl * 512),
                                 x2s[dc * 128:(dc + 1) * 128, g0 - 1:g0 - 1 + 512],
                                 reads=[B("X2SB")], writes=[X2R[xsl]])
                            vop(lambda e, d=dc, g0=g0, xsl=xsl: e.tensor_tensor(
                                    sF[:, d, g0:g0 + 512], sF[:, d, g0:g0 + 512],
                                    fv("TAB", 512, xsl * 512), ALU.add),
                                reads=[X2R[xsl]] + gsf, writes=gsf)
                            dmaG("OUT", outT[dc * 128:(dc + 1) * 128, g0 - 1:g0 - 1 + 512],
                                 sF[:, dc, g0:g0 + 512], reads=gsf)
                    if tg == 0 and gi + 1 < len(groups):
                        for k, osn in enumerate(groups[gi + 1]):
                            load_w2(l, osn, ((gi + 1) % 2) * 4 + k)

    for l in range(2):
        stage_c_attn(l)
        kb.barrier()
        if l == 0:
            tap_sf("ya0")
            tap("qfc0", qfc, [QFCB])
            tap("kfc0", kfc, [KFCB[0], KFCB[1]])
            tap("vtc0", vtc, [VTCB[0], VTCB[1]])
            tap("ofc0", ofc, [OFCB])
        stage_c_mlp(l, last=(l == 1))
        kb.barrier()
        if l == 0:
            tap_sf("ym0")

    kb.finalize_blocks()
    kb.tap_t = tap_t
    return kb


_BUILD_CACHE = {}

def _get_kb():
    if "kb" not in _BUILD_CACHE:
        _BUILD_CACHE["kb"] = build_program()
    return _BUILD_CACHE["kb"]


def prep_common(inputs):
    consts = _host_constants()
    pv = _pack_pvec(inputs)
    qs = np.asarray(inputs["L0_ffn_q_scale"], np.float32)
    ks = np.asarray(inputs["L0_ffn_k_scale"], np.float32)
    qsc = np.zeros((2, 4, 128, HD), np.float32)
    for l in range(2):
        qsc[l, 0] = np.broadcast_to(qs[l], (128, HD))
        qsc[l, 1] = np.broadcast_to(np.concatenate([qs[l, 32:], qs[l, :32]]), (128, HD))
        qsc[l, 2] = np.broadcast_to(ks[l], (128, HD))
        qsc[l, 3] = np.broadcast_to(np.concatenate([ks[l, 32:], ks[l, :32]]), (128, HD))

    return dict(
        wqkvT=np.ascontiguousarray(np.asarray(inputs["L0_attn_w_qkv"], np.float32).T),
        wg1T=np.ascontiguousarray(np.asarray(inputs["L0_attn_w_g1"], np.float32).T),
        wg2T=np.ascontiguousarray(np.asarray(inputs["L0_attn_w_g2"], np.float32).T),
        woutT=np.ascontiguousarray(np.asarray(inputs["L0_attn_w_out"], np.float32).T),
        fqkv=np.ascontiguousarray(np.asarray(inputs["L0_ffn_wqkv"], np.float32)),
        fwo=np.ascontiguousarray(np.asarray(inputs["L0_ffn_wo"], np.float32)),
        fw1=np.ascontiguousarray(np.asarray(inputs["L0_ffn_w1"], np.float32)),
        fw2=np.ascontiguousarray(np.asarray(inputs["L0_ffn_w2"], np.float32)),
        qscale=qsc, pvec=pv,
        tabs=consts["tabs"], tabslo=consts["tabslo"], ml=consts["ml"], mh=consts["mh"],
        maskb=consts["maskb"], eye_f=consts["eye_f"], eye_b=consts["eye_b"],
        ones_f=consts["ones_f"], ones_b=consts["ones_b"],
    )


def kernel(**inputs):
    common = prep_common(inputs)
    x = np.asarray(inputs["L0_attn_x"], np.float32)
    kb = _get_kb()
    in_maps = []
    for b in range(8):
        m = dict(common)
        m["xT"] = np.ascontiguousarray(x[b].T)
        in_maps.append(m)
    res = run_bass_kernel_spmd(kb.nc, in_maps, core_ids=list(range(8)))
    out = np.stack([np.ascontiguousarray(r["outT"].T) for r in res.results])
    return out.astype(np.float32)


# version 14
# speedup vs baseline: 1.0870x; 1.0713x over previous
import numpy as np
import ml_dtypes
import concourse.bass as bass
import concourse.mybir as mybir
from concourse.bass_utils import run_bass_kernel_spmd

F32 = mybir.dt.float32
BF16 = mybir.dt.bfloat16
AF = mybir.ActivationFunctionType
ALU = mybir.AluOpType
AX = mybir.AxisListType

D = 1024
NCH = 8
N = 2048
SUP = 256
NSUP = N // SUP
HB = 16
HD = 64
HC = 8
WIN = 512
NW = N // WIN
TW = WIN // 128
HID = 2730
FFI = 1536
NEG = -1.0e30
OS_SIZES = [128] * (HID // 128) + ([HID % 128] if HID % 128 else [])
NOS = len(OS_SIZES)

def _os_ranges():
    o0, out = 0, []
    for w in OS_SIZES:
        out.append((o0, w))
        o0 += w
    return out

OSR = _os_ranges()

PVIDX = {}
_pc = 0
def _pvadd(name, n):
    global _pc
    PVIDX[name] = _pc
    _pc += n
_pvadd("eps", 1)
_pvadd("b_g1", 1)
_pvadd("b_g2", 8)
_pvadd("gamma", 8)
_pvadd("beta", 8)
for _l in range(2):
    _pvadd(f"ln_g{_l}", 8)
    _pvadd(f"ln_b{_l}", 8)
    _pvadd(f"b2_{_l}", 8)
for _l in range(2):
    _pvadd(f"b1a_{_l}", NOS)
    _pvadd(f"b1g_{_l}", NOS)
PV = _pc


def _bf(x):
    return np.ascontiguousarray(np.asarray(x)).astype(ml_dtypes.bfloat16)


def _host_constants():
    half_idx = np.arange(0, HD, 2, dtype=np.float64)
    inv_freq = 1.0 / 10000.0 ** (half_idx / HD)
    t = np.arange(N, dtype=np.float64)
    freqs = t[:, None] * inv_freq[None, :]
    freqs = np.concatenate([freqs, freqs], axis=-1)
    base = (half_idx + 0.4 * HD) / (1.4 * HD)
    power = (t - N // 2) / float(WIN // 2)
    scale = base[None, :] ** power[:, None]
    scale = np.concatenate([scale, scale], axis=-1)
    cs, sn = np.cos(freqs), np.sin(freqs)
    t64 = [cs * scale * 8.0, sn * scale * 8.0, cs / scale, sn / scale]

    def wtile(a):
        return np.ascontiguousarray(
            a.reshape(NW, TW, 128, HD).transpose(0, 2, 1, 3).reshape(NW, 128, TW * HD))

    hi = [a.astype(np.float32) for a in t64]
    lo = [(a - h.astype(np.float64)).astype(np.float32) for a, h in zip(t64, hi)]
    tabs = np.stack([wtile(h) for h in hi])
    tabslo = np.stack([wtile(x) for x in lo])
    ii = np.arange(128)
    ml = np.where(ii[None, :] < ii[:, None], NEG, 0.0).astype(np.float32)
    mh = np.where(ii[None, :] > ii[:, None], NEG, 0.0).astype(np.float32)
    triu_keep = (ii[:, None] <= ii[None, :]).astype(np.float32)
    maskb = np.concatenate([triu_keep, np.ones((128, 128), np.float32), triu_keep], 1)
    eye = np.eye(128, dtype=np.float32)
    ones = np.ones((128, 128), np.float32)
    return dict(tabs=tabs, tabslo=tabslo, ml=_bf(ml), mh=_bf(mh), maskb=_bf(maskb),
                eye_f=eye, eye_b=_bf(eye), ones_f=ones, ones_b=_bf(ones))


def _pack_pvec(ins):
    cols = np.zeros((128, PV), np.float32)
    def put(name, k, v):
        cols[: v.shape[0], PVIDX[name] + k] = v
    put("eps", 0, np.full((128,), 1e-5, np.float32))
    put("b_g1", 0, np.asarray(ins["L0_attn_b_g1"], np.float32))
    for c in range(NCH):
        put("b_g2", c, np.asarray(ins["L0_attn_b_g2"], np.float32)[c * 128:(c + 1) * 128])
        put("gamma", c, np.asarray(ins["L0_attn_gamma"], np.float32)[c * 128:(c + 1) * 128])
        put("beta", c, np.asarray(ins["L0_attn_beta"], np.float32)[c * 128:(c + 1) * 128])
    for l in range(2):
        for c in range(NCH):
            put(f"ln_g{l}", c, np.asarray(ins["L0_ffn_ln_g"], np.float32)[l, c * 128:(c + 1) * 128])
            put(f"ln_b{l}", c, np.asarray(ins["L0_ffn_ln_b"], np.float32)[l, c * 128:(c + 1) * 128])
            put(f"b2_{l}", c, np.asarray(ins["L0_ffn_b2"], np.float32)[l, c * 128:(c + 1) * 128])
        b1 = np.asarray(ins["L0_ffn_b1"], np.float32)[l]
        for k, (o0, ow) in enumerate(OSR):
            put(f"b1a_{l}", k, b1[o0:o0 + ow])
            put(f"b1g_{l}", k, b1[HID + o0:HID + o0 + ow])
    return cols



class Buf:
    __slots__ = ("w", "rs", "name")

    def __init__(self, name=""):
        self.w = {}
        self.rs = {}
        self.name = name

    def toks_w(self):
        return list(self.w.items())

    def toks_rw(self):
        out = dict(self.w)
        for s, v in self.rs.items():
            out[s] = max(out.get(s, 0), v)
        return list(out.items())

    def add_read(self, tok):
        s, v = tok
        self.rs[s] = max(self.rs.get(s, 0), v)

    def set_write(self, tok):
        s, v = tok
        self.w = {s: v}
        self.rs = {}


class KB:
    ENGS = ("T", "V", "S", "Q", "G")

    def __init__(self):
        self.nc = bass.Bass("TRN2", target_bir_lowering=False)
        self.prog = {e: [] for e in self.ENGS}
        self.cnt = {"T": 0, "V": 0, "S": 0, "Q": 0, "G": 0}
        self.wm = {e: {} for e in self.ENGS}
        self.n_wait = 0
        self.n_inst = 0

    def build_resources(self):
        import contextlib
        self.stack = contextlib.ExitStack()
        ent = self.stack.enter_context
        self.sem = {k: ent(self.nc.semaphore(f"sem_{k}"))
                    for k in ("T", "V", "S", "Q", "G")}
        return ent

    def dma_sem(self, name):
        if name not in self.sem:
            self.sem[name] = self.stack.enter_context(
                self.nc.semaphore(f"sem_{name}"))
            self.cnt[name] = 0
        return self.sem[name]

    def op(self, eng, fn, reads=(), writes=(), deps=(), dma=False):
        need = {}
        for b in reads:
            for s, v in b.toks_w():
                need[s] = max(need.get(s, 0), v)
        for b in writes:
            for s, v in b.toks_rw():
                need[s] = max(need.get(s, 0), v)
        for s, v in deps:
            need[s] = max(need.get(s, 0), v)
        wmm = self.wm[eng]
        waits = []
        for s, v in sorted(need.items()):
            if wmm.get(s, 0) < v:
                waits.append((s, v))
                wmm[s] = v
        if dma:
            kind = dma
            self.dma_sem(kind)
            inc = 16
        else:
            kind, inc = eng, 1
        self.cnt[kind] += inc
        tok = (kind, self.cnt[kind])
        self.n_wait += len(waits)
        self.n_inst += 1
        self.prog[eng].append((waits, fn, kind, inc))
        for b in reads:
            b.add_read(tok)
        for b in writes:
            b.set_write(tok)
        return tok

    def snapshot(self):
        return [(k, v) for k, v in self.cnt.items() if v > 0]

    def barrier(self):
        toks = self.snapshot()
        for e in self.ENGS:
            self.op(e, lambda en: en.nop(), deps=toks)

    def finalize_blocks(self):
        sem, prog, cnt = self.sem, self.prog, self.cnt
        engmap = {"T": "tensor", "V": "vector", "S": "scalar", "Q": "sync",
                  "G": "gpsimd"}
        with self.nc.Block() as block:
            for e in self.ENGS:
                def mk(e):
                    def body(eng):
                        for waits, fn, kind, inc in prog[e]:
                            for s, v in waits:
                                eng.wait_ge(sem[s], v)
                            fn(eng).then_inc(sem[kind], inc)
                        if e == "Q":
                            for k, v in cnt.items():
                                if v and k != "Q":
                                    eng.wait_ge(sem[k], v)
                    return body
                getattr(block, engmap[e])(mk(e))



def build_program(debug=False):
    kb = KB()
    nc = kb.nc
    ent = kb.build_resources()
    tap_t = {}

    def din(name, shape, dt=F32):
        return nc.dram_tensor(name, list(shape), dt, kind="ExternalInput").ap()

    xT = din("xT", [D, N])
    wqkvT = din("wqkvT", [D, 3 * D])
    wg1T = din("wg1T", [D, HD])
    wg2T = din("wg2T", [HD, D])
    woutT = din("woutT", [D, D])
    fqkv = din("fqkv", [2, D, FFI])
    fwo = din("fwo", [2, 512, D])
    fw1 = din("fw1", [2, D, 2 * HID])
    fw2 = din("fw2", [2, HID, D])
    qscaleD = din("qscale", [2, 4, 128, HD])
    pvecD = din("pvec", [128, PV])
    tabsD = din("tabs", [4, NW, 128, TW * HD])
    tabsloD = din("tabslo", [4, NW, 128, TW * HD])
    mlD = din("ml", [128, 128], BF16)
    mhD = din("mh", [128, 128], BF16)
    maskbD = din("maskb", [128, 384], BF16)
    eyefD = din("eye_f", [128, 128])
    eyebD = din("eye_b", [128, 128], BF16)
    onesfD = din("ones_f", [128, 128])
    onesbD = din("ones_b", [128, 128], BF16)

    outT = nc.dram_tensor("outT", [D, N], F32, kind="ExternalOutput").ap()
    x2s = nc.dram_tensor("x2s", [D, N], F32).ap()

    sF = ent(nc.sbuf_tensor("sF", [128, NCH, N + 1], F32))

    fplan, bplan = {}, {}
    _fc = [0]
    def fslot(name, cols):
        fplan[name] = (_fc[0], cols)
        _fc[0] += cols
    fslot("STAT", 1024)
    fslot("SMALL", 128)
    fslot("GATE", 2048)
    fslot("SCR", 1536)
    fslot("WST", 2048)
    fslot("TAB", 1024)
    fslot("CONST", 448)
    fslot("YBND", 32)
    fslot("S32", 512)
    AFt = ent(nc.sbuf_tensor("AFa", [128, _fc[0]], F32))

    _bc = [0]
    def bslot(name, cols):
        bplan[name] = (_bc[0], cols)
        _bc[0] += cols
    bslot("YBF", 2064)
    bslot("QF", 2048)
    bslot("KF", 4096)
    bslot("VT", 4096)
    bslot("KST", 2048)
    bslot("ATM", 1536)
    bslot("OF", 2048)
    bslot("SBF", 512)
    bslot("WA", 24576)
    bslot("WB", 8192)
    bslot("WG", 1536)
    bslot("CONB", 896)
    ABt = ent(nc.sbuf_tensor("ABa", [128, _bc[0]], BF16))

    total_bytes = (NCH * (N + 1)) * 4 + _fc[0] * 4 + _bc[0] * 2
    assert total_bytes <= 209500, total_bytes

    PA = ent(nc.psum_tensor("PA", [128, 1024], F32))
    PC = ent(nc.psum_tensor("PC", [128, 512], F32))
    PD = ent(nc.psum_tensor("PD", [128, 512], F32))
    PE = ent(nc.psum_tensor("PE", [128, 512], F32))
    PB = ent(nc.psum_tensor("PB", [128, 512], F32))
    PS = ent(nc.psum_tensor("PS", [128, 512], F32))
    PT = ent(nc.psum_tensor("PT", [128, 1024], BF16))

    def fv(name, cols=None, off=0):
        o, c = fplan[name]
        if cols is None:
            cols = c - off
        assert off >= 0 and off + cols <= c, (name, off, cols)
        return AFt[:, o + off:o + off + cols]

    def bv(name, cols=None, off=0):
        o, c = bplan[name]
        if cols is None:
            cols = c - off
        assert off >= 0 and off + cols <= c, (name, off, cols)
        return ABt[:, o + off:o + off + cols]

    bufs = {}
    def B(name):
        if name not in bufs:
            bufs[name] = Buf(name)
        return bufs[name]

    SFB = [B(f"sF{s}") for s in range(NSUP)]

    def sf_bufs(c0, c1):
        lo = max(0, (c0 - 1)) // SUP
        hi = min(NSUP - 1, max(0, (c1 - 2)) // SUP)
        return [SFB[i] for i in range(lo, hi + 1)]

    PAB0, PAB1 = B("PA0"), B("PA1")
    PCB, PDB, PEB, PBB, PSB, PTB = (B(x) for x in ("PCp", "PDp", "PEp", "PBp", "PSp", "PTp"))

    op = kb.op

    def mm(pbuf, out_ap, lhsT_ap, rhs_ap, rbufs, start, stop, deps=()):
        return op("T", lambda e, o=out_ap, l=lhsT_ap, r=rhs_ap, st=start, sp=stop:
                  e.matmul(o, l, r, start=st, stop=sp),
                  reads=list(rbufs), writes=[pbuf], deps=deps)

    def trans(pbuf, out_ap, in_ap, ident_ap, rbufs, start=True, stop=True):
        return op("T", lambda e, o=out_ap, i=in_ap, d=ident_ap, st=start, sp=stop:
                  e.matmul(o, i, d, start=st, stop=sp, is_transpose=True),
                  reads=list(rbufs), writes=[pbuf])

    def vop(fn, reads=(), writes=(), deps=()):
        return op("V", fn, reads=reads, writes=writes, deps=deps)

    def sop(fn, reads=(), writes=(), deps=()):
        return op("S", fn, reads=reads, writes=writes, deps=deps)

    def dmaQ(sem, out_ap, in_ap, reads=(), writes=()):
        return op("Q", lambda e, o=out_ap, i=in_ap: e.dma_start(out=o, in_=i),
                  reads=reads, writes=writes, dma=sem)

    def dmaG(sem, out_ap, in_ap, reads=(), writes=()):
        return op("G", lambda e, o=out_ap, i=in_ap: e.dma_start(out=o, in_=i),
                  reads=reads, writes=writes, dma=sem)

    def jwarm(n):
        for _ in range(n):
            op("T", lambda e: e.matmul(PA[:, 512:1024], eye_b, warm_rhs,
                                       start=True, stop=True))

    CONSTF, CONSTB, INB = B("CONSTF"), B("CONSTB"), B("INCHAIN")
    eye_f = fv("CONST", 128, 0)
    ones_f = fv("CONST", 128, 128)
    pvecS = fv("CONST", PV, 256)
    eye_b = bv("CONB", 128, 0)
    ones_b = bv("CONB", 128, 128)
    ml_b = bv("CONB", 128, 256)
    mh_b = bv("CONB", 128, 384)
    maskb_b = bv("CONB", 384, 512)
    dmaG("IN", eye_f, eyefD, writes=[CONSTF, INB])
    dmaG("IN", ones_f, onesfD, writes=[CONSTF, INB])
    dmaG("IN", pvecS, pvecD, writes=[CONSTF, INB])
    dmaG("IN", eye_b, eyebD, writes=[CONSTB, INB])
    dmaG("IN", ones_b, onesbD, writes=[CONSTB, INB])
    dmaG("IN", ml_b, mlD, writes=[CONSTB, INB])
    dmaG("IN", mh_b, mhD, writes=[CONSTB, INB])
    dmaG("IN", maskb_b, maskbD, writes=[CONSTB, INB])

    def pvcol(name, k=0, rows=128):
        return pvecS[0:rows, PVIDX[name] + k: PVIDX[name] + k + 1]

    def tap(name, ap, rbufs, rows=128):
        if not debug:
            return
        shp = [rows] + [int(d) for d in ap.shape[1:]]
        t = nc.dram_tensor("tap_" + name, shp, ap.dtype, kind="ExternalOutput").ap()
        tap_t[name] = shp
        dmaG("TAPS", t, ap, reads=list(rbufs))

    def tap_sf(name):
        if not debug:
            return
        t = nc.dram_tensor("tap_" + name, [D, N], F32, kind="ExternalOutput").ap()
        tap_t[name] = (D, N)
        for c in range(NCH):
            dmaG("TAPS", t[c * 128:(c + 1) * 128, :], sF[:, c, 1:N + 1],
                 reads=list(SFB))

    warm_rhs = bv("CONB", 512, 256)
    for i in range(24):
        op("T", lambda e: e.matmul(PB[:, 0:512], eye_b, warm_rhs,
                                   start=True, stop=True),
           reads=[CONSTB], writes=[PBB])

    for c in range(NCH):
        dmafn = dmaG if c % 2 else dmaQ
        dmafn("IN", sF[:, c, 1:N + 1], xT[c * 128:(c + 1) * 128, :],
              writes=list(SFB) + [INB])
    vop(lambda e: e.memset(sF[:, :, 0:1], 0.0), writes=[SFB[0]])

    WST = [B("WST0"), B("WST1")]
    _wi = [0]
    _rr = [0]
    RR_ENGS = ("G", "V", "S")

    def cast_w(dst_ap, dst_buf, src_ap, rows, cols, eng="S", extra_w=()):
        done = 0
        while done < cols:
            w = min(1024, cols - done)
            i = _wi[0] % 2
            _wi[0] += 1
            stg = fv("WST", w, 1024 * i)[0:rows, :]
            dmaQ(f"WST{i}", stg, src_ap[:, done:done + w], writes=[WST[i]])
            dpc = dst_ap[0:rows, done:done + w]
            ee = eng
            if ee == "RR":
                ee = RR_ENGS[_rr[0] % 3]
                _rr[0] += 1
            wl = [dst_buf] + list(extra_w)
            if ee == "S":
                sop(lambda e, d=dpc, g=stg: e.copy(d, g),
                    reads=[WST[i]], writes=wl)
            elif ee == "G":
                op("G", lambda e, d=dpc, g=stg: e.tensor_copy(d, g),
                   reads=[WST[i]], writes=wl)
            else:
                vop(lambda e, d=dpc, g=stg: e.tensor_copy(d, g),
                    reads=[WST[i]], writes=wl)
            done += w

    fqkv_b = bv("WA", 12288).rearrange("p (c o) -> p c o", c=NCH)
    fwo_b = bv("WA", 4096, 12288).rearrange("p (c o) -> p c o", c=4)
    FQC = [B(f"FQC{c}") for c in range(NCH)]
    FOC = [B(f"FOC{c}") for c in range(4)]

    def emit_attn_cast(l, extra=()):
        for c in range(NCH):
            cast_w(fqkv_b[:, c, :], FQC[c], fqkv[l, c * 128:(c + 1) * 128, :],
                   128, FFI, eng="RR", extra_w=extra)
        for c in range(4):
            cast_w(fwo_b[:, c, :], FOC[c], fwo[l, c * 128:(c + 1) * 128, :],
                   128, D, eng="RR", extra_w=extra)

    rs_row = fv("STAT", 512, 0)
    c0_row = fv("STAT", 512, 512)
    nmld_row = fv("STAT", 1024, 0)
    STATB = B("STATB")
    SCRB = B("SCRB")
    GATEB = B("GATEB")
    SMALLB = B("SMALLB")
    sq_scr = fv("SCR", 1024, 512)

    def stat_finish(width):
        jwarm(6)
        mu = sq_scr[0:1, 0:width]
        sop(lambda e, w=width: e.activation(mu[0:1, 0:w], PE[0:1, 0:w], AF.Copy,
                                            scale=1.0 / D),
            reads=[PEB], writes=[SCRB])
        vop(lambda e, w=width: e.scalar_tensor_tensor(c0_row[0:1, 0:w], mu[0:1, 0:w],
                                                      1.0, mu[0:1, 0:w],
                                                      ALU.mult, ALU.mult),
            reads=[SCRB], writes=[STATB])
        vop(lambda e, w=width: e.scalar_tensor_tensor(rs_row[0:1, 0:w], PB[0:1, 0:w],
                                                      1.0 / D, c0_row[0:1, 0:w],
                                                      ALU.mult, ALU.subtract),
            reads=[PBB, STATB], writes=[STATB])
        sop(lambda e, w=width: e.activation(rs_row[0:1, 0:w], rs_row[0:1, 0:w],
                                            AF.Ln, bias=pvcol("eps", rows=1)),
            reads=[STATB, CONSTF], writes=[STATB])
        sop(lambda e, w=width: e.activation(rs_row[0:1, 0:w], rs_row[0:1, 0:w],
                                            AF.Exp, scale=-0.5),
            reads=[STATB], writes=[STATB])
        vop(lambda e, w=width: e.scalar_tensor_tensor(c0_row[0:1, 0:w], mu[0:1, 0:w],
                                                      -1.0, rs_row[0:1, 0:w],
                                                      ALU.mult, ALU.mult),
            reads=[SCRB, STATB], writes=[STATB])

    def ln_stats_sf(c0, width, shifted=False):
        for c in range(NCH):
            off = -1 if (shifted and c >= 4) else 0
            ap = sF[:, c, c0 + off: c0 + off + width]
            rb = sf_bufs(c0 + off, c0 + off + width)
            mm(PEB, PE[0:1, 0:width], ones_f[:, 0:1], ap, [CONSTF] + rb,
               start=(c == 0), stop=(c == NCH - 1))
        for c in range(NCH):
            off = -1 if (shifted and c >= 4) else 0
            ap = sF[:, c, c0 + off: c0 + off + width]
            rb = sf_bufs(c0 + off, c0 + off + width)
            sop(lambda e, a=ap, w=width: e.activation(sq_scr[:, 0:w], a, AF.Square),
                reads=rb, writes=[SCRB])
            if c < 2:
                jwarm(1)
            mm(PBB, PB[0:1, 0:width], ones_f[:, 0:1], sq_scr[:, 0:width],
               [CONSTF, SCRB], start=(c == 0), stop=(c == NCH - 1))
        stat_finish(width)

    def bcast_rows(width):
        mm(PCB, PC[:, 0:width], ones_f[0:1, 0:128], rs_row[0:1, 0:width],
           [CONSTF, STATB], start=True, stop=True)
        mm(PDB, PD[:, 0:width], ones_f[0:1, 0:128], c0_row[0:1, 0:width],
           [CONSTF, STATB], start=True, stop=True)

    WAB, WBB, WGB, SBFB = B("WAB"), B("WBB"), B("WGB"), B("SBFB")
    wqkv_b = bv("WA", 24576).rearrange("p (c o) -> p c o", c=NCH)
    wout_b = bv("WB", 8192).rearrange("p (c o) -> p c o", c=NCH)
    wg1_b = bv("WG", 512, 0).rearrange("p (c o) -> p c o", c=NCH)
    wg2_b = bv("WG", 1024, 512)

    for c in range(NCH):
        cast_w(wqkv_b[:, c, :], WAB, wqkvT[c * 128:(c + 1) * 128, :], 128, 3 * D,
               eng="RR")
        cast_w(wout_b[:, c, :], WBB, woutT[c * 128:(c + 1) * 128, :], 128, D,
               eng="RR")
        cast_w(wg1_b[:, c, :], WGB, wg1T[c * 128:(c + 1) * 128, :], 128, HD,
               eng="RR")
    cast_w(wg2_b, WGB, wg2T, 64, D, eng="RR")

    sbf = bv("SBF", 512)
    s32 = fv("S32", 512)
    S32B = B("S32B")
    vop(lambda e: e.memset(sbf[:], 0.0), writes=[SBFB])
    op("T", lambda e: e.matmul(PA[:, 512:1024], eye_b, warm_rhs,
                               start=True, stop=True), reads=[INB, CONSTB])
    jwarm(16)

    h1 = bv("YBF", 2048).rearrange("p (c t) -> p c t", c=NCH)
    H1C = [B(f"H1C{c}") for c in range(NCH)]
    qf = bv("QF", 2048).rearrange("p (c t) -> p c t", c=NCH)
    kf = bv("KF", 2048).rearrange("p (c t) -> p c t", c=NCH)
    vt = bv("VT", 2048).rearrange("p (t o) -> p t o", t=2)
    kst = bv("KST", 2048).rearrange("p (t o) -> p t o", t=2)
    QFB, KFB, VTB, KSTB = B("QFB"), B("KFB"), B("VTB"), B("KSTB")
    atm = bv("ATM", 768).rearrange("p (i c) -> p i c", i=2)
    ATMB = [B("ATM0"), B("ATM1")]
    g1f = bv("ATM", 256, 768)
    G1B = B("G1B")
    gate = fv("GATE", 2048).rearrange("p (c t) -> p c t", c=NCH)
    oln = bv("OF", 2048).rearrange("p (c t) -> p c t", c=NCH)
    OLNC = [B(f"OLNC{c}") for c in range(NCH)]
    scr = fv("SCR", 1024, 0)

    ps_open = [False]

    for s in range(NSUP):
        c0 = 1 + s * SUP
        sfb = [SFB[s]]

        ln_stats_sf(c0, SUP)
        bcast_rows(SUP)
        jwarm(3)
        for c in range(NCH):
            vop(lambda e, c=c, c0=c0: e.tensor_tensor(scr[:, 0:SUP],
                                                      sF[:, c, c0:c0 + SUP],
                                                      PC[:, 0:SUP], ALU.mult),
                reads=sfb + [PCB], writes=[SCRB])
            vop(lambda e, c=c: e.tensor_tensor(h1[:, c, :], scr[:, 0:SUP],
                                               PD[:, 0:SUP], ALU.add),
                reads=[SCRB, PDB], writes=[H1C[c]])

        if s == 0:
            tap("h1", h1, list(H1C))
        for oc in range(16):
            pb, pbuf = (PC, PCB) if oc % 2 == 0 else (PD, PDB)
            for kc in range(NCH):
                mm(pbuf, pb[:, 0:SUP], wqkv_b[:, kc, oc * 128:(oc + 1) * 128],
                   h1[:, kc, :], [WAB, H1C[kc]], start=(kc == 0), stop=(kc == NCH - 1))
            dst, dbuf = (qf, QFB) if oc < 8 else (kf, KFB)
            sop(lambda e, d=dst[:, oc % 8, :], p=pb: e.activation(d, p[:, 0:SUP],
                                                                  AF.Silu),
                reads=[pbuf], writes=[dbuf])
        for tt in range(2):
            pb, pbuf = (PC, PCB) if tt == 0 else (PD, PDB)
            for half in range(2):
                for kc in range(NCH):
                    mm(pbuf, pb[:, 0:512], h1[:, kc, tt * 128:(tt + 1) * 128],
                       wqkv_b[:, kc, 2048 + half * 512: 2048 + (half + 1) * 512],
                       [WAB, H1C[kc]], start=(kc == 0), stop=(kc == NCH - 1))
                sop(lambda e, t=tt, h=half, p=pb: e.activation(
                        vt[:, t, h * 512:(h + 1) * 512], p[:, 0:512], AF.Copy),
                    reads=[pbuf], writes=[VTB])
        for tt in range(2):
            for c in range(NCH):
                trans(PTB, PT[:, c * 128:(c + 1) * 128],
                      kf[:, c, tt * 128:(tt + 1) * 128], eye_b, [KFB, CONSTB],
                      start=(c == 0), stop=(c == NCH - 1))
            vop(lambda e, t=tt: e.tensor_copy(kst[:, t, :], PT[:]),
                reads=[PTB], writes=[KSTB])

        if s == 0:
            tap("qf", qf, [QFB])
            tap("kf", kf, [KFB])
            tap("vt", vt, [VTB])
            tap("kst", kst, [KSTB])
        for kc in range(NCH):
            mm(PEB, PE[0:64, 0:SUP], wg1_b[:, kc, :], h1[:, kc, :],
               [WGB, H1C[kc]], start=(kc == 0), stop=(kc == NCH - 1))
        sop(lambda e: e.activation(g1f[0:64, :], PE[0:64, 0:SUP], AF.Identity,
                                   bias=pvcol("b_g1", rows=64)),
            reads=[PEB, CONSTF], writes=[G1B])
        for dc in range(NCH):
            pb, pbuf = (PC, PCB) if dc % 2 == 0 else (PD, PDB)
            mm(pbuf, pb[:, 0:SUP], wg2_b[0:64, dc * 128:(dc + 1) * 128],
               g1f[0:64, :], [WGB, G1B], start=True, stop=True)
            sop(lambda e, d=dc, p=pb: e.activation(gate[:, d, :], p[:, 0:SUP],
                                                   AF.Sigmoid, bias=pvcol("b_g2", d)),
                reads=[pbuf, CONSTF], writes=[GATEB])

        if s == 0:
            tap("gate", gate, [GATEB])
        for pr in range(8):
            opb, opbuf = (PE, PEB) if pr % 2 == 0 else (PB, PBB)
            for hh in range(2):
                h = pr * 2 + hh
                jwarm(1)
                hc, hp = h // 2, (h % 2) * 64
                ab = ATMB[h % 2]
                mm(PAB0, PA[:, 0:256], kf[hp:hp + 64, hc, 0:128],
                   qf[hp:hp + 64, hc, 0:256], [KFB, QFB], start=True, stop=False)
                mm(PAB0, PA[:, 256:384], kf[hp:hp + 64, hc, 128:256],
                   qf[hp:hp + 64, hc, 128:256], [KFB, QFB], start=False, stop=True)
                vop(lambda e, i=h % 2: e.tensor_tensor(atm[:, i, :], PA[:, 0:384],
                                                       maskb_b, ALU.mult),
                    reads=[PAB0, CONSTB], writes=[ab])
                if s == 0 and h == 0:
                    tap("atm_h0", atm[:, 0, :], [ab])
                first_o = [True]
                def omm(lhs, rhs, rb, stop):
                    st = first_o[0]
                    first_o[0] = False
                    mm(opbuf, lhs[0], lhs[1], rhs, rb, start=st, stop=stop)
                if s > 0:
                    omm((opb[hp:hp + 64, 0:256],
                         sbf[hp:hp + 64, hc * 64:(hc + 1) * 64]),
                        qf[hp:hp + 64, hc, 0:256], [SBFB, QFB], False)
                omm((opb[hp:hp + 64, 0:256], vt[:, 0, h * 64:(h + 1) * 64]),
                    atm[:, h % 2, 0:256], [VTB, ab], False)
                omm((opb[hp:hp + 64, 128:256], vt[:, 1, h * 64:(h + 1) * 64]),
                    atm[:, h % 2, 256:384], [VTB, ab], stop=True)
                mm(PSB, PS[hp:hp + 64, hc * 64:(hc + 1) * 64],
                   kst[:, 0, h * 64:(h + 1) * 64], vt[:, 0, h * 64:(h + 1) * 64],
                   [KSTB, VTB], start=True, stop=False)
                mm(PSB, PS[hp:hp + 64, hc * 64:(hc + 1) * 64],
                   kst[:, 1, h * 64:(h + 1) * 64], vt[:, 1, h * 64:(h + 1) * 64],
                   [KSTB, VTB], start=False, stop=True)
            vop(lambda e, pr=pr, p=opb: e.tensor_tensor(gate[:, pr, :],
                                                        p[:, 0:256],
                                                        gate[:, pr, :], ALU.mult),
                reads=[opbuf, GATEB], writes=[GATEB])
        if s == 0:
            vop(lambda e: e.tensor_copy(s32[:], PS[:]), reads=[PSB], writes=[S32B])
        else:
            vop(lambda e: e.tensor_tensor(s32[:], PS[:], s32[:], ALU.add),
                reads=[PSB, S32B], writes=[S32B])
        if s < NSUP - 1:
            sop(lambda e: e.copy(sbf[:], s32[:]), reads=[S32B], writes=[SBFB])

        if s <= 1:
            tap(f"og{s}", gate, [GATEB])
            tap(f"sbf{s}", sbf, [SBFB])
        for c in range(NCH):
            mm(PEB, PE[0:1, 0:SUP], ones_f[:, 0:1], gate[:, c, :],
               [CONSTF, GATEB], start=(c == 0), stop=(c == NCH - 1))
        for c in range(NCH):
            sop(lambda e, c=c: e.activation(sq_scr[:, 0:SUP], gate[:, c, :],
                                            AF.Square),
                reads=[GATEB], writes=[SCRB])
            if c < 2:
                jwarm(1)
            mm(PBB, PB[0:1, 0:SUP], ones_f[:, 0:1], sq_scr[:, 0:SUP],
               [CONSTF, SCRB], start=(c == 0), stop=(c == NCH - 1))
        stat_finish(SUP)
        bcast_rows(SUP)
        jwarm(3)
        for c in range(NCH):
            vop(lambda e, c=c: e.tensor_tensor(scr[:, 0:SUP], gate[:, c, :],
                                               PC[:, 0:SUP], ALU.mult),
                reads=[GATEB, PCB], writes=[SCRB])
            vop(lambda e: e.tensor_tensor(scr[:, 256:256 + SUP], scr[:, 0:SUP],
                                          PD[:, 0:SUP], ALU.add),
                reads=[PDB, SCRB], writes=[SCRB])
            sop(lambda e, c=c: e.activation(oln[:, c, :], scr[:, 256:256 + SUP],
                                            AF.Identity, bias=pvcol("beta", c),
                                            scale=pvcol("gamma", c)),
                reads=[SCRB, CONSTF], writes=[OLNC[c]])

        if s == 0:
            tap("oln", oln, list(OLNC))
        jwarm(3)
        for dc in range(NCH):
            pb, pbuf = (PC, PCB) if dc % 2 == 0 else (PD, PDB)
            for kc in range(NCH):
                mm(pbuf, pb[:, 0:SUP], wout_b[:, kc, dc * 128:(dc + 1) * 128],
                   oln[:, kc, :], [WBB, OLNC[kc]], start=(kc == 0), stop=(kc == NCH - 1))
            vop(lambda e, d=dc, p=pb, c0=c0: e.tensor_tensor(
                    sF[:, d, c0:c0 + SUP], p[:, 0:SUP], sF[:, d, c0:c0 + SUP],
                    ALU.add),
                reads=[pbuf] + sfb, writes=sfb)
            dmaG("SPILL", x2s[dc * 128:(dc + 1) * 128, s * SUP:(s + 1) * SUP],
                 sF[:, dc, c0:c0 + SUP], reads=sfb, writes=[B("X2SB")])

    emit_attn_cast(0, extra=(WAB,))

    kb.barrier()
    tap_sf("x2")

    for g in range(4):
        c0 = 1 + g * 512
        gb = sf_bufs(c0, c0 + 512)
        ln_stats_sf(c0, 512)
        bcast_rows(512)
        for c in range(NCH):
            vop(lambda e, c=c, c0=c0: e.tensor_tensor(scr[:, 0:512],
                                                      sF[:, c, c0:c0 + 512],
                                                      PC[:, 0:512], ALU.mult),
                reads=gb + [PCB], writes=[SCRB])
            vop(lambda e, c=c, c0=c0: e.tensor_tensor(sF[:, c, c0:c0 + 512],
                                                      scr[:, 0:512],
                                                      PD[:, 0:512], ALU.add),
                reads=[SCRB, PDB], writes=gb)

    kb.barrier()
    tap_sf("y0")

    YBND = B("YBNDB")
    ybnd = fv("YBND", 32).rearrange("p (c k) -> p c k", c=NCH)

    w1v = [bv("WA", 2048, i * 2048).rearrange("p (c o) -> p c o", c=NCH)
           for i in range(2)]
    W1B = [B("W1B0"), B("W1B1")]
    w2v = [bv("WB", 1024, i * 1024) for i in range(8)]
    W2B = [B(f"W2B{i}") for i in range(8)]

    qfc = bv("QF", 2048).rearrange("p (b t) -> p b t", b=4)
    kfc = bv("KF", 4096).rearrange("p (w b t) -> p w b t", w=2, b=4)
    vtc = bv("VT", 4096).rearrange("p (w b t) -> p w b t", w=2, b=4)
    QFCB = B("QFCB")
    KFCB = [B("KFC0"), B("KFC1")]
    VTCB = [B("VTC0"), B("VTC1")]
    pa_junk = bv("KST", 2048)
    PAJB = B("PAJB")
    ptile8 = bv("ATM", 1536).rearrange("p (i c) -> p i c", i=12)
    PTL8 = [B(f"PTL{i}") for i in range(12)]
    _pabk = [B("PABK0"), B("PABK1")]
    PASL = [_pabk[i // 4] for i in range(8)]
    bcastB = fv("GATE", 2048, 0)
    BCB = None
    ofc = bv("OF", 2048).rearrange("p (b t) -> p b t", b=4)
    OFCB = B("OFCB")
    ybf = bv("YBF", 2064).rearrange("p (i c k) -> p i c k", i=2, c=NCH)
    YBFB = [B("YBF0"), B("YBF1")]
    tabw = fv("TAB", 1024).rearrange("p (i k) -> p i k", i=4)
    TABB = B("TABB")
    qscale_t = fv("SCR", 256, 1024).rearrange("p (i k) -> p i k", i=4)
    QSB = B("QSB")
    scrA = fv("GATE", 512, 0)
    scrB2 = fv("GATE", 512, 512)
    scrC = fv("GATE", 512, 1024)
    scrD = fv("SCR", 512, 0)
    def nm_v(par):
        return fv("SMALL", 8, par * 48 + 0)
    def dsP_v(par):
        return fv("SMALL", 8, par * 48 + 8)
    def dsC_v(par):
        return fv("SMALL", 8, par * 48 + 16)
    def lnd_v(par):
        return fv("SMALL", 8, par * 48 + 24)
    def nmld_v(par):
        return fv("SMALL", 8, par * 48 + 32)
    def tmp1_v(par):
        return fv("SMALL", 8, par * 48 + 40)
    rn_t = fv("SMALL", 8, 96)
    rn_k = fv("SMALL", 8, 104)
    RNB = [B("RNQ"), B("RNK")]
    WKB = B("WKB")
    QSC = B("QSCR")
    PAQB, PAKB = B("PAQB"), B("PAKB")
    PAJQ, PAJK = B("PAJQ"), B("PAJK")
    X2R = [B("X2R0"), B("X2R1")]

    hg_regions = []
    for i in range(20):
        hg_regions.append(bv("WA", 1024, 4096 + i * 1024))
    hg_regions.append(bv("QF", 1024, 0))
    hg_regions.append(bv("QF", 1024, 1024))
    HGB = B("HGB")
    ln4f = ABt[:, bplan["KF"][0]: bplan["KF"][0] + 8192].rearrange(
        "p (c t) -> p c t", c=NCH)
    LN4C = [B(f"LN4C{i}") for i in range(2 * NCH)]
    gel = fv("GATE", 1024, 0)

    def rotary_pair(l, w, tt, qp, qpb, kp, kpb, kfw, kb_cur):
        t1 = [fv("S32", 512, 0), fv("WST", 512, 0)]
        Ax = [fv("SCR", 512, 0), fv("WST", 512, 512)]
        Cx = [fv("SCR", 512, 512), fv("WST", 512, 1024)]
        Dx = [fv("S32", 512, 0), fv("WST", 512, 1536)]
        T1f = [QSC, WKB]
        DBf = [QSC, WKB]
        GBf = [SCRB, WKB]
        rn = [rn_t, rn_k]
        ps = [(qp, qpb), (kp, kpb)]
        pj = [pa_junk[:, 1024:1536], pa_junk[:, 1536:2048]]
        PJb = [PAJQ, PAJK]
        cs = [tabw[:, 0, tt * 64:(tt + 1) * 64], tabw[:, 2, tt * 64:(tt + 1) * 64]]
        sn = [tabw[:, 1, tt * 64:(tt + 1) * 64], tabw[:, 3, tt * 64:(tt + 1) * 64]]
        h3 = lambda a: a.rearrange("p (h d) -> p h d", h=HC)
        for i in (0, 1):
            sop(lambda e, i=i: e.activation(Ax[i], ps[i][0][:, 0:512], AF.Square),
                reads=[ps[i][1]], writes=[GBf[i]])
        for i in (0, 1):
            vop(lambda e, i=i: e.reduce_sum(rn[i][:, 0:8], h3(Ax[i]), AX.X),
                reads=[GBf[i]], writes=[RNB[i]])
        for i in (0, 1):
            sop(lambda e, i=i: e.activation(rn[i][:, 0:8], rn[i][:, 0:8], AF.Sqrt),
                reads=[RNB[i]], writes=[RNB[i]])
        for i in (0, 1):
            vop(lambda e, i=i: e.tensor_scalar_max(rn[i][:, 0:8], rn[i][:, 0:8],
                                                   1e-12),
                reads=[RNB[i]], writes=[RNB[i]])
            vop(lambda e, i=i: e.reciprocal(rn[i][:, 0:8], rn[i][:, 0:8]),
                reads=[RNB[i]], writes=[RNB[i]])
        for i in (0, 1):
            vop(lambda e, i=i: e.tensor_tensor(
                    h3(t1[i]), h3(ps[i][0][:, 0:512]),
                    rn[i][:, 0:8].unsqueeze(2).broadcast_to([128, HC, HD]),
                    ALU.mult),
                reads=[ps[i][1], RNB[i]], writes=[T1f[i]])
        for i in (0, 1):
            vop(lambda e, i=i: e.tensor_tensor(
                    h3(Ax[i]), h3(t1[i]),
                    cs[i].unsqueeze(1).broadcast_to([128, HC, HD]), ALU.mult),
                reads=[T1f[i], TABB], writes=[GBf[i]])
        for i in (0, 1):
            vop(lambda e, i=i: e.tensor_tensor(
                    h3(Cx[i])[:, :, 0:32], h3(t1[i])[:, :, 32:64],
                    sn[i][:, 0:32].unsqueeze(1).broadcast_to([128, HC, 32]),
                    ALU.mult),
                reads=[T1f[i], TABB], writes=[GBf[i]])
            vop(lambda e, i=i: e.tensor_tensor(
                    h3(Cx[i])[:, :, 32:64], h3(t1[i])[:, :, 0:32],
                    sn[i][:, 32:64].unsqueeze(1).broadcast_to([128, HC, 32]),
                    ALU.mult),
                reads=[T1f[i], TABB], writes=[GBf[i]])
        for i in (0, 1):
            vop(lambda e, i=i: e.tensor_sub(
                    h3(Dx[i])[:, :, 0:32], h3(Ax[i])[:, :, 0:32],
                    h3(Cx[i])[:, :, 0:32]),
                reads=[GBf[i]], writes=[DBf[i]])
            vop(lambda e, i=i: e.tensor_add(
                    h3(Dx[i])[:, :, 32:64], h3(Ax[i])[:, :, 32:64],
                    h3(Cx[i])[:, :, 32:64]),
                reads=[GBf[i]], writes=[DBf[i]])
        for i in (0, 1):
            sop(lambda e, i=i: e.copy(pj[i], Dx[i]),
                reads=[DBf[i]], writes=[PJb[i]])
        for i in (0, 1):
            for blk in range(4):
                trans(PTB, PT[:, i * 512 + blk * 128: i * 512 + (blk + 1) * 128],
                      pj[i][:, blk * 128:(blk + 1) * 128], eye_b,
                      [PJb[i], CONSTB],
                      start=(i == 0 and blk == 0), stop=(i == 1 and blk == 3))
        vop(lambda e, tt=tt: e.tensor_copy(
                qfc[:, :, tt * 128:(tt + 1) * 128],
                PT[:, 0:512].rearrange("p (b c) -> p b c", b=4)),
            reads=[PTB], writes=[QFCB])
        vop(lambda e, tt=tt, kfw=kfw: e.tensor_copy(
                kfw[:, :, tt * 128:(tt + 1) * 128],
                PT[:, 512:1024].rearrange("p (b c) -> p b c", b=4)),
            reads=[PTB], writes=[kb_cur])

    def stage_c_attn(l):
        if l > 0:
            emit_attn_cast(l)
        op("V", lambda e: e.nop(), writes=[WST[0], WST[1], WKB])
        for i in range(4):
            dmaG("QS", qscale_t[:, i, :], qscaleD[l, i], writes=[QSB])
        for w in range(1, NW):
            vop(lambda e, w=w: e.tensor_copy(ybnd[:, :, w:w + 1],
                                             sF[:, :, w * WIN:w * WIN + 1]),
                reads=sf_bufs(w * WIN, w * WIN + 1), writes=[YBND])
        jwarm(12)

        for w in range(NW):
            wsl = w % 2
            kfw, kfp = kfc[:, wsl], kfc[:, 1 - wsl]
            vtw, vtp = vtc[:, wsl], vtc[:, 1 - wsl]
            kb_cur, kb_prev = KFCB[wsl], KFCB[1 - wsl]
            vb_cur, vb_prev = VTCB[wsl], VTCB[1 - wsl]
            wsf = sf_bufs(1 + w * WIN, 1 + (w + 1) * WIN)

            dmaQ("TAB", tabw[:, :, :], tabsD[:, w].transpose([1, 0, 2]), writes=[TABB])
            dmaQ("TABL", fv("SCR", 1024, 0).rearrange("p (i k) -> p i k", i=4),
                 tabsloD[:, w].transpose([1, 0, 2]), writes=[SCRB])
            vop(lambda e: e.tensor_tensor(tabw[:, :, :], tabw[:, :, :],
                                          fv("SCR", 1024, 0).rearrange(
                                              "p (i k) -> p i k", i=4), ALU.add),
                reads=[SCRB, TABB], writes=[TABB])
            vop(lambda e: e.tensor_tensor(
                    tabw.rearrange("p i (a b) -> p i a b", a=TW),
                    tabw.rearrange("p i (a b) -> p i a b", a=TW),
                    qscale_t[:, :, :].unsqueeze(2).broadcast_to([128, 4, TW, HD]),
                    ALU.mult),
                reads=[QSB, TABB], writes=[TABB])

            def emit_ybf(tt):
                t0g = 1 + w * WIN + tt * 128
                ti = tt % 2
                if w > 0 and tt == 0:
                    op("G", lambda e, ti=ti, t0g=t0g: e.tensor_copy(
                           ybf[:, ti, :, 1:129], sF[:, :, t0g:t0g + 128]),
                       reads=wsf, writes=[YBFB[ti]])
                    op("G", lambda e, ti=ti, w=w: e.tensor_copy(
                           ybf[:, ti, :, 0:1], ybnd[:, :, w:w + 1]),
                       reads=[YBND], writes=[YBFB[ti]])
                else:
                    op("G", lambda e, ti=ti, t0g=t0g: e.tensor_copy(
                           ybf[:, ti, :, 0:129], sF[:, :, t0g - 1:t0g + 128]),
                       reads=sf_bufs(t0g - 1, t0g + 128), writes=[YBFB[ti]])

            slot_ctr = [0]

            def slot_ap(sl):
                if sl < 4:
                    return PC[:, sl * 128:(sl + 1) * 128], PCB
                return PD[:, (sl - 4) * 128:(sl - 3) * 128], PDB

            def kts_of(tt):
                return ([("p", j) for j in range(tt, TW)] if w > 0 else []) \
                    + [("c", j) for j in range(0, tt + 1)]

            def pass1(tt):
                par = tt % 2
                nm, dsP, dsC = nm_v(par), dsP_v(par), dsC_v(par)
                tmp1 = tmp1_v(par)
                wp = (TW - tt) * 128
                wc = (tt + 1) * 128
                for h in range(HC):
                    hp, hpr = (h % 2) * 64, h // 2
                    ppv = PB[:, 0:wp]
                    pcu = PE[:, 0:wc]
                    if w > 0:
                        op("T", lambda e, o=ppv, l1=qfc[hp:hp + 64, hpr, tt * 128:(tt + 1) * 128],
                           r=kfp[hp:hp + 64, hpr, tt * 128:512]:
                           e.matmul(o, l1, r, start=True, stop=True),
                           reads=[QFCB, kb_prev], writes=[PBB])
                        vop(lambda e, o=ppv: e.tensor_tensor(o[:, 0:128], o[:, 0:128],
                                                             ml_b, ALU.add),
                            reads=[CONSTB], writes=[PBB])
                    op("T", lambda e, o=pcu, l1=qfc[hp:hp + 64, hpr, tt * 128:(tt + 1) * 128],
                       r=kfw[hp:hp + 64, hpr, 0:wc]:
                       e.matmul(o, l1, r, start=True, stop=True),
                       reads=[QFCB, kb_cur], writes=[PEB])
                    vop(lambda e, o=pcu, wc=wc: e.tensor_tensor(
                            o[:, wc - 128:wc], o[:, wc - 128:wc], mh_b, ALU.add),
                        reads=[CONSTB], writes=[PEB])
                    if w > 0:
                        vop(lambda e, o=ppv: e.reduce_max(tmp1[:, 0:1], o, AX.X,
                                                          negate=True),
                            reads=[PBB], writes=[SMALLB])
                        vop(lambda e, o=pcu: e.reduce_max(tmp1[:, 1:2], o, AX.X,
                                                          negate=True),
                            reads=[PEB], writes=[SMALLB])
                        vop(lambda e, h=h: e.tensor_tensor(nm[:, h:h + 1],
                                                           tmp1[:, 0:1],
                                                           tmp1[:, 1:2], ALU.min),
                            reads=[SMALLB], writes=[SMALLB])
                        sop(lambda e, o=ppv, wp=wp, h=h: e.activation(
                                pa_junk[:, 0:wp], o, AF.Exp,
                                bias=nm[:, h:h + 1], accum_out=dsP[:, h:h + 1]),
                            reads=[PBB, SMALLB], writes=[PAJB, SMALLB])
                    else:
                        vop(lambda e, o=pcu, h=h: e.reduce_max(
                                nm[:, h:h + 1], o, AX.X, negate=True),
                            reads=[PEB], writes=[SMALLB])
                    sop(lambda e, o=pcu, wc=wc, h=h: e.activation(
                            pa_junk[:, 512:512 + wc], o, AF.Exp,
                            bias=nm[:, h:h + 1], accum_out=dsC[:, h:h + 1]),
                        reads=[PEB, SMALLB], writes=[PAJB, SMALLB])

            def nmphase(tt):
                par = tt % 2
                nm, dsP, dsC, lnd, nmld = (nm_v(par), dsP_v(par), dsC_v(par),
                                           lnd_v(par), nmld_v(par))
                bcB = bcastB[:, par * 1024:(par + 1) * 1024]
                if w > 0:
                    vop(lambda e: e.tensor_add(dsC[:, 0:8], dsC[:, 0:8], dsP[:, 0:8]),
                        reads=[SMALLB], writes=[SMALLB])
                sop(lambda e: e.activation(lnd[:, 0:8], dsC[:, 0:8], AF.Ln),
                    reads=[SMALLB], writes=[SMALLB])
                vop(lambda e: e.tensor_sub(nmld[:, 0:8], nm[:, 0:8], lnd[:, 0:8]),
                    reads=[SMALLB], writes=[SMALLB])
                for grp in range(2):
                    for hh in range(4):
                        h = grp * 4 + hh
                        trans(PBB, PB[0:1, hh * 128:(hh + 1) * 128],
                              nmld[:, h:h + 1], eye_f, [SMALLB, CONSTF],
                              start=(hh == 0), stop=(hh == 3))
                    vop(lambda e, g=grp: e.tensor_copy(
                            nmld_row[0:1, g * 512:(g + 1) * 512], PB[0:1, 0:512]),
                        reads=[PBB], writes=[STATB])
                for half in range(2):
                    mm(PBB, PB[:, 0:512], ones_f[0:1, 0:128],
                       nmld_row[0:1, half * 512:(half + 1) * 512],
                       [STATB, CONSTF], start=True, stop=True)
                    sop(lambda e, half=half, b=bcB: e.copy(
                            b[:, half * 512:(half + 1) * 512], PB[:, 0:512]),
                        reads=[PBB], writes=[GATEB])

            def pass2(tt):
                par = tt % 2
                bcB = bcastB[:, par * 1024:(par + 1) * 1024]
                nkts = (TW - tt if w > 0 else 0) + (tt + 1)

                def emit_at(h, sls):
                    hp, hpr = (h % 2) * 64, h // 2
                    for idx, (wk, j) in enumerate(kts_of(tt)):
                        kfx, kbx = (kfp, kb_prev) if wk == "p" else (kfw, kb_cur)
                        sl = slot_ctr[0] % 8
                        slot_ctr[0] += 1
                        sls.append((sl, wk, j))
                        pap, pbf = slot_ap(sl)
                        op("T", lambda e, o=pap, l1=kfx[hp:hp + 64, hpr, j * 128:(j + 1) * 128],
                           r=qfc[hp:hp + 64, hpr, tt * 128:(tt + 1) * 128]:
                           e.matmul(o, l1, r, start=True, stop=True),
                           reads=[kbx, QFCB], writes=[pbf])

                def slot_runs(sls):
                    runs = []
                    for sl, wk, j in sls:
                        if runs and runs[-1][0] + runs[-1][1] == sl and sl % 4 != 0:
                            runs[-1][1] += 1
                        else:
                            runs.append([sl, 1])
                    return runs

                def rest_a(h, sls):
                    for s0, ln_ in slot_runs(sls):
                        pbf = slot_ap(s0)[1]
                        wide = PC if s0 < 4 else PD
                        off = (s0 % 4) * 128
                        vop(lambda e, w_=wide, off=off, ln_=ln_, h=h, b=bcB:
                            e.tensor_tensor(
                                w_[:, off:off + ln_ * 128].rearrange(
                                    "p (a b) -> p a b", a=ln_),
                                w_[:, off:off + ln_ * 128].rearrange(
                                    "p (a b) -> p a b", a=ln_),
                                b[:, h * 128:(h + 1) * 128].unsqueeze(1)
                                .broadcast_to([128, ln_, 128]), ALU.add),
                            reads=[GATEB], writes=[pbf])
                    for sl, wk, j in sls:
                        pap, pbf = slot_ap(sl)
                        if wk == "p" and j == tt:
                            vop(lambda e, o=pap: e.tensor_tensor(o, o, mh_b, ALU.add),
                                reads=[CONSTB], writes=[pbf])
                        if wk == "c" and j == tt:
                            vop(lambda e, o=pap: e.tensor_tensor(o, o, ml_b, ALU.add),
                                reads=[CONSTB], writes=[pbf])
                    for s0, ln_ in slot_runs(sls):
                        pbf = slot_ap(s0)[1]
                        wide = PC if s0 < 4 else PD
                        off = (s0 % 4) * 128
                        pp = s0
                        sop(lambda e, w_=wide, off=off, ln_=ln_, pp=pp: e.activation(
                                ptile8[:, pp:pp + ln_, :].rearrange("p a b -> p (a b)"),
                                w_[:, off:off + ln_ * 128], AF.Exp),
                            reads=[pbf],
                            writes=[PTL8[s0 + k] for k in range(ln_)])
                def rest_b(h, sls):
                    hp = (h % 2) * 64
                    oc0 = (h // 2) * 128
                    for idx, (sl, wk, j) in enumerate(sls):
                        vtx, vbx = (vtp, vb_prev) if wk == "p" else (vtw, vb_cur)
                        op("T", lambda e, pp=sl, l1=vtx[:, j, h * 64:(h + 1) * 64],
                           hp=hp, oc0=oc0, st=(idx == 0), sp=(idx == nkts - 1):
                           e.matmul(PS[hp:hp + 64, oc0:oc0 + 128], l1,
                                    ptile8[:, pp, :], start=st, stop=sp),
                           reads=[vbx, PTL8[sl]], writes=[PSB])

                prev = None
                for h in range(HC):
                    sls = []
                    if prev is not None:
                        rest_a(prev[0], prev[1])
                    emit_at(h, sls)
                    if prev is not None:
                        rest_b(prev[0], prev[1])
                    prev = (h, sls)
                rest_a(prev[0], prev[1])
                rest_b(prev[0], prev[1])
                for pr in range(4):
                    vop(lambda e, pr=pr, tt=tt: e.tensor_copy(
                            ofc[:, pr, tt * 128:(tt + 1) * 128],
                            PS[:, pr * 128:(pr + 1) * 128]),
                        reads=[PSB], writes=[OFCB])

            emit_ybf(0)
            for tt in range(TW):
                ti = tt % 2
                qp, qpb = PA[:, 0:512], PAQB
                kp, kpb = PA[:, 512:1024], PAKB
                vp, vpb = PE[:, 0:512], PEB
                for kc in range(NCH):
                    lhs = ybf[:, ti, kc, 1:129] if kc < 4 else ybf[:, ti, kc, 0:128]
                    mm(qpb, qp[:, 0:512], lhs, fqkv_b[:, kc, 0:512],
                       [YBFB[ti], FQC[kc]], start=(kc == 0), stop=(kc == NCH - 1))
                    mm(kpb, kp[:, 0:512], lhs, fqkv_b[:, kc, 512:1024],
                       [YBFB[ti], FQC[kc]], start=(kc == 0), stop=(kc == NCH - 1))
                    mm(vpb, vp[:, 0:512], lhs, fqkv_b[:, kc, 1024:1536],
                       [YBFB[ti], FQC[kc]], start=(kc == 0), stop=(kc == NCH - 1))
                if tt + 1 < TW:
                    emit_ybf(tt + 1)
                sop(lambda e, tt=tt, vv=vtw, vp=vp: e.activation(vv[:, tt, :],
                                                                 vp[:, 0:512], AF.Copy),
                    reads=[vpb], writes=[vb_cur])
                rotary_pair(l, w, tt, qp, qpb, kp, kpb, kfw, kb_cur)

            pass1(0)
            nmphase(0)
            for tt in range(TW):
                if tt + 1 < TW:
                    pass1(tt + 1)
                pass2(tt)
                if tt + 1 < TW:
                    nmphase(tt + 1)

            c0 = 1 + w * WIN
            for dc in range(NCH):
                pb, pbuf = (PC, PCB) if dc % 2 == 0 else (PD, PDB)
                for kc in range(4):
                    mm(pbuf, pb[:, 0:WIN], fwo_b[:, kc, dc * 128:(dc + 1) * 128],
                       ofc[:, kc, :], [FOC[kc], OFCB], start=(kc == 0), stop=(kc == 3))
                vop(lambda e, d=dc, p=pb, c0=c0: e.tensor_tensor(
                        sF[:, d, c0:c0 + WIN], p[:, 0:WIN],
                        sF[:, d, c0:c0 + WIN], ALU.add),
                    reads=[pbuf] + wsf, writes=wsf)
            if l == 0:
                tap(f"vt0_w{w}", vtc[:, 0], [VTCB[0]])
                tap(f"kf0_w{w}", kfc[:, 0], [KFCB[0]])

    fw1r = [fw1[l].rearrange("(c p) o -> p c o", p=128) for l in range(2)]

    def load_w1(l, osn, slot):
        o0, ow = OSR[osn]
        for part, off in ((0, o0), (1, HID + o0)):
            i = _wi[0] % 2
            _wi[0] += 1
            stg = fv("WST", 1024, 1024 * i).rearrange(
                "p (c o) -> p c o", c=NCH)[:, :, 0:ow]
            dmaQ(f"WST{i}", stg, fw1r[l][:, :, off:off + ow], writes=[WST[i]])
            eng = "G" if part == 0 else "V"
            op(eng, lambda e, s=slot, ow=ow, g=stg, part=part: e.tensor_copy(
                   w1v[s][:, :, part * 128:part * 128 + ow], g),
               reads=[WST[i]], writes=[W1B[slot]])

    def load_w2(l, osn, slot):
        o0, ow = OSR[osn]
        i = _wi[0] % 2
        _wi[0] += 1
        stg = fv("WST", 1024, 1024 * i)[0:ow, :]
        dmaQ(f"WST{i}", stg, fw2[l, o0:o0 + ow, :], writes=[WST[i]])
        eng = "G" if slot % 2 == 0 else "V"
        op(eng, lambda e, s=slot, ow=ow, g=stg: e.tensor_copy(w2v[s][0:ow, :], g),
           reads=[WST[i]], writes=[W2B[slot]])

    def stage_c_mlp(l, last):
        vop(lambda e: e.tensor_copy(ybnd[:, :, 0:1], sF[:, :, N // 2:N // 2 + 1]),
            reads=sf_bufs(N // 2, N // 2 + 1), writes=[YBND])

        def emit_ln4_stats(half, g):
            ln_stats_sf(1 + half * (N // 2) + g * 512, 512, shifted=True)

        def emit_ln4_apply(half, g):
            g0 = 1 + half * (N // 2) + g * 512
            bcast_rows(512)
            for c in range(NCH):
                jwarm(2)
                off = -1 if c >= 4 else 0
                ap = sF[:, c, g0 + off: g0 + off + 512]
                rb = sf_bufs(g0 + off, g0 + off + 512)
                vop(lambda e, a=ap: e.tensor_tensor(scr[:, 0:512], a,
                                                    PC[:, 0:512], ALU.mult),
                    reads=rb + [PCB], writes=[SCRB])
                vop(lambda e: e.tensor_tensor(scr[:, 512:1024], scr[:, 0:512],
                                              PD[:, 0:512], ALU.add),
                    reads=[SCRB, PDB], writes=[SCRB])
                sop(lambda e, c=c, g=g: e.activation(
                        ln4f[:, c, g * 512:(g + 1) * 512], scr[:, 512:1024],
                        AF.Identity, bias=pvcol(f"ln_b{l}", c),
                        scale=pvcol(f"ln_g{l}", c)),
                    reads=[SCRB, CONSTF], writes=[LN4C[g * NCH + c]])

        def w1_phase(tgs):
            for osn in range(NOS):
                slot = osn % 2
                o0, ow = OSR[osn]
                for ti, tg in enumerate(tgs):
                    par = (osn * 2 + ti) % 2
                    pa_, pab = (PC, PCB) if par == 0 else (PE, PEB)
                    pg_, pgb = (PD, PDB) if par == 0 else (PB, PBB)
                    for kc in range(NCH):
                        mm(pab, pa_[0:ow, 0:512], w1v[slot][:, kc, 0:ow],
                           ln4f[:, kc, tg * 512:(tg + 1) * 512],
                           [W1B[slot], LN4C[tg * NCH + kc]],
                           start=(kc == 0), stop=(kc == NCH - 1))
                        mm(pgb, pg_[0:ow, 0:512], w1v[slot][:, kc, 128:128 + ow],
                           ln4f[:, kc, tg * 512:(tg + 1) * 512],
                           [W1B[slot], LN4C[tg * NCH + kc]],
                           start=(kc == 0), stop=(kc == NCH - 1))
                    gslot = gel[0:ow, tg * 512:(tg + 1) * 512]
                    sop(lambda e, g=gslot, ow=ow, osn=osn, p=pg_: e.activation(
                            g, p[0:ow, 0:512], AF.Gelu,
                            bias=pvcol(f"b1g_{l}", osn, rows=ow)),
                        reads=[pgb, CONSTF], writes=[GATEB])
                    vop(lambda e, g=gslot, ow=ow, osn=osn, tg=tg, p=pa_:
                        e.scalar_tensor_tensor(
                            hg_regions[osn][0:ow, tg * 512:(tg + 1) * 512],
                            p[0:ow, 0:512], pvcol(f"b1a_{l}", osn, rows=ow),
                            g, ALU.add, ALU.mult),
                        reads=[pab, GATEB, CONSTF], writes=[HGB])
                if osn + 2 < NOS:
                    load_w1(l, osn + 2, slot)

        groups = [list(range(g, min(g + 4, NOS))) for g in range(0, NOS, 4)]

        def w2_phase(half, interleave):
            hb0 = 1 + half * (N // 2)
            for k, osn in enumerate(groups[0]):
                load_w2(l, osn, k)
            for gi, grp in enumerate(groups):
                if interleave and gi == 4:
                    emit_ln4_stats(1, 1)
                for tg in range(2):
                    g0 = hb0 + tg * 512
                    gsf = sf_bufs(g0, g0 + 512)
                    for dc in range(NCH):
                        pb, pbuf = (PC, PCB) if dc % 2 == 0 else (PD, PDB)
                        for k, osn in enumerate(grp):
                            o0, ow = OSR[osn]
                            sl = (gi % 2) * 4 + k
                            mm(pbuf, pb[:, 0:512],
                               w2v[sl][0:ow, dc * 128:(dc + 1) * 128],
                               hg_regions[osn][0:ow, tg * 512:(tg + 1) * 512],
                               [W2B[sl], HGB], start=(k == 0),
                               stop=(k == len(grp) - 1))
                        if gi == 0:
                            vop(lambda e, d=dc, p=pb, g0=g0: e.scalar_tensor_tensor(
                                    sF[:, d, g0:g0 + 512], p[:, 0:512],
                                    pvcol(f"b2_{l}", d), sF[:, d, g0:g0 + 512],
                                    ALU.add, ALU.add),
                                reads=[pbuf, CONSTF] + gsf, writes=gsf)
                        else:
                            vop(lambda e, d=dc, p=pb, g0=g0: e.tensor_tensor(
                                    sF[:, d, g0:g0 + 512], p[:, 0:512],
                                    sF[:, d, g0:g0 + 512], ALU.add),
                                reads=[pbuf] + gsf, writes=gsf)
                        if last and gi == len(groups) - 1:
                            xsl = dc % 2
                            dmaG(f"X2R{xsl}", fv("TAB", 512, xsl * 512),
                                 x2s[dc * 128:(dc + 1) * 128, g0 - 1:g0 - 1 + 512],
                                 reads=[B("X2SB")], writes=[X2R[xsl]])
                            vop(lambda e, d=dc, g0=g0, xsl=xsl: e.tensor_tensor(
                                    sF[:, d, g0:g0 + 512], sF[:, d, g0:g0 + 512],
                                    fv("TAB", 512, xsl * 512), ALU.add),
                                reads=[X2R[xsl]] + gsf, writes=gsf)
                            dmaG("OUT", outT[dc * 128:(dc + 1) * 128, g0 - 1:g0 - 1 + 512],
                                 sF[:, dc, g0:g0 + 512], reads=gsf)
                    if tg == 0 and gi + 1 < len(groups):
                        for k, osn in enumerate(groups[gi + 1]):
                            load_w2(l, osn, ((gi + 1) % 2) * 4 + k)

        load_w1(l, 0, 0)
        load_w1(l, 1, 1)
        for g in range(2):
            emit_ln4_stats(0, g)
            emit_ln4_apply(0, g)
        w1_phase((0, 1))
        w2_phase(0, interleave=True)
        emit_ln4_apply(1, 1)
        load_w1(l, 0, 0)
        load_w1(l, 1, 1)
        vop(lambda e: e.tensor_copy(ybnd[:, :, 1:2],
                                    sF[:, :, N // 2:N // 2 + 1]),
            reads=sf_bufs(N // 2, N // 2 + 1), writes=[YBND])
        vop(lambda e: e.tensor_copy(sF[:, :, N // 2:N // 2 + 1],
                                    ybnd[:, :, 0:1]),
            reads=[YBND], writes=sf_bufs(N // 2, N // 2 + 1))
        emit_ln4_stats(1, 0)
        emit_ln4_apply(1, 0)
        vop(lambda e: e.tensor_copy(sF[:, :, N // 2:N // 2 + 1],
                                    ybnd[:, :, 1:2]),
            reads=[YBND], writes=sf_bufs(N // 2, N // 2 + 1))
        w1_phase((1, 0))
        w2_phase(1, interleave=False)

    for l in range(2):
        stage_c_attn(l)
        kb.barrier()
        if l == 0:
            tap_sf("ya0")
            tap("qfc0", qfc, [QFCB])
            tap("kfc0", kfc, [KFCB[0], KFCB[1]])
            tap("vtc0", vtc, [VTCB[0], VTCB[1]])
            tap("ofc0", ofc, [OFCB])
        stage_c_mlp(l, last=(l == 1))
        kb.barrier()
        if l == 0:
            tap_sf("ym0")

    kb.finalize_blocks()
    kb.tap_t = tap_t
    return kb


_BUILD_CACHE = {}

def _get_kb():
    if "kb" not in _BUILD_CACHE:
        _BUILD_CACHE["kb"] = build_program()
    return _BUILD_CACHE["kb"]


def prep_common(inputs):
    consts = _host_constants()
    pv = _pack_pvec(inputs)
    qs = np.asarray(inputs["L0_ffn_q_scale"], np.float32)
    ks = np.asarray(inputs["L0_ffn_k_scale"], np.float32)
    qsc = np.zeros((2, 4, 128, HD), np.float32)
    for l in range(2):
        qsc[l, 0] = np.broadcast_to(qs[l], (128, HD))
        qsc[l, 1] = np.broadcast_to(np.concatenate([qs[l, 32:], qs[l, :32]]), (128, HD))
        qsc[l, 2] = np.broadcast_to(ks[l], (128, HD))
        qsc[l, 3] = np.broadcast_to(np.concatenate([ks[l, 32:], ks[l, :32]]), (128, HD))

    return dict(
        wqkvT=np.ascontiguousarray(np.asarray(inputs["L0_attn_w_qkv"], np.float32).T),
        wg1T=np.ascontiguousarray(np.asarray(inputs["L0_attn_w_g1"], np.float32).T),
        wg2T=np.ascontiguousarray(np.asarray(inputs["L0_attn_w_g2"], np.float32).T),
        woutT=np.ascontiguousarray(np.asarray(inputs["L0_attn_w_out"], np.float32).T),
        fqkv=np.ascontiguousarray(np.asarray(inputs["L0_ffn_wqkv"], np.float32)),
        fwo=np.ascontiguousarray(np.asarray(inputs["L0_ffn_wo"], np.float32)),
        fw1=np.ascontiguousarray(np.asarray(inputs["L0_ffn_w1"], np.float32)),
        fw2=np.ascontiguousarray(np.asarray(inputs["L0_ffn_w2"], np.float32)),
        qscale=qsc, pvec=pv,
        tabs=consts["tabs"], tabslo=consts["tabslo"], ml=consts["ml"], mh=consts["mh"],
        maskb=consts["maskb"], eye_f=consts["eye_f"], eye_b=consts["eye_b"],
        ones_f=consts["ones_f"], ones_b=consts["ones_b"],
    )


def kernel(**inputs):
    common = prep_common(inputs)
    x = np.asarray(inputs["L0_attn_x"], np.float32)
    kb = _get_kb()
    in_maps = []
    for b in range(8):
        m = dict(common)
        m["xT"] = np.ascontiguousarray(x[b].T)
        in_maps.append(m)
    res = run_bass_kernel_spmd(kb.nc, in_maps, core_ids=list(range(8)))
    out = np.stack([np.ascontiguousarray(r["outT"].T) for r in res.results])
    return out.astype(np.float32)


# version 15
# speedup vs baseline: 1.0892x; 1.0020x over previous
import numpy as np
import ml_dtypes
import concourse.bass as bass
import concourse.mybir as mybir
from concourse.bass_utils import run_bass_kernel_spmd

F32 = mybir.dt.float32
BF16 = mybir.dt.bfloat16
AF = mybir.ActivationFunctionType
ALU = mybir.AluOpType
AX = mybir.AxisListType

D = 1024
NCH = 8
N = 2048
SUP = 256
NSUP = N // SUP
HB = 16
HD = 64
HC = 8
WIN = 512
NW = N // WIN
TW = WIN // 128
HID = 2730
FFI = 1536
NEG = -1.0e30
OS_SIZES = [128] * (HID // 128) + ([HID % 128] if HID % 128 else [])
NOS = len(OS_SIZES)

def _os_ranges():
    o0, out = 0, []
    for w in OS_SIZES:
        out.append((o0, w))
        o0 += w
    return out

OSR = _os_ranges()

PVIDX = {}
_pc = 0
def _pvadd(name, n):
    global _pc
    PVIDX[name] = _pc
    _pc += n
_pvadd("eps", 1)
_pvadd("b_g1", 1)
_pvadd("b_g2", 8)
_pvadd("gamma", 8)
_pvadd("beta", 8)
for _l in range(2):
    _pvadd(f"ln_g{_l}", 8)
    _pvadd(f"ln_b{_l}", 8)
    _pvadd(f"b2_{_l}", 8)
for _l in range(2):
    _pvadd(f"b1a_{_l}", NOS)
    _pvadd(f"b1g_{_l}", NOS)
PV = _pc


def _bf(x):
    return np.ascontiguousarray(np.asarray(x)).astype(ml_dtypes.bfloat16)


def _host_constants():
    half_idx = np.arange(0, HD, 2, dtype=np.float64)
    inv_freq = 1.0 / 10000.0 ** (half_idx / HD)
    t = np.arange(N, dtype=np.float64)
    freqs = t[:, None] * inv_freq[None, :]
    freqs = np.concatenate([freqs, freqs], axis=-1)
    base = (half_idx + 0.4 * HD) / (1.4 * HD)
    power = (t - N // 2) / float(WIN // 2)
    scale = base[None, :] ** power[:, None]
    scale = np.concatenate([scale, scale], axis=-1)
    cs, sn = np.cos(freqs), np.sin(freqs)
    t64 = [cs * scale * 8.0, sn * scale * 8.0, cs / scale, sn / scale]

    def wtile(a):
        return np.ascontiguousarray(
            a.reshape(NW, TW, 128, HD).transpose(0, 2, 1, 3).reshape(NW, 128, TW * HD))

    hi = [a.astype(np.float32) for a in t64]
    lo = [(a - h.astype(np.float64)).astype(np.float32) for a, h in zip(t64, hi)]
    tabs = np.stack([wtile(h) for h in hi])
    tabslo = np.stack([wtile(x) for x in lo])
    ii = np.arange(128)
    ml = np.where(ii[None, :] < ii[:, None], NEG, 0.0).astype(np.float32)
    mh = np.where(ii[None, :] > ii[:, None], NEG, 0.0).astype(np.float32)
    triu_keep = (ii[:, None] <= ii[None, :]).astype(np.float32)
    maskb = np.concatenate([triu_keep, np.ones((128, 128), np.float32), triu_keep], 1)
    eye = np.eye(128, dtype=np.float32)
    ones = np.ones((128, 128), np.float32)
    return dict(tabs=tabs, tabslo=tabslo, ml=_bf(ml), mh=_bf(mh), maskb=_bf(maskb),
                eye_f=eye, eye_b=_bf(eye), ones_f=ones, ones_b=_bf(ones))


def _pack_pvec(ins):
    cols = np.zeros((128, PV), np.float32)
    def put(name, k, v):
        cols[: v.shape[0], PVIDX[name] + k] = v
    put("eps", 0, np.full((128,), 1e-5, np.float32))
    put("b_g1", 0, np.asarray(ins["L0_attn_b_g1"], np.float32))
    for c in range(NCH):
        put("b_g2", c, np.asarray(ins["L0_attn_b_g2"], np.float32)[c * 128:(c + 1) * 128])
        put("gamma", c, np.asarray(ins["L0_attn_gamma"], np.float32)[c * 128:(c + 1) * 128])
        put("beta", c, np.asarray(ins["L0_attn_beta"], np.float32)[c * 128:(c + 1) * 128])
    for l in range(2):
        for c in range(NCH):
            put(f"ln_g{l}", c, np.asarray(ins["L0_ffn_ln_g"], np.float32)[l, c * 128:(c + 1) * 128])
            put(f"ln_b{l}", c, np.asarray(ins["L0_ffn_ln_b"], np.float32)[l, c * 128:(c + 1) * 128])
            put(f"b2_{l}", c, np.asarray(ins["L0_ffn_b2"], np.float32)[l, c * 128:(c + 1) * 128])
        b1 = np.asarray(ins["L0_ffn_b1"], np.float32)[l]
        for k, (o0, ow) in enumerate(OSR):
            put(f"b1a_{l}", k, b1[o0:o0 + ow])
            put(f"b1g_{l}", k, b1[HID + o0:HID + o0 + ow])
    return cols



class Buf:
    __slots__ = ("w", "rs", "name")

    def __init__(self, name=""):
        self.w = {}
        self.rs = {}
        self.name = name

    def toks_w(self):
        return list(self.w.items())

    def toks_rw(self):
        out = dict(self.w)
        for s, v in self.rs.items():
            out[s] = max(out.get(s, 0), v)
        return list(out.items())

    def add_read(self, tok):
        s, v = tok
        self.rs[s] = max(self.rs.get(s, 0), v)

    def set_write(self, tok):
        s, v = tok
        self.w = {s: v}
        self.rs = {}


class KB:
    ENGS = ("T", "V", "S", "Q", "G")

    def __init__(self):
        self.nc = bass.Bass("TRN2", target_bir_lowering=False)
        self.prog = {e: [] for e in self.ENGS}
        self.cnt = {"T": 0, "V": 0, "S": 0, "Q": 0, "G": 0}
        self.wm = {e: {} for e in self.ENGS}
        self.n_wait = 0
        self.n_inst = 0

    def build_resources(self):
        import contextlib
        self.stack = contextlib.ExitStack()
        ent = self.stack.enter_context
        self.sem = {k: ent(self.nc.semaphore(f"sem_{k}"))
                    for k in ("T", "V", "S", "Q", "G")}
        return ent

    def dma_sem(self, name):
        if name not in self.sem:
            self.sem[name] = self.stack.enter_context(
                self.nc.semaphore(f"sem_{name}"))
            self.cnt[name] = 0
        return self.sem[name]

    def op(self, eng, fn, reads=(), writes=(), deps=(), dma=False):
        need = {}
        for b in reads:
            for s, v in b.toks_w():
                need[s] = max(need.get(s, 0), v)
        for b in writes:
            for s, v in b.toks_rw():
                need[s] = max(need.get(s, 0), v)
        for s, v in deps:
            need[s] = max(need.get(s, 0), v)
        wmm = self.wm[eng]
        waits = []
        for s, v in sorted(need.items()):
            if wmm.get(s, 0) < v:
                waits.append((s, v))
                wmm[s] = v
        if dma:
            kind = dma
            self.dma_sem(kind)
            inc = 16
        else:
            kind, inc = eng, 1
        self.cnt[kind] += inc
        tok = (kind, self.cnt[kind])
        self.n_wait += len(waits)
        self.n_inst += 1
        self.prog[eng].append((waits, fn, kind, inc))
        for b in reads:
            b.add_read(tok)
        for b in writes:
            b.set_write(tok)
        return tok

    def snapshot(self):
        return [(k, v) for k, v in self.cnt.items() if v > 0]

    def barrier(self):
        toks = self.snapshot()
        for e in self.ENGS:
            self.op(e, lambda en: en.nop(), deps=toks)

    def finalize_blocks(self):
        sem, prog, cnt = self.sem, self.prog, self.cnt
        engmap = {"T": "tensor", "V": "vector", "S": "scalar", "Q": "sync",
                  "G": "gpsimd"}
        with self.nc.Block() as block:
            for e in self.ENGS:
                def mk(e):
                    def body(eng):
                        for waits, fn, kind, inc in prog[e]:
                            for s, v in waits:
                                eng.wait_ge(sem[s], v)
                            fn(eng).then_inc(sem[kind], inc)
                        if e == "Q":
                            for k, v in cnt.items():
                                if v and k != "Q":
                                    eng.wait_ge(sem[k], v)
                    return body
                getattr(block, engmap[e])(mk(e))



def build_program(debug=False):
    kb = KB()
    nc = kb.nc
    ent = kb.build_resources()
    tap_t = {}

    def din(name, shape, dt=F32):
        return nc.dram_tensor(name, list(shape), dt, kind="ExternalInput").ap()

    xT = din("xT", [D, N])
    wqkvT = din("wqkvT", [D, 3 * D])
    wg1T = din("wg1T", [D, HD])
    wg2T = din("wg2T", [HD, D])
    woutT = din("woutT", [D, D])
    fqkv = din("fqkv", [2, D, FFI])
    fwo = din("fwo", [2, 512, D])
    fw1 = din("fw1", [2, D, 2 * HID])
    fw2 = din("fw2", [2, HID, D])
    qscaleD = din("qscale", [2, 4, 128, HD])
    pvecD = din("pvec", [128, PV])
    tabsD = din("tabs", [4, NW, 128, TW * HD])
    tabsloD = din("tabslo", [4, NW, 128, TW * HD])
    mlD = din("ml", [128, 128], BF16)
    mhD = din("mh", [128, 128], BF16)
    maskbD = din("maskb", [128, 384], BF16)
    eyefD = din("eye_f", [128, 128])
    eyebD = din("eye_b", [128, 128], BF16)
    onesfD = din("ones_f", [128, 128])
    onesbD = din("ones_b", [128, 128], BF16)

    outT = nc.dram_tensor("outT", [D, N], F32, kind="ExternalOutput").ap()
    x2s = nc.dram_tensor("x2s", [D, N], F32).ap()

    sF = ent(nc.sbuf_tensor("sF", [128, NCH, N + 1], F32))

    fplan, bplan = {}, {}
    _fc = [0]
    def fslot(name, cols):
        fplan[name] = (_fc[0], cols)
        _fc[0] += cols
    fslot("STAT", 1024)
    fslot("SMALL", 128)
    fslot("GATE", 2048)
    fslot("SCR", 1536)
    fslot("WST", 2048)
    fslot("TAB", 1024)
    fslot("CONST", 448)
    fslot("YBND", 32)
    fslot("S32", 512)
    AFt = ent(nc.sbuf_tensor("AFa", [128, _fc[0]], F32))

    _bc = [0]
    def bslot(name, cols):
        bplan[name] = (_bc[0], cols)
        _bc[0] += cols
    bslot("YBF", 2064)
    bslot("QF", 2048)
    bslot("KF", 4096)
    bslot("VT", 4096)
    bslot("KST", 2048)
    bslot("ATM", 1536)
    bslot("OF", 2048)
    bslot("SBF", 512)
    bslot("WA", 24576)
    bslot("WB", 8192)
    bslot("WG", 1536)
    bslot("CONB", 896)
    ABt = ent(nc.sbuf_tensor("ABa", [128, _bc[0]], BF16))

    total_bytes = (NCH * (N + 1)) * 4 + _fc[0] * 4 + _bc[0] * 2
    assert total_bytes <= 209500, total_bytes

    PA = ent(nc.psum_tensor("PA", [128, 1024], F32))
    PC = ent(nc.psum_tensor("PC", [128, 512], F32))
    PD = ent(nc.psum_tensor("PD", [128, 512], F32))
    PE = ent(nc.psum_tensor("PE", [128, 512], F32))
    PB = ent(nc.psum_tensor("PB", [128, 512], F32))
    PS = ent(nc.psum_tensor("PS", [128, 512], F32))
    PT = ent(nc.psum_tensor("PT", [128, 1024], BF16))

    def fv(name, cols=None, off=0):
        o, c = fplan[name]
        if cols is None:
            cols = c - off
        assert off >= 0 and off + cols <= c, (name, off, cols)
        return AFt[:, o + off:o + off + cols]

    def bv(name, cols=None, off=0):
        o, c = bplan[name]
        if cols is None:
            cols = c - off
        assert off >= 0 and off + cols <= c, (name, off, cols)
        return ABt[:, o + off:o + off + cols]

    bufs = {}
    def B(name):
        if name not in bufs:
            bufs[name] = Buf(name)
        return bufs[name]

    SFB = [B(f"sF{s}") for s in range(NSUP)]

    def sf_bufs(c0, c1):
        lo = max(0, (c0 - 1)) // SUP
        hi = min(NSUP - 1, max(0, (c1 - 2)) // SUP)
        return [SFB[i] for i in range(lo, hi + 1)]

    PAB0, PAB1 = B("PA0"), B("PA1")
    PCB, PDB, PEB, PBB, PSB, PTB = (B(x) for x in ("PCp", "PDp", "PEp", "PBp", "PSp", "PTp"))

    op = kb.op

    def mm(pbuf, out_ap, lhsT_ap, rhs_ap, rbufs, start, stop, deps=()):
        return op("T", lambda e, o=out_ap, l=lhsT_ap, r=rhs_ap, st=start, sp=stop:
                  e.matmul(o, l, r, start=st, stop=sp),
                  reads=list(rbufs), writes=[pbuf], deps=deps)

    def trans(pbuf, out_ap, in_ap, ident_ap, rbufs, start=True, stop=True):
        return op("T", lambda e, o=out_ap, i=in_ap, d=ident_ap, st=start, sp=stop:
                  e.matmul(o, i, d, start=st, stop=sp, is_transpose=True),
                  reads=list(rbufs), writes=[pbuf])

    def vop(fn, reads=(), writes=(), deps=()):
        return op("V", fn, reads=reads, writes=writes, deps=deps)

    def sop(fn, reads=(), writes=(), deps=()):
        return op("S", fn, reads=reads, writes=writes, deps=deps)

    def dmaQ(sem, out_ap, in_ap, reads=(), writes=()):
        return op("Q", lambda e, o=out_ap, i=in_ap: e.dma_start(out=o, in_=i),
                  reads=reads, writes=writes, dma=sem)

    def dmaG(sem, out_ap, in_ap, reads=(), writes=()):
        return op("G", lambda e, o=out_ap, i=in_ap: e.dma_start(out=o, in_=i),
                  reads=reads, writes=writes, dma=sem)

    def jwarm(n):
        for _ in range(n):
            op("T", lambda e: e.matmul(PA[:, 512:1024], eye_b, warm_rhs,
                                       start=True, stop=True))

    CONSTF, CONSTB, INB = B("CONSTF"), B("CONSTB"), B("INCHAIN")
    eye_f = fv("CONST", 128, 0)
    ones_f = fv("CONST", 128, 128)
    pvecS = fv("CONST", PV, 256)
    eye_b = bv("CONB", 128, 0)
    ones_b = bv("CONB", 128, 128)
    ml_b = bv("CONB", 128, 256)
    mh_b = bv("CONB", 128, 384)
    maskb_b = bv("CONB", 384, 512)
    dmaG("IN", eye_f, eyefD, writes=[CONSTF, INB])
    dmaG("IN", ones_f, onesfD, writes=[CONSTF, INB])
    dmaG("IN", pvecS, pvecD, writes=[CONSTF, INB])
    dmaG("IN", eye_b, eyebD, writes=[CONSTB, INB])
    dmaG("IN", ones_b, onesbD, writes=[CONSTB, INB])
    dmaG("IN", ml_b, mlD, writes=[CONSTB, INB])
    dmaG("IN", mh_b, mhD, writes=[CONSTB, INB])
    dmaG("IN", maskb_b, maskbD, writes=[CONSTB, INB])

    def pvcol(name, k=0, rows=128):
        return pvecS[0:rows, PVIDX[name] + k: PVIDX[name] + k + 1]

    def tap(name, ap, rbufs, rows=128):
        if not debug:
            return
        shp = [rows] + [int(d) for d in ap.shape[1:]]
        t = nc.dram_tensor("tap_" + name, shp, ap.dtype, kind="ExternalOutput").ap()
        tap_t[name] = shp
        dmaG("TAPS", t, ap, reads=list(rbufs))

    def tap_sf(name):
        if not debug:
            return
        t = nc.dram_tensor("tap_" + name, [D, N], F32, kind="ExternalOutput").ap()
        tap_t[name] = (D, N)
        for c in range(NCH):
            dmaG("TAPS", t[c * 128:(c + 1) * 128, :], sF[:, c, 1:N + 1],
                 reads=list(SFB))

    warm_rhs = bv("CONB", 512, 256)
    for i in range(24):
        op("T", lambda e: e.matmul(PB[:, 0:512], eye_b, warm_rhs,
                                   start=True, stop=True),
           reads=[CONSTB], writes=[PBB])

    for c in range(NCH):
        dmafn = dmaG if c % 2 else dmaQ
        dmafn("IN", sF[:, c, 1:N + 1], xT[c * 128:(c + 1) * 128, :],
              writes=list(SFB) + [INB])
    vop(lambda e: e.memset(sF[:, :, 0:1], 0.0), writes=[SFB[0]])

    WST = [B("WST0"), B("WST1")]
    _wi = [0]
    _rr = [0]
    RR_ENGS = ("G", "V", "S")

    def cast_w(dst_ap, dst_buf, src_ap, rows, cols, eng="S", extra_w=()):
        done = 0
        while done < cols:
            w = min(1024, cols - done)
            i = _wi[0] % 2
            _wi[0] += 1
            stg = fv("WST", w, 1024 * i)[0:rows, :]
            dmaQ(f"WST{i}", stg, src_ap[:, done:done + w], writes=[WST[i]])
            dpc = dst_ap[0:rows, done:done + w]
            ee = eng
            if ee == "RR":
                ee = RR_ENGS[_rr[0] % 3]
                _rr[0] += 1
            wl = [dst_buf] + list(extra_w)
            if ee == "S":
                sop(lambda e, d=dpc, g=stg: e.copy(d, g),
                    reads=[WST[i]], writes=wl)
            elif ee == "G":
                op("G", lambda e, d=dpc, g=stg: e.tensor_copy(d, g),
                   reads=[WST[i]], writes=wl)
            else:
                vop(lambda e, d=dpc, g=stg: e.tensor_copy(d, g),
                    reads=[WST[i]], writes=wl)
            done += w

    fqkv_b = bv("WA", 12288).rearrange("p (c o) -> p c o", c=NCH)
    fwo_b = bv("WA", 4096, 12288).rearrange("p (c o) -> p c o", c=4)
    FQC = [B(f"FQC{c}") for c in range(NCH)]
    FOC = [B(f"FOC{c}") for c in range(4)]

    def emit_attn_cast(l, extra=()):
        for c in range(NCH):
            cast_w(fqkv_b[:, c, :], FQC[c], fqkv[l, c * 128:(c + 1) * 128, :],
                   128, FFI, eng="RR", extra_w=extra)
        for c in range(4):
            cast_w(fwo_b[:, c, :], FOC[c], fwo[l, c * 128:(c + 1) * 128, :],
                   128, D, eng="RR", extra_w=extra)

    rs_row = fv("STAT", 512, 0)
    c0_row = fv("STAT", 512, 512)
    nmld_row = fv("STAT", 1024, 0)
    STATB = B("STATB")
    SCRB = B("SCRB")
    GATEB = B("GATEB")
    SMALLB = B("SMALLB")
    sq_scr = fv("SCR", 1024, 512)

    def stat_finish(width):
        jwarm(6)
        mu = sq_scr[0:1, 0:width]
        sop(lambda e, w=width: e.activation(mu[0:1, 0:w], PE[0:1, 0:w], AF.Copy,
                                            scale=1.0 / D),
            reads=[PEB], writes=[SCRB])
        vop(lambda e, w=width: e.scalar_tensor_tensor(c0_row[0:1, 0:w], mu[0:1, 0:w],
                                                      1.0, mu[0:1, 0:w],
                                                      ALU.mult, ALU.mult),
            reads=[SCRB], writes=[STATB])
        vop(lambda e, w=width: e.scalar_tensor_tensor(rs_row[0:1, 0:w], PB[0:1, 0:w],
                                                      1.0 / D, c0_row[0:1, 0:w],
                                                      ALU.mult, ALU.subtract),
            reads=[PBB, STATB], writes=[STATB])
        sop(lambda e, w=width: e.activation(rs_row[0:1, 0:w], rs_row[0:1, 0:w],
                                            AF.Ln, bias=pvcol("eps", rows=1)),
            reads=[STATB, CONSTF], writes=[STATB])
        sop(lambda e, w=width: e.activation(rs_row[0:1, 0:w], rs_row[0:1, 0:w],
                                            AF.Exp, scale=-0.5),
            reads=[STATB], writes=[STATB])
        vop(lambda e, w=width: e.scalar_tensor_tensor(c0_row[0:1, 0:w], mu[0:1, 0:w],
                                                      -1.0, rs_row[0:1, 0:w],
                                                      ALU.mult, ALU.mult),
            reads=[SCRB, STATB], writes=[STATB])

    def ln_stats_sf(c0, width, shifted=False):
        for c in range(NCH):
            off = -1 if (shifted and c >= 4) else 0
            ap = sF[:, c, c0 + off: c0 + off + width]
            rb = sf_bufs(c0 + off, c0 + off + width)
            mm(PEB, PE[0:1, 0:width], ones_f[:, 0:1], ap, [CONSTF] + rb,
               start=(c == 0), stop=(c == NCH - 1))
        for c in range(NCH):
            off = -1 if (shifted and c >= 4) else 0
            ap = sF[:, c, c0 + off: c0 + off + width]
            rb = sf_bufs(c0 + off, c0 + off + width)
            sop(lambda e, a=ap, w=width: e.activation(sq_scr[:, 0:w], a, AF.Square),
                reads=rb, writes=[SCRB])
            if c < 2:
                jwarm(1)
            mm(PBB, PB[0:1, 0:width], ones_f[:, 0:1], sq_scr[:, 0:width],
               [CONSTF, SCRB], start=(c == 0), stop=(c == NCH - 1))
        stat_finish(width)

    def bcast_rows(width):
        mm(PCB, PC[:, 0:width], ones_f[0:1, 0:128], rs_row[0:1, 0:width],
           [CONSTF, STATB], start=True, stop=True)
        mm(PDB, PD[:, 0:width], ones_f[0:1, 0:128], c0_row[0:1, 0:width],
           [CONSTF, STATB], start=True, stop=True)

    WAB, WBB, WGB, SBFB = B("WAB"), B("WBB"), B("WGB"), B("SBFB")
    wqkv_b = bv("WA", 24576).rearrange("p (c o) -> p c o", c=NCH)
    wout_b = bv("WB", 8192).rearrange("p (c o) -> p c o", c=NCH)
    wg1_b = bv("WG", 512, 0).rearrange("p (c o) -> p c o", c=NCH)
    wg2_b = bv("WG", 1024, 512)

    for c in range(NCH):
        cast_w(wqkv_b[:, c, :], WAB, wqkvT[c * 128:(c + 1) * 128, :], 128, 3 * D,
               eng="RR")
        cast_w(wout_b[:, c, :], WBB, woutT[c * 128:(c + 1) * 128, :], 128, D,
               eng="RR")
        cast_w(wg1_b[:, c, :], WGB, wg1T[c * 128:(c + 1) * 128, :], 128, HD,
               eng="RR")
    cast_w(wg2_b, WGB, wg2T, 64, D, eng="RR")

    sbf = bv("SBF", 512)
    s32 = fv("S32", 512)
    S32B = B("S32B")
    vop(lambda e: e.memset(sbf[:], 0.0), writes=[SBFB])
    op("T", lambda e: e.matmul(PA[:, 512:1024], eye_b, warm_rhs,
                               start=True, stop=True), reads=[INB, CONSTB])
    jwarm(16)

    h1 = bv("YBF", 2048).rearrange("p (c t) -> p c t", c=NCH)
    H1C = [B(f"H1C{c}") for c in range(NCH)]
    qf = bv("QF", 2048).rearrange("p (c t) -> p c t", c=NCH)
    kf = bv("KF", 2048).rearrange("p (c t) -> p c t", c=NCH)
    vt = bv("VT", 2048).rearrange("p (t o) -> p t o", t=2)
    kst = bv("KST", 2048).rearrange("p (t o) -> p t o", t=2)
    QFB, KFB, VTB, KSTB = B("QFB"), B("KFB"), B("VTB"), B("KSTB")
    atm = bv("ATM", 768).rearrange("p (i c) -> p i c", i=2)
    ATMB = [B("ATM0"), B("ATM1")]
    g1f = bv("ATM", 256, 768)
    G1B = B("G1B")
    gate = fv("GATE", 2048).rearrange("p (c t) -> p c t", c=NCH)
    oln = bv("OF", 2048).rearrange("p (c t) -> p c t", c=NCH)
    OLNC = [B(f"OLNC{c}") for c in range(NCH)]
    scr = fv("SCR", 1024, 0)

    ps_open = [False]

    for s in range(NSUP):
        c0 = 1 + s * SUP
        sfb = [SFB[s]]

        ln_stats_sf(c0, SUP)
        bcast_rows(SUP)
        jwarm(3)
        for c in range(NCH):
            vop(lambda e, c=c, c0=c0: e.tensor_tensor(scr[:, 0:SUP],
                                                      sF[:, c, c0:c0 + SUP],
                                                      PC[:, 0:SUP], ALU.mult),
                reads=sfb + [PCB], writes=[SCRB])
            vop(lambda e, c=c: e.tensor_tensor(h1[:, c, :], scr[:, 0:SUP],
                                               PD[:, 0:SUP], ALU.add),
                reads=[SCRB, PDB], writes=[H1C[c]])

        if s == 0:
            tap("h1", h1, list(H1C))
        for oc in range(16):
            pb, pbuf = (PC, PCB) if oc % 2 == 0 else (PD, PDB)
            for kc in range(NCH):
                mm(pbuf, pb[:, 0:SUP], wqkv_b[:, kc, oc * 128:(oc + 1) * 128],
                   h1[:, kc, :], [WAB, H1C[kc]], start=(kc == 0), stop=(kc == NCH - 1))
            dst, dbuf = (qf, QFB) if oc < 8 else (kf, KFB)
            sop(lambda e, d=dst[:, oc % 8, :], p=pb: e.activation(d, p[:, 0:SUP],
                                                                  AF.Silu),
                reads=[pbuf], writes=[dbuf])
        for tt in range(2):
            pb, pbuf = (PC, PCB) if tt == 0 else (PD, PDB)
            for half in range(2):
                for kc in range(NCH):
                    mm(pbuf, pb[:, 0:512], h1[:, kc, tt * 128:(tt + 1) * 128],
                       wqkv_b[:, kc, 2048 + half * 512: 2048 + (half + 1) * 512],
                       [WAB, H1C[kc]], start=(kc == 0), stop=(kc == NCH - 1))
                sop(lambda e, t=tt, h=half, p=pb: e.activation(
                        vt[:, t, h * 512:(h + 1) * 512], p[:, 0:512], AF.Copy),
                    reads=[pbuf], writes=[VTB])
        for tt in range(2):
            for c in range(NCH):
                trans(PTB, PT[:, c * 128:(c + 1) * 128],
                      kf[:, c, tt * 128:(tt + 1) * 128], eye_b, [KFB, CONSTB],
                      start=(c == 0), stop=(c == NCH - 1))
            vop(lambda e, t=tt: e.tensor_copy(kst[:, t, :], PT[:]),
                reads=[PTB], writes=[KSTB])

        if s == 0:
            tap("qf", qf, [QFB])
            tap("kf", kf, [KFB])
            tap("vt", vt, [VTB])
            tap("kst", kst, [KSTB])
        for kc in range(NCH):
            mm(PEB, PE[0:64, 0:SUP], wg1_b[:, kc, :], h1[:, kc, :],
               [WGB, H1C[kc]], start=(kc == 0), stop=(kc == NCH - 1))
        sop(lambda e: e.activation(g1f[0:64, :], PE[0:64, 0:SUP], AF.Identity,
                                   bias=pvcol("b_g1", rows=64)),
            reads=[PEB, CONSTF], writes=[G1B])
        for dc in range(NCH):
            pb, pbuf = (PC, PCB) if dc % 2 == 0 else (PD, PDB)
            mm(pbuf, pb[:, 0:SUP], wg2_b[0:64, dc * 128:(dc + 1) * 128],
               g1f[0:64, :], [WGB, G1B], start=True, stop=True)
            sop(lambda e, d=dc, p=pb: e.activation(gate[:, d, :], p[:, 0:SUP],
                                                   AF.Sigmoid, bias=pvcol("b_g2", d)),
                reads=[pbuf, CONSTF], writes=[GATEB])

        if s == 0:
            tap("gate", gate, [GATEB])
        for pr in range(8):
            opb, opbuf = (PE, PEB) if pr % 2 == 0 else (PB, PBB)
            for hh in range(2):
                h = pr * 2 + hh
                jwarm(1)
                hc, hp = h // 2, (h % 2) * 64
                ab = ATMB[h % 2]
                mm(PAB0, PA[:, 0:256], kf[hp:hp + 64, hc, 0:128],
                   qf[hp:hp + 64, hc, 0:256], [KFB, QFB], start=True, stop=False)
                mm(PAB0, PA[:, 256:384], kf[hp:hp + 64, hc, 128:256],
                   qf[hp:hp + 64, hc, 128:256], [KFB, QFB], start=False, stop=True)
                vop(lambda e, i=h % 2: e.tensor_tensor(atm[:, i, :], PA[:, 0:384],
                                                       maskb_b, ALU.mult),
                    reads=[PAB0, CONSTB], writes=[ab])
                if s == 0 and h == 0:
                    tap("atm_h0", atm[:, 0, :], [ab])
                first_o = [True]
                def omm(lhs, rhs, rb, stop):
                    st = first_o[0]
                    first_o[0] = False
                    mm(opbuf, lhs[0], lhs[1], rhs, rb, start=st, stop=stop)
                if s > 0:
                    omm((opb[hp:hp + 64, 0:256],
                         sbf[hp:hp + 64, hc * 64:(hc + 1) * 64]),
                        qf[hp:hp + 64, hc, 0:256], [SBFB, QFB], False)
                omm((opb[hp:hp + 64, 0:256], vt[:, 0, h * 64:(h + 1) * 64]),
                    atm[:, h % 2, 0:256], [VTB, ab], False)
                omm((opb[hp:hp + 64, 128:256], vt[:, 1, h * 64:(h + 1) * 64]),
                    atm[:, h % 2, 256:384], [VTB, ab], stop=True)
                mm(PSB, PS[hp:hp + 64, hc * 64:(hc + 1) * 64],
                   kst[:, 0, h * 64:(h + 1) * 64], vt[:, 0, h * 64:(h + 1) * 64],
                   [KSTB, VTB], start=True, stop=False)
                mm(PSB, PS[hp:hp + 64, hc * 64:(hc + 1) * 64],
                   kst[:, 1, h * 64:(h + 1) * 64], vt[:, 1, h * 64:(h + 1) * 64],
                   [KSTB, VTB], start=False, stop=True)
            vop(lambda e, pr=pr, p=opb: e.tensor_tensor(gate[:, pr, :],
                                                        p[:, 0:256],
                                                        gate[:, pr, :], ALU.mult),
                reads=[opbuf, GATEB], writes=[GATEB])
        if s == 0:
            vop(lambda e: e.tensor_copy(s32[:], PS[:]), reads=[PSB], writes=[S32B])
        else:
            vop(lambda e: e.tensor_tensor(s32[:], PS[:], s32[:], ALU.add),
                reads=[PSB, S32B], writes=[S32B])
        if s < NSUP - 1:
            sop(lambda e: e.copy(sbf[:], s32[:]), reads=[S32B], writes=[SBFB])

        if s <= 1:
            tap(f"og{s}", gate, [GATEB])
            tap(f"sbf{s}", sbf, [SBFB])
        for c in range(NCH):
            mm(PEB, PE[0:1, 0:SUP], ones_f[:, 0:1], gate[:, c, :],
               [CONSTF, GATEB], start=(c == 0), stop=(c == NCH - 1))
        for c in range(NCH):
            sop(lambda e, c=c: e.activation(sq_scr[:, 0:SUP], gate[:, c, :],
                                            AF.Square),
                reads=[GATEB], writes=[SCRB])
            if c < 2:
                jwarm(1)
            mm(PBB, PB[0:1, 0:SUP], ones_f[:, 0:1], sq_scr[:, 0:SUP],
               [CONSTF, SCRB], start=(c == 0), stop=(c == NCH - 1))
        stat_finish(SUP)
        bcast_rows(SUP)
        jwarm(3)
        for c in range(NCH):
            vop(lambda e, c=c: e.tensor_tensor(scr[:, 0:SUP], gate[:, c, :],
                                               PC[:, 0:SUP], ALU.mult),
                reads=[GATEB, PCB], writes=[SCRB])
            vop(lambda e: e.tensor_tensor(scr[:, 256:256 + SUP], scr[:, 0:SUP],
                                          PD[:, 0:SUP], ALU.add),
                reads=[PDB, SCRB], writes=[SCRB])
            sop(lambda e, c=c: e.activation(oln[:, c, :], scr[:, 256:256 + SUP],
                                            AF.Identity, bias=pvcol("beta", c),
                                            scale=pvcol("gamma", c)),
                reads=[SCRB, CONSTF], writes=[OLNC[c]])

        if s == 0:
            tap("oln", oln, list(OLNC))
        jwarm(3)
        for dc in range(NCH):
            pb, pbuf = (PC, PCB) if dc % 2 == 0 else (PD, PDB)
            for kc in range(NCH):
                mm(pbuf, pb[:, 0:SUP], wout_b[:, kc, dc * 128:(dc + 1) * 128],
                   oln[:, kc, :], [WBB, OLNC[kc]], start=(kc == 0), stop=(kc == NCH - 1))
            vop(lambda e, d=dc, p=pb, c0=c0: e.tensor_tensor(
                    sF[:, d, c0:c0 + SUP], p[:, 0:SUP], sF[:, d, c0:c0 + SUP],
                    ALU.add),
                reads=[pbuf] + sfb, writes=sfb)
            dmaG("SPILL", x2s[dc * 128:(dc + 1) * 128, s * SUP:(s + 1) * SUP],
                 sF[:, dc, c0:c0 + SUP], reads=sfb, writes=[B("X2SB")])

    emit_attn_cast(0, extra=(WAB,))

    kb.barrier()
    tap_sf("x2")

    for g in range(4):
        c0 = 1 + g * 512
        gb = sf_bufs(c0, c0 + 512)
        ln_stats_sf(c0, 512)
        bcast_rows(512)
        for c in range(NCH):
            vop(lambda e, c=c, c0=c0: e.tensor_tensor(scr[:, 0:512],
                                                      sF[:, c, c0:c0 + 512],
                                                      PC[:, 0:512], ALU.mult),
                reads=gb + [PCB], writes=[SCRB])
            vop(lambda e, c=c, c0=c0: e.tensor_tensor(sF[:, c, c0:c0 + 512],
                                                      scr[:, 0:512],
                                                      PD[:, 0:512], ALU.add),
                reads=[SCRB, PDB], writes=gb)

    kb.barrier()
    tap_sf("y0")

    YBND = B("YBNDB")
    ybnd = fv("YBND", 32).rearrange("p (c k) -> p c k", c=NCH)

    w1v = [bv("WA", 2048, i * 2048).rearrange("p (c o) -> p c o", c=NCH)
           for i in range(2)]
    W1B = [B("W1B0"), B("W1B1")]
    w2v = [bv("WB", 1024, i * 1024) for i in range(8)]
    W2B = [B(f"W2B{i}") for i in range(8)]

    qfc = bv("QF", 2048).rearrange("p (b t) -> p b t", b=4)
    kfc = bv("KF", 4096).rearrange("p (w b t) -> p w b t", w=2, b=4)
    vtc = bv("VT", 4096).rearrange("p (w b t) -> p w b t", w=2, b=4)
    QFCB = B("QFCB")
    KFCB = [B("KFC0"), B("KFC1")]
    VTCB = [B("VTC0"), B("VTC1")]
    pa_junk = bv("KST", 2048)
    PAJB = B("PAJB")
    ptile8 = bv("ATM", 1536).rearrange("p (i c) -> p i c", i=12)
    PTL8 = [B(f"PTL{i}") for i in range(12)]
    _pabk = [B("PABK0"), B("PABK1")]
    PASL = [_pabk[i // 4] for i in range(8)]
    bcastB = fv("GATE", 2048, 0)
    BCB = None
    ofc = bv("OF", 2048).rearrange("p (b t) -> p b t", b=4)
    OFCB = B("OFCB")
    ybf = bv("YBF", 2064).rearrange("p (i c k) -> p i c k", i=2, c=NCH)
    YBFB = [B("YBF0"), B("YBF1")]
    tabw = fv("TAB", 1024).rearrange("p (i k) -> p i k", i=4)
    TABB = B("TABB")
    qscale_t = fv("SCR", 256, 1024).rearrange("p (i k) -> p i k", i=4)
    QSB = B("QSB")
    scrA = fv("GATE", 512, 0)
    scrB2 = fv("GATE", 512, 512)
    scrC = fv("GATE", 512, 1024)
    scrD = fv("SCR", 512, 0)
    def nm_v(par):
        return fv("SMALL", 8, par * 48 + 0)
    def dsP_v(par):
        return fv("SMALL", 8, par * 48 + 8)
    def dsC_v(par):
        return fv("SMALL", 8, par * 48 + 16)
    def lnd_v(par):
        return fv("SMALL", 8, par * 48 + 24)
    def nmld_v(par):
        return fv("SMALL", 8, par * 48 + 32)
    def tmp1_v(par):
        return fv("SMALL", 8, par * 48 + 40)
    rn_t = fv("SMALL", 8, 96)
    rn_k = fv("SMALL", 8, 104)
    RNB = [B("RNQ"), B("RNK")]
    WKB = B("WKB")
    QSC = B("QSCR")
    PAQB, PAKB = B("PAQB"), B("PAKB")
    PAJQ, PAJK = B("PAJQ"), B("PAJK")
    X2R = [B("X2R0"), B("X2R1")]

    hg_regions = []
    for i in range(20):
        hg_regions.append(bv("WA", 1024, 4096 + i * 1024))
    hg_regions.append(bv("QF", 1024, 0))
    hg_regions.append(bv("QF", 1024, 1024))
    HGB = B("HGB")
    ln4f = ABt[:, bplan["KF"][0]: bplan["KF"][0] + 8192].rearrange(
        "p (c t) -> p c t", c=NCH)
    LN4C = [B(f"LN4C{i}") for i in range(2 * NCH)]
    gel = fv("GATE", 1024, 0)

    def rotary_pair(l, w, tt, qp, qpb, kp, kpb, kfw, kb_cur):
        t1 = [fv("S32", 512, 0), fv("WST", 512, 0)]
        Ax = [fv("SCR", 512, 0), fv("WST", 512, 512)]
        Cx = [fv("SCR", 512, 512), fv("WST", 512, 1024)]
        Dx = [fv("S32", 512, 0), fv("WST", 512, 1536)]
        T1f = [QSC, WKB]
        DBf = [QSC, WKB]
        GBf = [SCRB, WKB]
        rn = [rn_t, rn_k]
        ps = [(qp, qpb), (kp, kpb)]
        pj = [pa_junk[:, 1024:1536], pa_junk[:, 1536:2048]]
        PJb = [PAJQ, PAJK]
        cs = [tabw[:, 0, tt * 64:(tt + 1) * 64], tabw[:, 2, tt * 64:(tt + 1) * 64]]
        sn = [tabw[:, 1, tt * 64:(tt + 1) * 64], tabw[:, 3, tt * 64:(tt + 1) * 64]]
        h3 = lambda a: a.rearrange("p (h d) -> p h d", h=HC)
        for i in (0, 1):
            sop(lambda e, i=i: e.activation(Ax[i], ps[i][0][:, 0:512], AF.Square),
                reads=[ps[i][1]], writes=[GBf[i]])
        for i in (0, 1):
            vop(lambda e, i=i: e.reduce_sum(rn[i][:, 0:8], h3(Ax[i]), AX.X),
                reads=[GBf[i]], writes=[RNB[i]])
        for i in (0, 1):
            sop(lambda e, i=i: e.activation(rn[i][:, 0:8], rn[i][:, 0:8], AF.Sqrt),
                reads=[RNB[i]], writes=[RNB[i]])
        for i in (0, 1):
            vop(lambda e, i=i: e.tensor_scalar_max(rn[i][:, 0:8], rn[i][:, 0:8],
                                                   1e-12),
                reads=[RNB[i]], writes=[RNB[i]])
            vop(lambda e, i=i: e.reciprocal(rn[i][:, 0:8], rn[i][:, 0:8]),
                reads=[RNB[i]], writes=[RNB[i]])
        for i in (0, 1):
            vop(lambda e, i=i: e.tensor_tensor(
                    h3(t1[i]), h3(ps[i][0][:, 0:512]),
                    rn[i][:, 0:8].unsqueeze(2).broadcast_to([128, HC, HD]),
                    ALU.mult),
                reads=[ps[i][1], RNB[i]], writes=[T1f[i]])
        for i in (0, 1):
            vop(lambda e, i=i: e.tensor_tensor(
                    h3(Ax[i]), h3(t1[i]),
                    cs[i].unsqueeze(1).broadcast_to([128, HC, HD]), ALU.mult),
                reads=[T1f[i], TABB], writes=[GBf[i]])
        for i in (0, 1):
            vop(lambda e, i=i: e.tensor_tensor(
                    h3(Cx[i])[:, :, 0:32], h3(t1[i])[:, :, 32:64],
                    sn[i][:, 0:32].unsqueeze(1).broadcast_to([128, HC, 32]),
                    ALU.mult),
                reads=[T1f[i], TABB], writes=[GBf[i]])
            vop(lambda e, i=i: e.tensor_tensor(
                    h3(Cx[i])[:, :, 32:64], h3(t1[i])[:, :, 0:32],
                    sn[i][:, 32:64].unsqueeze(1).broadcast_to([128, HC, 32]),
                    ALU.mult),
                reads=[T1f[i], TABB], writes=[GBf[i]])
        for i in (0, 1):
            vop(lambda e, i=i: e.tensor_sub(
                    h3(Dx[i])[:, :, 0:32], h3(Ax[i])[:, :, 0:32],
                    h3(Cx[i])[:, :, 0:32]),
                reads=[GBf[i]], writes=[DBf[i]])
            vop(lambda e, i=i: e.tensor_add(
                    h3(Dx[i])[:, :, 32:64], h3(Ax[i])[:, :, 32:64],
                    h3(Cx[i])[:, :, 32:64]),
                reads=[GBf[i]], writes=[DBf[i]])
        for i in (0, 1):
            sop(lambda e, i=i: e.copy(pj[i], Dx[i]),
                reads=[DBf[i]], writes=[PJb[i]])
        for i in (0, 1):
            for blk in range(4):
                trans(PTB, PT[:, i * 512 + blk * 128: i * 512 + (blk + 1) * 128],
                      pj[i][:, blk * 128:(blk + 1) * 128], eye_b,
                      [PJb[i], CONSTB],
                      start=(i == 0 and blk == 0), stop=(i == 1 and blk == 3))
        vop(lambda e, tt=tt: e.tensor_copy(
                qfc[:, :, tt * 128:(tt + 1) * 128],
                PT[:, 0:512].rearrange("p (b c) -> p b c", b=4)),
            reads=[PTB], writes=[QFCB])
        vop(lambda e, tt=tt, kfw=kfw: e.tensor_copy(
                kfw[:, :, tt * 128:(tt + 1) * 128],
                PT[:, 512:1024].rearrange("p (b c) -> p b c", b=4)),
            reads=[PTB], writes=[kb_cur])

    def stage_c_attn(l):
        if l > 0:
            emit_attn_cast(l)
        op("V", lambda e: e.nop(), writes=[WST[0], WST[1], WKB])
        for i in range(4):
            dmaG("QS", qscale_t[:, i, :], qscaleD[l, i], writes=[QSB])
        for w in range(1, NW):
            vop(lambda e, w=w: e.tensor_copy(ybnd[:, :, w:w + 1],
                                             sF[:, :, w * WIN:w * WIN + 1]),
                reads=sf_bufs(w * WIN, w * WIN + 1), writes=[YBND])
        jwarm(12)

        for w in range(NW):
            wsl = w % 2
            kfw, kfp = kfc[:, wsl], kfc[:, 1 - wsl]
            vtw, vtp = vtc[:, wsl], vtc[:, 1 - wsl]
            kb_cur, kb_prev = KFCB[wsl], KFCB[1 - wsl]
            vb_cur, vb_prev = VTCB[wsl], VTCB[1 - wsl]
            wsf = sf_bufs(1 + w * WIN, 1 + (w + 1) * WIN)

            dmaQ("TAB", tabw[:, :, :], tabsD[:, w].transpose([1, 0, 2]), writes=[TABB])
            dmaQ("TABL", fv("SCR", 1024, 0).rearrange("p (i k) -> p i k", i=4),
                 tabsloD[:, w].transpose([1, 0, 2]), writes=[SCRB])
            vop(lambda e: e.tensor_tensor(tabw[:, :, :], tabw[:, :, :],
                                          fv("SCR", 1024, 0).rearrange(
                                              "p (i k) -> p i k", i=4), ALU.add),
                reads=[SCRB, TABB], writes=[TABB])
            vop(lambda e: e.tensor_tensor(
                    tabw.rearrange("p i (a b) -> p i a b", a=TW),
                    tabw.rearrange("p i (a b) -> p i a b", a=TW),
                    qscale_t[:, :, :].unsqueeze(2).broadcast_to([128, 4, TW, HD]),
                    ALU.mult),
                reads=[QSB, TABB], writes=[TABB])

            def emit_ybf(tt):
                t0g = 1 + w * WIN + tt * 128
                ti = tt % 2
                if w > 0 and tt == 0:
                    op("G", lambda e, ti=ti, t0g=t0g: e.tensor_copy(
                           ybf[:, ti, :, 1:129], sF[:, :, t0g:t0g + 128]),
                       reads=wsf, writes=[YBFB[ti]])
                    op("G", lambda e, ti=ti, w=w: e.tensor_copy(
                           ybf[:, ti, :, 0:1], ybnd[:, :, w:w + 1]),
                       reads=[YBND], writes=[YBFB[ti]])
                else:
                    op("G", lambda e, ti=ti, t0g=t0g: e.tensor_copy(
                           ybf[:, ti, :, 0:129], sF[:, :, t0g - 1:t0g + 128]),
                       reads=sf_bufs(t0g - 1, t0g + 128), writes=[YBFB[ti]])

            slot_ctr = [0]

            def slot_ap(sl):
                if sl < 4:
                    return PC[:, sl * 128:(sl + 1) * 128], PCB
                return PD[:, (sl - 4) * 128:(sl - 3) * 128], PDB

            def kts_of(tt):
                return ([("p", j) for j in range(tt, TW)] if w > 0 else []) \
                    + [("c", j) for j in range(0, tt + 1)]

            def pass1(tt):
                par = tt % 2
                nm, dsP, dsC = nm_v(par), dsP_v(par), dsC_v(par)
                tmp1 = tmp1_v(par)
                wp = (TW - tt) * 128
                wc = (tt + 1) * 128
                for h in range(HC):
                    hp, hpr = (h % 2) * 64, h // 2
                    ppv = PB[:, 0:wp]
                    pcu = PE[:, 0:wc]
                    if w > 0:
                        op("T", lambda e, o=ppv, l1=qfc[hp:hp + 64, hpr, tt * 128:(tt + 1) * 128],
                           r=kfp[hp:hp + 64, hpr, tt * 128:512]:
                           e.matmul(o, l1, r, start=True, stop=True),
                           reads=[QFCB, kb_prev], writes=[PBB])
                        vop(lambda e, o=ppv: e.tensor_tensor(o[:, 0:128], o[:, 0:128],
                                                             ml_b, ALU.add),
                            reads=[CONSTB], writes=[PBB])
                    op("T", lambda e, o=pcu, l1=qfc[hp:hp + 64, hpr, tt * 128:(tt + 1) * 128],
                       r=kfw[hp:hp + 64, hpr, 0:wc]:
                       e.matmul(o, l1, r, start=True, stop=True),
                       reads=[QFCB, kb_cur], writes=[PEB])
                    vop(lambda e, o=pcu, wc=wc: e.tensor_tensor(
                            o[:, wc - 128:wc], o[:, wc - 128:wc], mh_b, ALU.add),
                        reads=[CONSTB], writes=[PEB])
                    if w > 0:
                        vop(lambda e, o=ppv: e.reduce_max(tmp1[:, 0:1], o, AX.X,
                                                          negate=True),
                            reads=[PBB], writes=[SMALLB])
                        vop(lambda e, o=pcu: e.reduce_max(tmp1[:, 1:2], o, AX.X,
                                                          negate=True),
                            reads=[PEB], writes=[SMALLB])
                        vop(lambda e, h=h: e.tensor_tensor(nm[:, h:h + 1],
                                                           tmp1[:, 0:1],
                                                           tmp1[:, 1:2], ALU.min),
                            reads=[SMALLB], writes=[SMALLB])
                        sop(lambda e, o=ppv, wp=wp, h=h: e.activation(
                                pa_junk[:, 0:wp], o, AF.Exp,
                                bias=nm[:, h:h + 1], accum_out=dsP[:, h:h + 1]),
                            reads=[PBB, SMALLB], writes=[PAJB, SMALLB])
                    else:
                        vop(lambda e, o=pcu, h=h: e.reduce_max(
                                nm[:, h:h + 1], o, AX.X, negate=True),
                            reads=[PEB], writes=[SMALLB])
                    sop(lambda e, o=pcu, wc=wc, h=h: e.activation(
                            pa_junk[:, 512:512 + wc], o, AF.Exp,
                            bias=nm[:, h:h + 1], accum_out=dsC[:, h:h + 1]),
                        reads=[PEB, SMALLB], writes=[PAJB, SMALLB])

            def nmphase(tt):
                par = tt % 2
                nm, dsP, dsC, lnd, nmld = (nm_v(par), dsP_v(par), dsC_v(par),
                                           lnd_v(par), nmld_v(par))
                bcB = bcastB[:, par * 1024:(par + 1) * 1024]
                if w > 0:
                    vop(lambda e: e.tensor_add(dsC[:, 0:8], dsC[:, 0:8], dsP[:, 0:8]),
                        reads=[SMALLB], writes=[SMALLB])
                sop(lambda e: e.activation(lnd[:, 0:8], dsC[:, 0:8], AF.Ln),
                    reads=[SMALLB], writes=[SMALLB])
                vop(lambda e: e.tensor_sub(nmld[:, 0:8], nm[:, 0:8], lnd[:, 0:8]),
                    reads=[SMALLB], writes=[SMALLB])
                for grp in range(2):
                    for hh in range(4):
                        h = grp * 4 + hh
                        trans(PBB, PB[0:1, hh * 128:(hh + 1) * 128],
                              nmld[:, h:h + 1], eye_f, [SMALLB, CONSTF],
                              start=(hh == 0), stop=(hh == 3))
                    vop(lambda e, g=grp: e.tensor_copy(
                            nmld_row[0:1, g * 512:(g + 1) * 512], PB[0:1, 0:512]),
                        reads=[PBB], writes=[STATB])
                for half in range(2):
                    mm(PBB, PB[:, 0:512], ones_f[0:1, 0:128],
                       nmld_row[0:1, half * 512:(half + 1) * 512],
                       [STATB, CONSTF], start=True, stop=True)
                    sop(lambda e, half=half, b=bcB: e.copy(
                            b[:, half * 512:(half + 1) * 512], PB[:, 0:512]),
                        reads=[PBB], writes=[GATEB])

            def pass2(tt):
                par = tt % 2
                bcB = bcastB[:, par * 1024:(par + 1) * 1024]
                nkts = (TW - tt if w > 0 else 0) + (tt + 1)

                def emit_at(h, sls):
                    hp, hpr = (h % 2) * 64, h // 2
                    for idx, (wk, j) in enumerate(kts_of(tt)):
                        kfx, kbx = (kfp, kb_prev) if wk == "p" else (kfw, kb_cur)
                        sl = slot_ctr[0] % 8
                        slot_ctr[0] += 1
                        sls.append((sl, wk, j))
                        pap, pbf = slot_ap(sl)
                        op("T", lambda e, o=pap, l1=kfx[hp:hp + 64, hpr, j * 128:(j + 1) * 128],
                           r=qfc[hp:hp + 64, hpr, tt * 128:(tt + 1) * 128]:
                           e.matmul(o, l1, r, start=True, stop=True),
                           reads=[kbx, QFCB], writes=[pbf])

                def slot_runs(sls):
                    runs = []
                    for sl, wk, j in sls:
                        if runs and runs[-1][0] + runs[-1][1] == sl and sl % 4 != 0:
                            runs[-1][1] += 1
                        else:
                            runs.append([sl, 1])
                    return runs

                def rest_a(h, sls):
                    for s0, ln_ in slot_runs(sls):
                        pbf = slot_ap(s0)[1]
                        wide = PC if s0 < 4 else PD
                        off = (s0 % 4) * 128
                        vop(lambda e, w_=wide, off=off, ln_=ln_, h=h, b=bcB:
                            e.tensor_tensor(
                                w_[:, off:off + ln_ * 128].rearrange(
                                    "p (a b) -> p a b", a=ln_),
                                w_[:, off:off + ln_ * 128].rearrange(
                                    "p (a b) -> p a b", a=ln_),
                                b[:, h * 128:(h + 1) * 128].unsqueeze(1)
                                .broadcast_to([128, ln_, 128]), ALU.add),
                            reads=[GATEB], writes=[pbf])
                    for sl, wk, j in sls:
                        pap, pbf = slot_ap(sl)
                        if wk == "p" and j == tt:
                            vop(lambda e, o=pap: e.tensor_tensor(o, o, mh_b, ALU.add),
                                reads=[CONSTB], writes=[pbf])
                        if wk == "c" and j == tt:
                            vop(lambda e, o=pap: e.tensor_tensor(o, o, ml_b, ALU.add),
                                reads=[CONSTB], writes=[pbf])
                    for s0, ln_ in slot_runs(sls):
                        pbf = slot_ap(s0)[1]
                        wide = PC if s0 < 4 else PD
                        off = (s0 % 4) * 128
                        pp = s0
                        sop(lambda e, w_=wide, off=off, ln_=ln_, pp=pp: e.activation(
                                ptile8[:, pp:pp + ln_, :].rearrange("p a b -> p (a b)"),
                                w_[:, off:off + ln_ * 128], AF.Exp),
                            reads=[pbf],
                            writes=[PTL8[s0 + k] for k in range(ln_)])
                def rest_b(h, sls):
                    hp = (h % 2) * 64
                    oc0 = (h // 2) * 128
                    for idx, (sl, wk, j) in enumerate(sls):
                        vtx, vbx = (vtp, vb_prev) if wk == "p" else (vtw, vb_cur)
                        op("T", lambda e, pp=sl, l1=vtx[:, j, h * 64:(h + 1) * 64],
                           hp=hp, oc0=oc0, st=(idx == 0), sp=(idx == nkts - 1):
                           e.matmul(PS[hp:hp + 64, oc0:oc0 + 128], l1,
                                    ptile8[:, pp, :], start=st, stop=sp),
                           reads=[vbx, PTL8[sl]], writes=[PSB])

                prev = None
                for h in range(HC):
                    sls = []
                    if prev is not None:
                        rest_a(prev[0], prev[1])
                    emit_at(h, sls)
                    if prev is not None:
                        rest_b(prev[0], prev[1])
                    prev = (h, sls)
                rest_a(prev[0], prev[1])
                rest_b(prev[0], prev[1])
                for pr in range(4):
                    vop(lambda e, pr=pr, tt=tt: e.tensor_copy(
                            ofc[:, pr, tt * 128:(tt + 1) * 128],
                            PS[:, pr * 128:(pr + 1) * 128]),
                        reads=[PSB], writes=[OFCB])

            emit_ybf(0)
            for tt in range(TW):
                ti = tt % 2
                qp, qpb = PA[:, 0:512], PAQB
                kp, kpb = PA[:, 512:1024], PAKB
                vp, vpb = PE[:, 0:512], PEB
                for kc in range(NCH):
                    lhs = ybf[:, ti, kc, 1:129] if kc < 4 else ybf[:, ti, kc, 0:128]
                    mm(qpb, qp[:, 0:512], lhs, fqkv_b[:, kc, 0:512],
                       [YBFB[ti], FQC[kc]], start=(kc == 0), stop=(kc == NCH - 1))
                    mm(kpb, kp[:, 0:512], lhs, fqkv_b[:, kc, 512:1024],
                       [YBFB[ti], FQC[kc]], start=(kc == 0), stop=(kc == NCH - 1))
                    mm(vpb, vp[:, 0:512], lhs, fqkv_b[:, kc, 1024:1536],
                       [YBFB[ti], FQC[kc]], start=(kc == 0), stop=(kc == NCH - 1))
                if tt + 1 < TW:
                    emit_ybf(tt + 1)
                sop(lambda e, tt=tt, vv=vtw, vp=vp: e.activation(vv[:, tt, :],
                                                                 vp[:, 0:512], AF.Copy),
                    reads=[vpb], writes=[vb_cur])
                rotary_pair(l, w, tt, qp, qpb, kp, kpb, kfw, kb_cur)

            pass1(0)
            nmphase(0)
            for tt in range(TW):
                if tt + 1 < TW:
                    pass1(tt + 1)
                pass2(tt)
                if tt + 1 < TW:
                    nmphase(tt + 1)

            c0 = 1 + w * WIN
            for dc in range(NCH):
                pb, pbuf = (PC, PCB) if dc % 2 == 0 else (PD, PDB)
                for kc in range(4):
                    mm(pbuf, pb[:, 0:WIN], fwo_b[:, kc, dc * 128:(dc + 1) * 128],
                       ofc[:, kc, :], [FOC[kc], OFCB], start=(kc == 0), stop=(kc == 3))
                vop(lambda e, d=dc, p=pb, c0=c0: e.tensor_tensor(
                        sF[:, d, c0:c0 + WIN], p[:, 0:WIN],
                        sF[:, d, c0:c0 + WIN], ALU.add),
                    reads=[pbuf] + wsf, writes=wsf)
            if l == 0:
                tap(f"vt0_w{w}", vtc[:, 0], [VTCB[0]])
                tap(f"kf0_w{w}", kfc[:, 0], [KFCB[0]])

    fw1r = [fw1[l].rearrange("(c p) o -> p c o", p=128) for l in range(2)]

    def load_w1(l, osn, slot):
        o0, ow = OSR[osn]
        for part, off in ((0, o0), (1, HID + o0)):
            i = _wi[0] % 2
            _wi[0] += 1
            stg = fv("WST", 1024, 1024 * i).rearrange(
                "p (c o) -> p c o", c=NCH)[:, :, 0:ow]
            dmaQ(f"WST{i}", stg, fw1r[l][:, :, off:off + ow], writes=[WST[i]])
            eng = "G" if part == 0 else "V"
            op(eng, lambda e, s=slot, ow=ow, g=stg, part=part: e.tensor_copy(
                   w1v[s][:, :, part * 128:part * 128 + ow], g),
               reads=[WST[i]], writes=[W1B[slot]])

    def load_w2(l, osn, slot):
        o0, ow = OSR[osn]
        i = _wi[0] % 2
        _wi[0] += 1
        stg = fv("WST", 1024, 1024 * i)[0:ow, :]
        dmaQ(f"WST{i}", stg, fw2[l, o0:o0 + ow, :], writes=[WST[i]])
        eng = "G" if slot % 2 == 0 else "V"
        op(eng, lambda e, s=slot, ow=ow, g=stg: e.tensor_copy(w2v[s][0:ow, :], g),
           reads=[WST[i]], writes=[W2B[slot]])

    def stage_c_mlp(l, last):
        vop(lambda e: e.tensor_copy(ybnd[:, :, 0:1], sF[:, :, N // 2:N // 2 + 1]),
            reads=sf_bufs(N // 2, N // 2 + 1), writes=[YBND])

        def emit_ln4_stats(half, g):
            ln_stats_sf(1 + half * (N // 2) + g * 512, 512, shifted=True)

        def emit_ln4_apply(half, g):
            g0 = 1 + half * (N // 2) + g * 512
            bcast_rows(512)
            for c in range(NCH):
                jwarm(2)
                off = -1 if c >= 4 else 0
                ap = sF[:, c, g0 + off: g0 + off + 512]
                rb = sf_bufs(g0 + off, g0 + off + 512)
                vop(lambda e, a=ap: e.tensor_tensor(scr[:, 0:512], a,
                                                    PC[:, 0:512], ALU.mult),
                    reads=rb + [PCB], writes=[SCRB])
                vop(lambda e: e.tensor_tensor(scr[:, 512:1024], scr[:, 0:512],
                                              PD[:, 0:512], ALU.add),
                    reads=[SCRB, PDB], writes=[SCRB])
                sop(lambda e, c=c, g=g: e.activation(
                        ln4f[:, c, g * 512:(g + 1) * 512], scr[:, 512:1024],
                        AF.Identity, bias=pvcol(f"ln_b{l}", c),
                        scale=pvcol(f"ln_g{l}", c)),
                    reads=[SCRB, CONSTF], writes=[LN4C[g * NCH + c]])

        def w1_phase(tgs):
            for osn in range(NOS):
                slot = osn % 2
                o0, ow = OSR[osn]
                for ti, tg in enumerate(tgs):
                    par = (osn * 2 + ti) % 2
                    pa_, pab = (PC, PCB) if par == 0 else (PE, PEB)
                    pg_, pgb = (PD, PDB) if par == 0 else (PB, PBB)
                    for kc in range(NCH):
                        mm(pab, pa_[0:ow, 0:512], w1v[slot][:, kc, 0:ow],
                           ln4f[:, kc, tg * 512:(tg + 1) * 512],
                           [W1B[slot], LN4C[tg * NCH + kc]],
                           start=(kc == 0), stop=(kc == NCH - 1))
                        mm(pgb, pg_[0:ow, 0:512], w1v[slot][:, kc, 128:128 + ow],
                           ln4f[:, kc, tg * 512:(tg + 1) * 512],
                           [W1B[slot], LN4C[tg * NCH + kc]],
                           start=(kc == 0), stop=(kc == NCH - 1))
                    gslot = gel[0:ow, tg * 512:(tg + 1) * 512]
                    sop(lambda e, g=gslot, ow=ow, osn=osn, p=pg_: e.activation(
                            g, p[0:ow, 0:512], AF.Gelu,
                            bias=pvcol(f"b1g_{l}", osn, rows=ow)),
                        reads=[pgb, CONSTF], writes=[GATEB])
                    vop(lambda e, g=gslot, ow=ow, osn=osn, tg=tg, p=pa_:
                        e.scalar_tensor_tensor(
                            hg_regions[osn][0:ow, tg * 512:(tg + 1) * 512],
                            p[0:ow, 0:512], pvcol(f"b1a_{l}", osn, rows=ow),
                            g, ALU.add, ALU.mult),
                        reads=[pab, GATEB, CONSTF], writes=[HGB])
                if osn + 2 < NOS:
                    load_w1(l, osn + 2, slot)

        groups = [list(range(g, min(g + 4, NOS))) for g in range(0, NOS, 4)]

        def w2_phase(half, interleave):
            hb0 = 1 + half * (N // 2)
            for k, osn in enumerate(groups[0]):
                load_w2(l, osn, k)
            for gi, grp in enumerate(groups):
                if interleave and gi == 4:
                    emit_ln4_stats(1, 1)
                for tg in range(2):
                    g0 = hb0 + tg * 512
                    gsf = sf_bufs(g0, g0 + 512)
                    for dc in range(NCH):
                        pb, pbuf = [(PC, PCB), (PD, PDB), (PE, PEB),
                                    (PB, PBB)][dc % 4]
                        for k, osn in enumerate(grp):
                            o0, ow = OSR[osn]
                            sl = (gi % 2) * 4 + k
                            mm(pbuf, pb[:, 0:512],
                               w2v[sl][0:ow, dc * 128:(dc + 1) * 128],
                               hg_regions[osn][0:ow, tg * 512:(tg + 1) * 512],
                               [W2B[sl], HGB], start=(k == 0),
                               stop=(k == len(grp) - 1))
                        if gi == 0:
                            vop(lambda e, d=dc, p=pb, g0=g0: e.scalar_tensor_tensor(
                                    sF[:, d, g0:g0 + 512], p[:, 0:512],
                                    pvcol(f"b2_{l}", d), sF[:, d, g0:g0 + 512],
                                    ALU.add, ALU.add),
                                reads=[pbuf, CONSTF] + gsf, writes=gsf)
                        else:
                            vop(lambda e, d=dc, p=pb, g0=g0: e.tensor_tensor(
                                    sF[:, d, g0:g0 + 512], p[:, 0:512],
                                    sF[:, d, g0:g0 + 512], ALU.add),
                                reads=[pbuf] + gsf, writes=gsf)
                        if last and gi == len(groups) - 1:
                            xsl = dc % 2
                            dmaG(f"X2R{xsl}", fv("TAB", 512, xsl * 512),
                                 x2s[dc * 128:(dc + 1) * 128, g0 - 1:g0 - 1 + 512],
                                 reads=[B("X2SB")], writes=[X2R[xsl]])
                            vop(lambda e, d=dc, g0=g0, xsl=xsl: e.tensor_tensor(
                                    sF[:, d, g0:g0 + 512], sF[:, d, g0:g0 + 512],
                                    fv("TAB", 512, xsl * 512), ALU.add),
                                reads=[X2R[xsl]] + gsf, writes=gsf)
                            dmaG("OUT", outT[dc * 128:(dc + 1) * 128, g0 - 1:g0 - 1 + 512],
                                 sF[:, dc, g0:g0 + 512], reads=gsf)
                    if tg == 0 and gi + 1 < len(groups):
                        for k, osn in enumerate(groups[gi + 1]):
                            load_w2(l, osn, ((gi + 1) % 2) * 4 + k)

        load_w1(l, 0, 0)
        load_w1(l, 1, 1)
        for g in range(2):
            emit_ln4_stats(0, g)
            emit_ln4_apply(0, g)
        w1_phase((0, 1))
        w2_phase(0, interleave=True)
        emit_ln4_apply(1, 1)
        load_w1(l, 0, 0)
        load_w1(l, 1, 1)
        vop(lambda e: e.tensor_copy(ybnd[:, :, 1:2],
                                    sF[:, :, N // 2:N // 2 + 1]),
            reads=sf_bufs(N // 2, N // 2 + 1), writes=[YBND])
        vop(lambda e: e.tensor_copy(sF[:, :, N // 2:N // 2 + 1],
                                    ybnd[:, :, 0:1]),
            reads=[YBND], writes=sf_bufs(N // 2, N // 2 + 1))
        emit_ln4_stats(1, 0)
        emit_ln4_apply(1, 0)
        vop(lambda e: e.tensor_copy(sF[:, :, N // 2:N // 2 + 1],
                                    ybnd[:, :, 1:2]),
            reads=[YBND], writes=sf_bufs(N // 2, N // 2 + 1))
        w1_phase((1, 0))
        w2_phase(1, interleave=False)

    for l in range(2):
        stage_c_attn(l)
        kb.barrier()
        if l == 0:
            tap_sf("ya0")
            tap("qfc0", qfc, [QFCB])
            tap("kfc0", kfc, [KFCB[0], KFCB[1]])
            tap("vtc0", vtc, [VTCB[0], VTCB[1]])
            tap("ofc0", ofc, [OFCB])
        stage_c_mlp(l, last=(l == 1))
        kb.barrier()
        if l == 0:
            tap_sf("ym0")

    kb.finalize_blocks()
    kb.tap_t = tap_t
    return kb


_BUILD_CACHE = {}

def _get_kb():
    if "kb" not in _BUILD_CACHE:
        _BUILD_CACHE["kb"] = build_program()
    return _BUILD_CACHE["kb"]


def prep_common(inputs):
    consts = _host_constants()
    pv = _pack_pvec(inputs)
    qs = np.asarray(inputs["L0_ffn_q_scale"], np.float32)
    ks = np.asarray(inputs["L0_ffn_k_scale"], np.float32)
    qsc = np.zeros((2, 4, 128, HD), np.float32)
    for l in range(2):
        qsc[l, 0] = np.broadcast_to(qs[l], (128, HD))
        qsc[l, 1] = np.broadcast_to(np.concatenate([qs[l, 32:], qs[l, :32]]), (128, HD))
        qsc[l, 2] = np.broadcast_to(ks[l], (128, HD))
        qsc[l, 3] = np.broadcast_to(np.concatenate([ks[l, 32:], ks[l, :32]]), (128, HD))

    return dict(
        wqkvT=np.ascontiguousarray(np.asarray(inputs["L0_attn_w_qkv"], np.float32).T),
        wg1T=np.ascontiguousarray(np.asarray(inputs["L0_attn_w_g1"], np.float32).T),
        wg2T=np.ascontiguousarray(np.asarray(inputs["L0_attn_w_g2"], np.float32).T),
        woutT=np.ascontiguousarray(np.asarray(inputs["L0_attn_w_out"], np.float32).T),
        fqkv=np.ascontiguousarray(np.asarray(inputs["L0_ffn_wqkv"], np.float32)),
        fwo=np.ascontiguousarray(np.asarray(inputs["L0_ffn_wo"], np.float32)),
        fw1=np.ascontiguousarray(np.asarray(inputs["L0_ffn_w1"], np.float32)),
        fw2=np.ascontiguousarray(np.asarray(inputs["L0_ffn_w2"], np.float32)),
        qscale=qsc, pvec=pv,
        tabs=consts["tabs"], tabslo=consts["tabslo"], ml=consts["ml"], mh=consts["mh"],
        maskb=consts["maskb"], eye_f=consts["eye_f"], eye_b=consts["eye_b"],
        ones_f=consts["ones_f"], ones_b=consts["ones_b"],
    )


def kernel(**inputs):
    common = prep_common(inputs)
    x = np.asarray(inputs["L0_attn_x"], np.float32)
    kb = _get_kb()
    in_maps = []
    for b in range(8):
        m = dict(common)
        m["xT"] = np.ascontiguousarray(x[b].T)
        in_maps.append(m)
    res = run_bass_kernel_spmd(kb.nc, in_maps, core_ids=list(range(8)))
    out = np.stack([np.ascontiguousarray(r["outT"].T) for r in res.results])
    return out.astype(np.float32)


# version 16 (final)
# speedup vs baseline: 1.0899x; 1.0007x over previous
import numpy as np
import ml_dtypes
import concourse.bass as bass
import concourse.mybir as mybir
from concourse.bass_utils import run_bass_kernel_spmd

F32 = mybir.dt.float32
BF16 = mybir.dt.bfloat16
AF = mybir.ActivationFunctionType
ALU = mybir.AluOpType
AX = mybir.AxisListType

D = 1024
NCH = 8
N = 2048
SUP = 256
NSUP = N // SUP
HB = 16
HD = 64
HC = 8
WIN = 512
NW = N // WIN
TW = WIN // 128
HID = 2730
FFI = 1536
NEG = -1.0e30
OS_SIZES = [128] * (HID // 128) + ([HID % 128] if HID % 128 else [])
NOS = len(OS_SIZES)

def _os_ranges():
    o0, out = 0, []
    for w in OS_SIZES:
        out.append((o0, w))
        o0 += w
    return out

OSR = _os_ranges()

PVIDX = {}
_pc = 0
def _pvadd(name, n):
    global _pc
    PVIDX[name] = _pc
    _pc += n
_pvadd("eps", 1)
_pvadd("b_g1", 1)
_pvadd("b_g2", 8)
_pvadd("gamma", 8)
_pvadd("beta", 8)
for _l in range(2):
    _pvadd(f"ln_g{_l}", 8)
    _pvadd(f"ln_b{_l}", 8)
    _pvadd(f"b2_{_l}", 8)
for _l in range(2):
    _pvadd(f"b1a_{_l}", NOS)
    _pvadd(f"b1g_{_l}", NOS)
PV = _pc


def _bf(x):
    return np.ascontiguousarray(np.asarray(x)).astype(ml_dtypes.bfloat16)


def _host_constants():
    half_idx = np.arange(0, HD, 2, dtype=np.float64)
    inv_freq = 1.0 / 10000.0 ** (half_idx / HD)
    t = np.arange(N, dtype=np.float64)
    freqs = t[:, None] * inv_freq[None, :]
    freqs = np.concatenate([freqs, freqs], axis=-1)
    base = (half_idx + 0.4 * HD) / (1.4 * HD)
    power = (t - N // 2) / float(WIN // 2)
    scale = base[None, :] ** power[:, None]
    scale = np.concatenate([scale, scale], axis=-1)
    cs, sn = np.cos(freqs), np.sin(freqs)
    t64 = [cs * scale * 8.0, sn * scale * 8.0, cs / scale, sn / scale]

    def wtile(a):
        return np.ascontiguousarray(
            a.reshape(NW, TW, 128, HD).transpose(0, 2, 1, 3).reshape(NW, 128, TW * HD))

    hi = [a.astype(np.float32) for a in t64]
    lo = [(a - h.astype(np.float64)).astype(np.float32) for a, h in zip(t64, hi)]
    tabs = np.stack([wtile(h) for h in hi])
    tabslo = np.stack([wtile(x) for x in lo])
    ii = np.arange(128)
    ml = np.where(ii[None, :] < ii[:, None], NEG, 0.0).astype(np.float32)
    mh = np.where(ii[None, :] > ii[:, None], NEG, 0.0).astype(np.float32)
    triu_keep = (ii[:, None] <= ii[None, :]).astype(np.float32)
    maskb = np.concatenate([triu_keep, np.ones((128, 128), np.float32), triu_keep], 1)
    eye = np.eye(128, dtype=np.float32)
    ones = np.ones((128, 128), np.float32)
    return dict(tabs=tabs, tabslo=tabslo, ml=_bf(ml), mh=_bf(mh), maskb=_bf(maskb),
                eye_f=eye, eye_b=_bf(eye), ones_f=ones, ones_b=_bf(ones))


def _pack_pvec(ins):
    cols = np.zeros((128, PV), np.float32)
    def put(name, k, v):
        cols[: v.shape[0], PVIDX[name] + k] = v
    put("eps", 0, np.full((128,), 1e-5, np.float32))
    put("b_g1", 0, np.asarray(ins["L0_attn_b_g1"], np.float32))
    for c in range(NCH):
        put("b_g2", c, np.asarray(ins["L0_attn_b_g2"], np.float32)[c * 128:(c + 1) * 128])
        put("gamma", c, np.asarray(ins["L0_attn_gamma"], np.float32)[c * 128:(c + 1) * 128])
        put("beta", c, np.asarray(ins["L0_attn_beta"], np.float32)[c * 128:(c + 1) * 128])
    for l in range(2):
        for c in range(NCH):
            put(f"ln_g{l}", c, np.asarray(ins["L0_ffn_ln_g"], np.float32)[l, c * 128:(c + 1) * 128])
            put(f"ln_b{l}", c, np.asarray(ins["L0_ffn_ln_b"], np.float32)[l, c * 128:(c + 1) * 128])
            put(f"b2_{l}", c, np.asarray(ins["L0_ffn_b2"], np.float32)[l, c * 128:(c + 1) * 128])
        b1 = np.asarray(ins["L0_ffn_b1"], np.float32)[l]
        for k, (o0, ow) in enumerate(OSR):
            put(f"b1a_{l}", k, b1[o0:o0 + ow])
            put(f"b1g_{l}", k, b1[HID + o0:HID + o0 + ow])
    return cols



class Buf:
    __slots__ = ("w", "rs", "name")

    def __init__(self, name=""):
        self.w = {}
        self.rs = {}
        self.name = name

    def toks_w(self):
        return list(self.w.items())

    def toks_rw(self):
        out = dict(self.w)
        for s, v in self.rs.items():
            out[s] = max(out.get(s, 0), v)
        return list(out.items())

    def add_read(self, tok):
        s, v = tok
        self.rs[s] = max(self.rs.get(s, 0), v)

    def set_write(self, tok):
        s, v = tok
        self.w = {s: v}
        self.rs = {}


class KB:
    ENGS = ("T", "V", "S", "Q", "G")

    def __init__(self):
        self.nc = bass.Bass("TRN2", target_bir_lowering=False)
        self.prog = {e: [] for e in self.ENGS}
        self.cnt = {"T": 0, "V": 0, "S": 0, "Q": 0, "G": 0}
        self.wm = {e: {} for e in self.ENGS}
        self.n_wait = 0
        self.n_inst = 0

    def build_resources(self):
        import contextlib
        self.stack = contextlib.ExitStack()
        ent = self.stack.enter_context
        self.sem = {k: ent(self.nc.semaphore(f"sem_{k}"))
                    for k in ("T", "V", "S", "Q", "G")}
        return ent

    def dma_sem(self, name):
        if name not in self.sem:
            self.sem[name] = self.stack.enter_context(
                self.nc.semaphore(f"sem_{name}"))
            self.cnt[name] = 0
        return self.sem[name]

    def op(self, eng, fn, reads=(), writes=(), deps=(), dma=False):
        need = {}
        for b in reads:
            for s, v in b.toks_w():
                need[s] = max(need.get(s, 0), v)
        for b in writes:
            for s, v in b.toks_rw():
                need[s] = max(need.get(s, 0), v)
        for s, v in deps:
            need[s] = max(need.get(s, 0), v)
        wmm = self.wm[eng]
        waits = []
        for s, v in sorted(need.items()):
            if wmm.get(s, 0) < v:
                waits.append((s, v))
                wmm[s] = v
        if dma:
            kind = dma
            self.dma_sem(kind)
            inc = 16
        else:
            kind, inc = eng, 1
        self.cnt[kind] += inc
        tok = (kind, self.cnt[kind])
        self.n_wait += len(waits)
        self.n_inst += 1
        self.prog[eng].append((waits, fn, kind, inc))
        for b in reads:
            b.add_read(tok)
        for b in writes:
            b.set_write(tok)
        return tok

    def snapshot(self):
        return [(k, v) for k, v in self.cnt.items() if v > 0]

    def barrier(self):
        toks = self.snapshot()
        for e in self.ENGS:
            self.op(e, lambda en: en.nop(), deps=toks)

    def finalize_blocks(self):
        sem, prog, cnt = self.sem, self.prog, self.cnt
        engmap = {"T": "tensor", "V": "vector", "S": "scalar", "Q": "sync",
                  "G": "gpsimd"}
        with self.nc.Block() as block:
            for e in self.ENGS:
                def mk(e):
                    def body(eng):
                        for waits, fn, kind, inc in prog[e]:
                            for s, v in waits:
                                eng.wait_ge(sem[s], v)
                            fn(eng).then_inc(sem[kind], inc)
                        if e == "Q":
                            for k, v in cnt.items():
                                if v and k != "Q":
                                    eng.wait_ge(sem[k], v)
                    return body
                getattr(block, engmap[e])(mk(e))



def build_program(debug=False):
    kb = KB()
    nc = kb.nc
    ent = kb.build_resources()
    tap_t = {}

    def din(name, shape, dt=F32):
        return nc.dram_tensor(name, list(shape), dt, kind="ExternalInput").ap()

    xT = din("xT", [D, N])
    wqkvT = din("wqkvT", [D, 3 * D])
    wg1T = din("wg1T", [D, HD])
    wg2T = din("wg2T", [HD, D])
    woutT = din("woutT", [D, D])
    fqkv = din("fqkv", [2, D, FFI])
    fwo = din("fwo", [2, 512, D])
    fw1 = din("fw1", [2, D, 2 * HID])
    fw2 = din("fw2", [2, HID, D])
    qscaleD = din("qscale", [2, 4, 128, HD])
    pvecD = din("pvec", [128, PV])
    tabsD = din("tabs", [4, NW, 128, TW * HD])
    tabsloD = din("tabslo", [4, NW, 128, TW * HD])
    mlD = din("ml", [128, 128], BF16)
    mhD = din("mh", [128, 128], BF16)
    maskbD = din("maskb", [128, 384], BF16)
    eyefD = din("eye_f", [128, 128])
    eyebD = din("eye_b", [128, 128], BF16)
    onesfD = din("ones_f", [128, 128])
    onesbD = din("ones_b", [128, 128], BF16)

    outT = nc.dram_tensor("outT", [D, N], F32, kind="ExternalOutput").ap()
    x2s = nc.dram_tensor("x2s", [D, N], F32).ap()

    sF = ent(nc.sbuf_tensor("sF", [128, NCH, N + 1], F32))

    fplan, bplan = {}, {}
    _fc = [0]
    def fslot(name, cols):
        fplan[name] = (_fc[0], cols)
        _fc[0] += cols
    fslot("STAT", 1024)
    fslot("SMALL", 128)
    fslot("GATE", 2048)
    fslot("SCR", 1536)
    fslot("WST", 2048)
    fslot("TAB", 1024)
    fslot("CONST", 448)
    fslot("YBND", 32)
    fslot("S32", 512)
    AFt = ent(nc.sbuf_tensor("AFa", [128, _fc[0]], F32))

    _bc = [0]
    def bslot(name, cols):
        bplan[name] = (_bc[0], cols)
        _bc[0] += cols
    bslot("YBF", 2064)
    bslot("QF", 2048)
    bslot("KF", 4096)
    bslot("VT", 4096)
    bslot("KST", 2048)
    bslot("ATM", 1536)
    bslot("OF", 2048)
    bslot("SBF", 512)
    bslot("WA", 24576)
    bslot("WB", 8192)
    bslot("WG", 1536)
    bslot("CONB", 896)
    ABt = ent(nc.sbuf_tensor("ABa", [128, _bc[0]], BF16))

    total_bytes = (NCH * (N + 1)) * 4 + _fc[0] * 4 + _bc[0] * 2
    assert total_bytes <= 209500, total_bytes

    PA = ent(nc.psum_tensor("PA", [128, 1024], F32))
    PC = ent(nc.psum_tensor("PC", [128, 512], F32))
    PD = ent(nc.psum_tensor("PD", [128, 512], F32))
    PE = ent(nc.psum_tensor("PE", [128, 512], F32))
    PB = ent(nc.psum_tensor("PB", [128, 512], F32))
    PS = ent(nc.psum_tensor("PS", [128, 512], F32))
    PT = ent(nc.psum_tensor("PT", [128, 1024], BF16))

    def fv(name, cols=None, off=0):
        o, c = fplan[name]
        if cols is None:
            cols = c - off
        assert off >= 0 and off + cols <= c, (name, off, cols)
        return AFt[:, o + off:o + off + cols]

    def bv(name, cols=None, off=0):
        o, c = bplan[name]
        if cols is None:
            cols = c - off
        assert off >= 0 and off + cols <= c, (name, off, cols)
        return ABt[:, o + off:o + off + cols]

    bufs = {}
    def B(name):
        if name not in bufs:
            bufs[name] = Buf(name)
        return bufs[name]

    SFB = [B(f"sF{s}") for s in range(NSUP)]

    def sf_bufs(c0, c1):
        lo = max(0, (c0 - 1)) // SUP
        hi = min(NSUP - 1, max(0, (c1 - 2)) // SUP)
        return [SFB[i] for i in range(lo, hi + 1)]

    PAB0, PAB1 = B("PA0"), B("PA1")
    PCB, PDB, PEB, PBB, PSB, PTB = (B(x) for x in ("PCp", "PDp", "PEp", "PBp", "PSp", "PTp"))

    op = kb.op

    def mm(pbuf, out_ap, lhsT_ap, rhs_ap, rbufs, start, stop, deps=()):
        return op("T", lambda e, o=out_ap, l=lhsT_ap, r=rhs_ap, st=start, sp=stop:
                  e.matmul(o, l, r, start=st, stop=sp),
                  reads=list(rbufs), writes=[pbuf], deps=deps)

    def trans(pbuf, out_ap, in_ap, ident_ap, rbufs, start=True, stop=True):
        return op("T", lambda e, o=out_ap, i=in_ap, d=ident_ap, st=start, sp=stop:
                  e.matmul(o, i, d, start=st, stop=sp, is_transpose=True),
                  reads=list(rbufs), writes=[pbuf])

    def vop(fn, reads=(), writes=(), deps=()):
        return op("V", fn, reads=reads, writes=writes, deps=deps)

    def sop(fn, reads=(), writes=(), deps=()):
        return op("S", fn, reads=reads, writes=writes, deps=deps)

    def dmaQ(sem, out_ap, in_ap, reads=(), writes=()):
        return op("Q", lambda e, o=out_ap, i=in_ap: e.dma_start(out=o, in_=i),
                  reads=reads, writes=writes, dma=sem)

    def dmaG(sem, out_ap, in_ap, reads=(), writes=()):
        return op("G", lambda e, o=out_ap, i=in_ap: e.dma_start(out=o, in_=i),
                  reads=reads, writes=writes, dma=sem)

    def jwarm(n):
        for _ in range(n):
            op("T", lambda e: e.matmul(PA[:, 512:1024], eye_b, warm_rhs,
                                       start=True, stop=True))

    CONSTF, CONSTB, INB = B("CONSTF"), B("CONSTB"), B("INCHAIN")
    eye_f = fv("CONST", 128, 0)
    ones_f = fv("CONST", 128, 128)
    pvecS = fv("CONST", PV, 256)
    eye_b = bv("CONB", 128, 0)
    ones_b = bv("CONB", 128, 128)
    ml_b = bv("CONB", 128, 256)
    mh_b = bv("CONB", 128, 384)
    maskb_b = bv("CONB", 384, 512)
    dmaG("IN", eye_f, eyefD, writes=[CONSTF, INB])
    dmaG("IN", ones_f, onesfD, writes=[CONSTF, INB])
    dmaG("IN", pvecS, pvecD, writes=[CONSTF, INB])
    dmaG("IN", eye_b, eyebD, writes=[CONSTB, INB])
    dmaG("IN", ones_b, onesbD, writes=[CONSTB, INB])
    dmaG("IN", ml_b, mlD, writes=[CONSTB, INB])
    dmaG("IN", mh_b, mhD, writes=[CONSTB, INB])
    dmaG("IN", maskb_b, maskbD, writes=[CONSTB, INB])

    def pvcol(name, k=0, rows=128):
        return pvecS[0:rows, PVIDX[name] + k: PVIDX[name] + k + 1]

    def tap(name, ap, rbufs, rows=128):
        if not debug:
            return
        shp = [rows] + [int(d) for d in ap.shape[1:]]
        t = nc.dram_tensor("tap_" + name, shp, ap.dtype, kind="ExternalOutput").ap()
        tap_t[name] = shp
        dmaG("TAPS", t, ap, reads=list(rbufs))

    def tap_sf(name):
        if not debug:
            return
        t = nc.dram_tensor("tap_" + name, [D, N], F32, kind="ExternalOutput").ap()
        tap_t[name] = (D, N)
        for c in range(NCH):
            dmaG("TAPS", t[c * 128:(c + 1) * 128, :], sF[:, c, 1:N + 1],
                 reads=list(SFB))

    warm_rhs = bv("CONB", 512, 256)
    for i in range(24):
        op("T", lambda e: e.matmul(PB[:, 0:512], eye_b, warm_rhs,
                                   start=True, stop=True),
           reads=[CONSTB], writes=[PBB])

    for c in range(NCH):
        dmafn = dmaG if c % 2 else dmaQ
        dmafn("IN" if c % 2 else "IN2", sF[:, c, 1:N + 1],
              xT[c * 128:(c + 1) * 128, :], writes=list(SFB) + [INB])
    vop(lambda e: e.memset(sF[:, :, 0:1], 0.0), writes=[SFB[0]])

    WST = [B("WST0"), B("WST1")]
    _wi = [0]
    _rr = [0]
    RR_ENGS = ("G", "V", "S")

    def cast_w(dst_ap, dst_buf, src_ap, rows, cols, eng="S", extra_w=()):
        done = 0
        while done < cols:
            w = min(1024, cols - done)
            i = _wi[0] % 2
            _wi[0] += 1
            stg = fv("WST", w, 1024 * i)[0:rows, :]
            dmaQ(f"WST{i}", stg, src_ap[:, done:done + w], writes=[WST[i]])
            dpc = dst_ap[0:rows, done:done + w]
            ee = eng
            if ee == "RR":
                ee = RR_ENGS[_rr[0] % 3]
                _rr[0] += 1
            wl = [dst_buf] + list(extra_w)
            if ee == "S":
                sop(lambda e, d=dpc, g=stg: e.copy(d, g),
                    reads=[WST[i]], writes=wl)
            elif ee == "G":
                op("G", lambda e, d=dpc, g=stg: e.tensor_copy(d, g),
                   reads=[WST[i]], writes=wl)
            else:
                vop(lambda e, d=dpc, g=stg: e.tensor_copy(d, g),
                    reads=[WST[i]], writes=wl)
            done += w

    fqkv_b = bv("WA", 12288).rearrange("p (c o) -> p c o", c=NCH)
    fwo_b = bv("WA", 4096, 12288).rearrange("p (c o) -> p c o", c=4)
    FQC = [B(f"FQC{c}") for c in range(NCH)]
    FOC = [B(f"FOC{c}") for c in range(4)]

    def emit_attn_cast(l, extra=()):
        for c in range(NCH):
            cast_w(fqkv_b[:, c, :], FQC[c], fqkv[l, c * 128:(c + 1) * 128, :],
                   128, FFI, eng="RR", extra_w=extra)
        for c in range(4):
            cast_w(fwo_b[:, c, :], FOC[c], fwo[l, c * 128:(c + 1) * 128, :],
                   128, D, eng="RR", extra_w=extra)

    rs_row = fv("STAT", 512, 0)
    c0_row = fv("STAT", 512, 512)
    nmld_row = fv("STAT", 1024, 0)
    STATB = B("STATB")
    SCRB = B("SCRB")
    GATEB = B("GATEB")
    SMALLB = B("SMALLB")
    sq_scr = fv("SCR", 1024, 512)

    def stat_finish(width):
        jwarm(6)
        mu = sq_scr[0:1, 0:width]
        sop(lambda e, w=width: e.activation(mu[0:1, 0:w], PE[0:1, 0:w], AF.Copy,
                                            scale=1.0 / D),
            reads=[PEB], writes=[SCRB])
        vop(lambda e, w=width: e.scalar_tensor_tensor(c0_row[0:1, 0:w], mu[0:1, 0:w],
                                                      1.0, mu[0:1, 0:w],
                                                      ALU.mult, ALU.mult),
            reads=[SCRB], writes=[STATB])
        vop(lambda e, w=width: e.scalar_tensor_tensor(rs_row[0:1, 0:w], PB[0:1, 0:w],
                                                      1.0 / D, c0_row[0:1, 0:w],
                                                      ALU.mult, ALU.subtract),
            reads=[PBB, STATB], writes=[STATB])
        sop(lambda e, w=width: e.activation(rs_row[0:1, 0:w], rs_row[0:1, 0:w],
                                            AF.Ln, bias=pvcol("eps", rows=1)),
            reads=[STATB, CONSTF], writes=[STATB])
        sop(lambda e, w=width: e.activation(rs_row[0:1, 0:w], rs_row[0:1, 0:w],
                                            AF.Exp, scale=-0.5),
            reads=[STATB], writes=[STATB])
        vop(lambda e, w=width: e.scalar_tensor_tensor(c0_row[0:1, 0:w], mu[0:1, 0:w],
                                                      -1.0, rs_row[0:1, 0:w],
                                                      ALU.mult, ALU.mult),
            reads=[SCRB, STATB], writes=[STATB])

    def ln_stats_sf(c0, width, shifted=False):
        for c in range(NCH):
            off = -1 if (shifted and c >= 4) else 0
            ap = sF[:, c, c0 + off: c0 + off + width]
            rb = sf_bufs(c0 + off, c0 + off + width)
            mm(PEB, PE[0:1, 0:width], ones_f[:, 0:1], ap, [CONSTF] + rb,
               start=(c == 0), stop=(c == NCH - 1))
        for c in range(NCH):
            off = -1 if (shifted and c >= 4) else 0
            ap = sF[:, c, c0 + off: c0 + off + width]
            rb = sf_bufs(c0 + off, c0 + off + width)
            sop(lambda e, a=ap, w=width: e.activation(sq_scr[:, 0:w], a, AF.Square),
                reads=rb, writes=[SCRB])
            if c < 2:
                jwarm(1)
            mm(PBB, PB[0:1, 0:width], ones_f[:, 0:1], sq_scr[:, 0:width],
               [CONSTF, SCRB], start=(c == 0), stop=(c == NCH - 1))
        stat_finish(width)

    def bcast_rows(width):
        mm(PCB, PC[:, 0:width], ones_f[0:1, 0:128], rs_row[0:1, 0:width],
           [CONSTF, STATB], start=True, stop=True)
        mm(PDB, PD[:, 0:width], ones_f[0:1, 0:128], c0_row[0:1, 0:width],
           [CONSTF, STATB], start=True, stop=True)

    WAB, WBB, WGB, SBFB = B("WAB"), B("WBB"), B("WGB"), B("SBFB")
    wqkv_b = bv("WA", 24576).rearrange("p (c o) -> p c o", c=NCH)
    wout_b = bv("WB", 8192).rearrange("p (c o) -> p c o", c=NCH)
    wg1_b = bv("WG", 512, 0).rearrange("p (c o) -> p c o", c=NCH)
    wg2_b = bv("WG", 1024, 512)

    for c in range(NCH):
        cast_w(wqkv_b[:, c, :], WAB, wqkvT[c * 128:(c + 1) * 128, :], 128, 3 * D,
               eng="RR")
        cast_w(wout_b[:, c, :], WBB, woutT[c * 128:(c + 1) * 128, :], 128, D,
               eng="RR")
        cast_w(wg1_b[:, c, :], WGB, wg1T[c * 128:(c + 1) * 128, :], 128, HD,
               eng="RR")
    cast_w(wg2_b, WGB, wg2T, 64, D, eng="RR")

    sbf = bv("SBF", 512)
    s32 = fv("S32", 512)
    S32B = B("S32B")
    vop(lambda e: e.memset(sbf[:], 0.0), writes=[SBFB])
    op("T", lambda e: e.matmul(PA[:, 512:1024], eye_b, warm_rhs,
                               start=True, stop=True), reads=[INB, CONSTB])
    jwarm(16)

    h1 = bv("YBF", 2048).rearrange("p (c t) -> p c t", c=NCH)
    H1C = [B(f"H1C{c}") for c in range(NCH)]
    qf = bv("QF", 2048).rearrange("p (c t) -> p c t", c=NCH)
    kf = bv("KF", 2048).rearrange("p (c t) -> p c t", c=NCH)
    vt = bv("VT", 2048).rearrange("p (t o) -> p t o", t=2)
    kst = bv("KST", 2048).rearrange("p (t o) -> p t o", t=2)
    QFB, KFB, VTB, KSTB = B("QFB"), B("KFB"), B("VTB"), B("KSTB")
    atm = bv("ATM", 768).rearrange("p (i c) -> p i c", i=2)
    ATMB = [B("ATM0"), B("ATM1")]
    g1f = bv("ATM", 256, 768)
    G1B = B("G1B")
    gate = fv("GATE", 2048).rearrange("p (c t) -> p c t", c=NCH)
    oln = bv("OF", 2048).rearrange("p (c t) -> p c t", c=NCH)
    OLNC = [B(f"OLNC{c}") for c in range(NCH)]
    scr = fv("SCR", 1024, 0)

    ps_open = [False]

    for s in range(NSUP):
        c0 = 1 + s * SUP
        sfb = [SFB[s]]

        ln_stats_sf(c0, SUP)
        bcast_rows(SUP)
        jwarm(3)
        for c in range(NCH):
            vop(lambda e, c=c, c0=c0: e.tensor_tensor(scr[:, 0:SUP],
                                                      sF[:, c, c0:c0 + SUP],
                                                      PC[:, 0:SUP], ALU.mult),
                reads=sfb + [PCB], writes=[SCRB])
            vop(lambda e, c=c: e.tensor_tensor(h1[:, c, :], scr[:, 0:SUP],
                                               PD[:, 0:SUP], ALU.add),
                reads=[SCRB, PDB], writes=[H1C[c]])

        if s == 0:
            tap("h1", h1, list(H1C))
        for oc in range(16):
            pb, pbuf = (PC, PCB) if oc % 2 == 0 else (PD, PDB)
            for kc in range(NCH):
                mm(pbuf, pb[:, 0:SUP], wqkv_b[:, kc, oc * 128:(oc + 1) * 128],
                   h1[:, kc, :], [WAB, H1C[kc]], start=(kc == 0), stop=(kc == NCH - 1))
            dst, dbuf = (qf, QFB) if oc < 8 else (kf, KFB)
            sop(lambda e, d=dst[:, oc % 8, :], p=pb: e.activation(d, p[:, 0:SUP],
                                                                  AF.Silu),
                reads=[pbuf], writes=[dbuf])
        for tt in range(2):
            pb, pbuf = (PC, PCB) if tt == 0 else (PD, PDB)
            for half in range(2):
                for kc in range(NCH):
                    mm(pbuf, pb[:, 0:512], h1[:, kc, tt * 128:(tt + 1) * 128],
                       wqkv_b[:, kc, 2048 + half * 512: 2048 + (half + 1) * 512],
                       [WAB, H1C[kc]], start=(kc == 0), stop=(kc == NCH - 1))
                sop(lambda e, t=tt, h=half, p=pb: e.activation(
                        vt[:, t, h * 512:(h + 1) * 512], p[:, 0:512], AF.Copy),
                    reads=[pbuf], writes=[VTB])
        for tt in range(2):
            for c in range(NCH):
                trans(PTB, PT[:, c * 128:(c + 1) * 128],
                      kf[:, c, tt * 128:(tt + 1) * 128], eye_b, [KFB, CONSTB],
                      start=(c == 0), stop=(c == NCH - 1))
            vop(lambda e, t=tt: e.tensor_copy(kst[:, t, :], PT[:]),
                reads=[PTB], writes=[KSTB])

        if s == 0:
            tap("qf", qf, [QFB])
            tap("kf", kf, [KFB])
            tap("vt", vt, [VTB])
            tap("kst", kst, [KSTB])
        for kc in range(NCH):
            mm(PEB, PE[0:64, 0:SUP], wg1_b[:, kc, :], h1[:, kc, :],
               [WGB, H1C[kc]], start=(kc == 0), stop=(kc == NCH - 1))
        sop(lambda e: e.activation(g1f[0:64, :], PE[0:64, 0:SUP], AF.Identity,
                                   bias=pvcol("b_g1", rows=64)),
            reads=[PEB, CONSTF], writes=[G1B])
        for dc in range(NCH):
            pb, pbuf = (PC, PCB) if dc % 2 == 0 else (PD, PDB)
            mm(pbuf, pb[:, 0:SUP], wg2_b[0:64, dc * 128:(dc + 1) * 128],
               g1f[0:64, :], [WGB, G1B], start=True, stop=True)
            sop(lambda e, d=dc, p=pb: e.activation(gate[:, d, :], p[:, 0:SUP],
                                                   AF.Sigmoid, bias=pvcol("b_g2", d)),
                reads=[pbuf, CONSTF], writes=[GATEB])

        if s == 0:
            tap("gate", gate, [GATEB])
        for pr in range(8):
            opb, opbuf = (PE, PEB) if pr % 2 == 0 else (PB, PBB)
            for hh in range(2):
                h = pr * 2 + hh
                jwarm(1)
                hc, hp = h // 2, (h % 2) * 64
                ab = ATMB[h % 2]
                mm(PAB0, PA[:, 0:256], kf[hp:hp + 64, hc, 0:128],
                   qf[hp:hp + 64, hc, 0:256], [KFB, QFB], start=True, stop=False)
                mm(PAB0, PA[:, 256:384], kf[hp:hp + 64, hc, 128:256],
                   qf[hp:hp + 64, hc, 128:256], [KFB, QFB], start=False, stop=True)
                vop(lambda e, i=h % 2: e.tensor_tensor(atm[:, i, :], PA[:, 0:384],
                                                       maskb_b, ALU.mult),
                    reads=[PAB0, CONSTB], writes=[ab])
                if s == 0 and h == 0:
                    tap("atm_h0", atm[:, 0, :], [ab])
                first_o = [True]
                def omm(lhs, rhs, rb, stop):
                    st = first_o[0]
                    first_o[0] = False
                    mm(opbuf, lhs[0], lhs[1], rhs, rb, start=st, stop=stop)
                if s > 0:
                    omm((opb[hp:hp + 64, 0:256],
                         sbf[hp:hp + 64, hc * 64:(hc + 1) * 64]),
                        qf[hp:hp + 64, hc, 0:256], [SBFB, QFB], False)
                omm((opb[hp:hp + 64, 0:256], vt[:, 0, h * 64:(h + 1) * 64]),
                    atm[:, h % 2, 0:256], [VTB, ab], False)
                omm((opb[hp:hp + 64, 128:256], vt[:, 1, h * 64:(h + 1) * 64]),
                    atm[:, h % 2, 256:384], [VTB, ab], stop=True)
                mm(PSB, PS[hp:hp + 64, hc * 64:(hc + 1) * 64],
                   kst[:, 0, h * 64:(h + 1) * 64], vt[:, 0, h * 64:(h + 1) * 64],
                   [KSTB, VTB], start=True, stop=False)
                mm(PSB, PS[hp:hp + 64, hc * 64:(hc + 1) * 64],
                   kst[:, 1, h * 64:(h + 1) * 64], vt[:, 1, h * 64:(h + 1) * 64],
                   [KSTB, VTB], start=False, stop=True)
            vop(lambda e, pr=pr, p=opb: e.tensor_tensor(gate[:, pr, :],
                                                        p[:, 0:256],
                                                        gate[:, pr, :], ALU.mult),
                reads=[opbuf, GATEB], writes=[GATEB])
        if s == 0:
            vop(lambda e: e.tensor_copy(s32[:], PS[:]), reads=[PSB], writes=[S32B])
        else:
            vop(lambda e: e.tensor_tensor(s32[:], PS[:], s32[:], ALU.add),
                reads=[PSB, S32B], writes=[S32B])
        if s < NSUP - 1:
            sop(lambda e: e.copy(sbf[:], s32[:]), reads=[S32B], writes=[SBFB])

        if s <= 1:
            tap(f"og{s}", gate, [GATEB])
            tap(f"sbf{s}", sbf, [SBFB])
        for c in range(NCH):
            mm(PEB, PE[0:1, 0:SUP], ones_f[:, 0:1], gate[:, c, :],
               [CONSTF, GATEB], start=(c == 0), stop=(c == NCH - 1))
        for c in range(NCH):
            sop(lambda e, c=c: e.activation(sq_scr[:, 0:SUP], gate[:, c, :],
                                            AF.Square),
                reads=[GATEB], writes=[SCRB])
            if c < 2:
                jwarm(1)
            mm(PBB, PB[0:1, 0:SUP], ones_f[:, 0:1], sq_scr[:, 0:SUP],
               [CONSTF, SCRB], start=(c == 0), stop=(c == NCH - 1))
        stat_finish(SUP)
        bcast_rows(SUP)
        jwarm(3)
        for c in range(NCH):
            vop(lambda e, c=c: e.tensor_tensor(scr[:, 0:SUP], gate[:, c, :],
                                               PC[:, 0:SUP], ALU.mult),
                reads=[GATEB, PCB], writes=[SCRB])
            vop(lambda e: e.tensor_tensor(scr[:, 256:256 + SUP], scr[:, 0:SUP],
                                          PD[:, 0:SUP], ALU.add),
                reads=[PDB, SCRB], writes=[SCRB])
            sop(lambda e, c=c: e.activation(oln[:, c, :], scr[:, 256:256 + SUP],
                                            AF.Identity, bias=pvcol("beta", c),
                                            scale=pvcol("gamma", c)),
                reads=[SCRB, CONSTF], writes=[OLNC[c]])

        if s == 0:
            tap("oln", oln, list(OLNC))
        jwarm(3)
        for dc in range(NCH):
            pb, pbuf = (PC, PCB) if dc % 2 == 0 else (PD, PDB)
            for kc in range(NCH):
                mm(pbuf, pb[:, 0:SUP], wout_b[:, kc, dc * 128:(dc + 1) * 128],
                   oln[:, kc, :], [WBB, OLNC[kc]], start=(kc == 0), stop=(kc == NCH - 1))
            vop(lambda e, d=dc, p=pb, c0=c0: e.tensor_tensor(
                    sF[:, d, c0:c0 + SUP], p[:, 0:SUP], sF[:, d, c0:c0 + SUP],
                    ALU.add),
                reads=[pbuf] + sfb, writes=sfb)
            dmaG("SPILL", x2s[dc * 128:(dc + 1) * 128, s * SUP:(s + 1) * SUP],
                 sF[:, dc, c0:c0 + SUP], reads=sfb, writes=[B("X2SB")])

    emit_attn_cast(0, extra=(WAB,))

    kb.barrier()
    tap_sf("x2")

    for g in range(4):
        c0 = 1 + g * 512
        gb = sf_bufs(c0, c0 + 512)
        ln_stats_sf(c0, 512)
        bcast_rows(512)
        for c in range(NCH):
            vop(lambda e, c=c, c0=c0: e.tensor_tensor(scr[:, 0:512],
                                                      sF[:, c, c0:c0 + 512],
                                                      PC[:, 0:512], ALU.mult),
                reads=gb + [PCB], writes=[SCRB])
            vop(lambda e, c=c, c0=c0: e.tensor_tensor(sF[:, c, c0:c0 + 512],
                                                      scr[:, 0:512],
                                                      PD[:, 0:512], ALU.add),
                reads=[SCRB, PDB], writes=gb)

    kb.barrier()
    tap_sf("y0")

    YBND = B("YBNDB")
    ybnd = fv("YBND", 32).rearrange("p (c k) -> p c k", c=NCH)

    w1v = [bv("WA", 2048, i * 2048).rearrange("p (c o) -> p c o", c=NCH)
           for i in range(2)]
    W1B = [B("W1B0"), B("W1B1")]
    w2v = [bv("WB", 1024, i * 1024) for i in range(8)]
    W2B = [B(f"W2B{i}") for i in range(8)]

    qfc = bv("QF", 2048).rearrange("p (b t) -> p b t", b=4)
    kfc = bv("KF", 4096).rearrange("p (w b t) -> p w b t", w=2, b=4)
    vtc = bv("VT", 4096).rearrange("p (w b t) -> p w b t", w=2, b=4)
    QFCB = B("QFCB")
    KFCB = [B("KFC0"), B("KFC1")]
    VTCB = [B("VTC0"), B("VTC1")]
    pa_junk = bv("KST", 2048)
    PAJB = B("PAJB")
    ptile8 = bv("ATM", 1536).rearrange("p (i c) -> p i c", i=12)
    PTL8 = [B(f"PTL{i}") for i in range(12)]
    _pabk = [B("PABK0"), B("PABK1")]
    PASL = [_pabk[i // 4] for i in range(8)]
    bcastB = fv("GATE", 2048, 0)
    BCB = None
    ofc = bv("OF", 2048).rearrange("p (b t) -> p b t", b=4)
    OFCB = B("OFCB")
    ybf = bv("YBF", 2064).rearrange("p (i c k) -> p i c k", i=2, c=NCH)
    YBFB = [B("YBF0"), B("YBF1")]
    tabw = fv("TAB", 1024).rearrange("p (i k) -> p i k", i=4)
    TABB = B("TABB")
    qscale_t = fv("SCR", 256, 1024).rearrange("p (i k) -> p i k", i=4)
    QSB = B("QSB")
    scrA = fv("GATE", 512, 0)
    scrB2 = fv("GATE", 512, 512)
    scrC = fv("GATE", 512, 1024)
    scrD = fv("SCR", 512, 0)
    def nm_v(par):
        return fv("SMALL", 8, par * 48 + 0)
    def dsP_v(par):
        return fv("SMALL", 8, par * 48 + 8)
    def dsC_v(par):
        return fv("SMALL", 8, par * 48 + 16)
    def lnd_v(par):
        return fv("SMALL", 8, par * 48 + 24)
    def nmld_v(par):
        return fv("SMALL", 8, par * 48 + 32)
    def tmp1_v(par):
        return fv("SMALL", 8, par * 48 + 40)
    rn_t = fv("SMALL", 8, 96)
    rn_k = fv("SMALL", 8, 104)
    RNB = [B("RNQ"), B("RNK")]
    WKB = B("WKB")
    QSC = B("QSCR")
    PAQB, PAKB = B("PAQB"), B("PAKB")
    PAJQ, PAJK = B("PAJQ"), B("PAJK")
    X2R = [B("X2R0"), B("X2R1")]

    hg_regions = []
    for i in range(20):
        hg_regions.append(bv("WA", 1024, 4096 + i * 1024))
    hg_regions.append(bv("QF", 1024, 0))
    hg_regions.append(bv("QF", 1024, 1024))
    HGB = B("HGB")
    ln4f = ABt[:, bplan["KF"][0]: bplan["KF"][0] + 8192].rearrange(
        "p (c t) -> p c t", c=NCH)
    LN4C = [B(f"LN4C{i}") for i in range(2 * NCH)]
    gel = fv("GATE", 1024, 0)

    def rotary_pair(l, w, tt, qp, qpb, kp, kpb, kfw, kb_cur):
        t1 = [fv("S32", 512, 0), fv("WST", 512, 0)]
        Ax = [fv("SCR", 512, 0), fv("WST", 512, 512)]
        Cx = [fv("SCR", 512, 512), fv("WST", 512, 1024)]
        Dx = [fv("S32", 512, 0), fv("WST", 512, 1536)]
        T1f = [QSC, WKB]
        DBf = [QSC, WKB]
        GBf = [SCRB, WKB]
        rn = [rn_t, rn_k]
        ps = [(qp, qpb), (kp, kpb)]
        pj = [pa_junk[:, 1024:1536], pa_junk[:, 1536:2048]]
        PJb = [PAJQ, PAJK]
        cs = [tabw[:, 0, tt * 64:(tt + 1) * 64], tabw[:, 2, tt * 64:(tt + 1) * 64]]
        sn = [tabw[:, 1, tt * 64:(tt + 1) * 64], tabw[:, 3, tt * 64:(tt + 1) * 64]]
        h3 = lambda a: a.rearrange("p (h d) -> p h d", h=HC)
        for i in (0, 1):
            sop(lambda e, i=i: e.activation(Ax[i], ps[i][0][:, 0:512], AF.Square),
                reads=[ps[i][1]], writes=[GBf[i]])
        for i in (0, 1):
            vop(lambda e, i=i: e.reduce_sum(rn[i][:, 0:8], h3(Ax[i]), AX.X),
                reads=[GBf[i]], writes=[RNB[i]])
        for i in (0, 1):
            sop(lambda e, i=i: e.activation(rn[i][:, 0:8], rn[i][:, 0:8], AF.Sqrt),
                reads=[RNB[i]], writes=[RNB[i]])
        for i in (0, 1):
            vop(lambda e, i=i: e.tensor_scalar_max(rn[i][:, 0:8], rn[i][:, 0:8],
                                                   1e-12),
                reads=[RNB[i]], writes=[RNB[i]])
            vop(lambda e, i=i: e.reciprocal(rn[i][:, 0:8], rn[i][:, 0:8]),
                reads=[RNB[i]], writes=[RNB[i]])
        for i in (0, 1):
            vop(lambda e, i=i: e.tensor_tensor(
                    h3(t1[i]), h3(ps[i][0][:, 0:512]),
                    rn[i][:, 0:8].unsqueeze(2).broadcast_to([128, HC, HD]),
                    ALU.mult),
                reads=[ps[i][1], RNB[i]], writes=[T1f[i]])
        for i in (0, 1):
            vop(lambda e, i=i: e.tensor_tensor(
                    h3(Ax[i]), h3(t1[i]),
                    cs[i].unsqueeze(1).broadcast_to([128, HC, HD]), ALU.mult),
                reads=[T1f[i], TABB], writes=[GBf[i]])
        for i in (0, 1):
            vop(lambda e, i=i: e.tensor_tensor(
                    h3(Cx[i])[:, :, 0:32], h3(t1[i])[:, :, 32:64],
                    sn[i][:, 0:32].unsqueeze(1).broadcast_to([128, HC, 32]),
                    ALU.mult),
                reads=[T1f[i], TABB], writes=[GBf[i]])
            vop(lambda e, i=i: e.tensor_tensor(
                    h3(Cx[i])[:, :, 32:64], h3(t1[i])[:, :, 0:32],
                    sn[i][:, 32:64].unsqueeze(1).broadcast_to([128, HC, 32]),
                    ALU.mult),
                reads=[T1f[i], TABB], writes=[GBf[i]])
        for i in (0, 1):
            vop(lambda e, i=i: e.tensor_sub(
                    h3(Dx[i])[:, :, 0:32], h3(Ax[i])[:, :, 0:32],
                    h3(Cx[i])[:, :, 0:32]),
                reads=[GBf[i]], writes=[DBf[i]])
            vop(lambda e, i=i: e.tensor_add(
                    h3(Dx[i])[:, :, 32:64], h3(Ax[i])[:, :, 32:64],
                    h3(Cx[i])[:, :, 32:64]),
                reads=[GBf[i]], writes=[DBf[i]])
        for i in (0, 1):
            sop(lambda e, i=i: e.copy(pj[i], Dx[i]),
                reads=[DBf[i]], writes=[PJb[i]])
        for i in (0, 1):
            for blk in range(4):
                trans(PTB, PT[:, i * 512 + blk * 128: i * 512 + (blk + 1) * 128],
                      pj[i][:, blk * 128:(blk + 1) * 128], eye_b,
                      [PJb[i], CONSTB],
                      start=(i == 0 and blk == 0), stop=(i == 1 and blk == 3))
        vop(lambda e, tt=tt: e.tensor_copy(
                qfc[:, :, tt * 128:(tt + 1) * 128],
                PT[:, 0:512].rearrange("p (b c) -> p b c", b=4)),
            reads=[PTB], writes=[QFCB])
        vop(lambda e, tt=tt, kfw=kfw: e.tensor_copy(
                kfw[:, :, tt * 128:(tt + 1) * 128],
                PT[:, 512:1024].rearrange("p (b c) -> p b c", b=4)),
            reads=[PTB], writes=[kb_cur])

    def stage_c_attn(l):
        if l > 0:
            emit_attn_cast(l)
        op("V", lambda e: e.nop(), writes=[WST[0], WST[1], WKB])
        for i in range(4):
            dmaG("QS", qscale_t[:, i, :], qscaleD[l, i], writes=[QSB])
        for w in range(1, NW):
            vop(lambda e, w=w: e.tensor_copy(ybnd[:, :, w:w + 1],
                                             sF[:, :, w * WIN:w * WIN + 1]),
                reads=sf_bufs(w * WIN, w * WIN + 1), writes=[YBND])
        jwarm(12)

        for w in range(NW):
            wsl = w % 2
            kfw, kfp = kfc[:, wsl], kfc[:, 1 - wsl]
            vtw, vtp = vtc[:, wsl], vtc[:, 1 - wsl]
            kb_cur, kb_prev = KFCB[wsl], KFCB[1 - wsl]
            vb_cur, vb_prev = VTCB[wsl], VTCB[1 - wsl]
            wsf = sf_bufs(1 + w * WIN, 1 + (w + 1) * WIN)

            dmaQ("TAB", tabw[:, :, :], tabsD[:, w].transpose([1, 0, 2]), writes=[TABB])
            dmaQ("TABL", fv("SCR", 1024, 0).rearrange("p (i k) -> p i k", i=4),
                 tabsloD[:, w].transpose([1, 0, 2]), writes=[SCRB])
            vop(lambda e: e.tensor_tensor(tabw[:, :, :], tabw[:, :, :],
                                          fv("SCR", 1024, 0).rearrange(
                                              "p (i k) -> p i k", i=4), ALU.add),
                reads=[SCRB, TABB], writes=[TABB])
            vop(lambda e: e.tensor_tensor(
                    tabw.rearrange("p i (a b) -> p i a b", a=TW),
                    tabw.rearrange("p i (a b) -> p i a b", a=TW),
                    qscale_t[:, :, :].unsqueeze(2).broadcast_to([128, 4, TW, HD]),
                    ALU.mult),
                reads=[QSB, TABB], writes=[TABB])

            def emit_ybf(tt):
                t0g = 1 + w * WIN + tt * 128
                ti = tt % 2
                if w > 0 and tt == 0:
                    op("G", lambda e, ti=ti, t0g=t0g: e.tensor_copy(
                           ybf[:, ti, :, 1:129], sF[:, :, t0g:t0g + 128]),
                       reads=wsf, writes=[YBFB[ti]])
                    op("G", lambda e, ti=ti, w=w: e.tensor_copy(
                           ybf[:, ti, :, 0:1], ybnd[:, :, w:w + 1]),
                       reads=[YBND], writes=[YBFB[ti]])
                else:
                    op("G", lambda e, ti=ti, t0g=t0g: e.tensor_copy(
                           ybf[:, ti, :, 0:129], sF[:, :, t0g - 1:t0g + 128]),
                       reads=sf_bufs(t0g - 1, t0g + 128), writes=[YBFB[ti]])

            slot_ctr = [0]

            def slot_ap(sl):
                if sl < 4:
                    return PC[:, sl * 128:(sl + 1) * 128], PCB
                return PD[:, (sl - 4) * 128:(sl - 3) * 128], PDB

            def kts_of(tt):
                return ([("p", j) for j in range(tt, TW)] if w > 0 else []) \
                    + [("c", j) for j in range(0, tt + 1)]

            def pass1(tt):
                par = tt % 2
                nm, dsP, dsC = nm_v(par), dsP_v(par), dsC_v(par)
                tmp1 = tmp1_v(par)
                wp = (TW - tt) * 128
                wc = (tt + 1) * 128
                for h in range(HC):
                    hp, hpr = (h % 2) * 64, h // 2
                    ppv = PB[:, 0:wp]
                    pcu = PE[:, 0:wc]
                    if w > 0:
                        op("T", lambda e, o=ppv, l1=qfc[hp:hp + 64, hpr, tt * 128:(tt + 1) * 128],
                           r=kfp[hp:hp + 64, hpr, tt * 128:512]:
                           e.matmul(o, l1, r, start=True, stop=True),
                           reads=[QFCB, kb_prev], writes=[PBB])
                        vop(lambda e, o=ppv: e.tensor_tensor(o[:, 0:128], o[:, 0:128],
                                                             ml_b, ALU.add),
                            reads=[CONSTB], writes=[PBB])
                    op("T", lambda e, o=pcu, l1=qfc[hp:hp + 64, hpr, tt * 128:(tt + 1) * 128],
                       r=kfw[hp:hp + 64, hpr, 0:wc]:
                       e.matmul(o, l1, r, start=True, stop=True),
                       reads=[QFCB, kb_cur], writes=[PEB])
                    vop(lambda e, o=pcu, wc=wc: e.tensor_tensor(
                            o[:, wc - 128:wc], o[:, wc - 128:wc], mh_b, ALU.add),
                        reads=[CONSTB], writes=[PEB])
                    if w > 0:
                        vop(lambda e, o=ppv: e.reduce_max(tmp1[:, 0:1], o, AX.X,
                                                          negate=True),
                            reads=[PBB], writes=[SMALLB])
                        vop(lambda e, o=pcu: e.reduce_max(tmp1[:, 1:2], o, AX.X,
                                                          negate=True),
                            reads=[PEB], writes=[SMALLB])
                        vop(lambda e, h=h: e.tensor_tensor(nm[:, h:h + 1],
                                                           tmp1[:, 0:1],
                                                           tmp1[:, 1:2], ALU.min),
                            reads=[SMALLB], writes=[SMALLB])
                        sop(lambda e, o=ppv, wp=wp, h=h: e.activation(
                                pa_junk[:, 0:wp], o, AF.Exp,
                                bias=nm[:, h:h + 1], accum_out=dsP[:, h:h + 1]),
                            reads=[PBB, SMALLB], writes=[PAJB, SMALLB])
                    else:
                        vop(lambda e, o=pcu, h=h: e.reduce_max(
                                nm[:, h:h + 1], o, AX.X, negate=True),
                            reads=[PEB], writes=[SMALLB])
                    sop(lambda e, o=pcu, wc=wc, h=h: e.activation(
                            pa_junk[:, 512:512 + wc], o, AF.Exp,
                            bias=nm[:, h:h + 1], accum_out=dsC[:, h:h + 1]),
                        reads=[PEB, SMALLB], writes=[PAJB, SMALLB])

            def nmphase(tt):
                par = tt % 2
                nm, dsP, dsC, lnd, nmld = (nm_v(par), dsP_v(par), dsC_v(par),
                                           lnd_v(par), nmld_v(par))
                bcB = bcastB[:, par * 1024:(par + 1) * 1024]
                if w > 0:
                    vop(lambda e: e.tensor_add(dsC[:, 0:8], dsC[:, 0:8], dsP[:, 0:8]),
                        reads=[SMALLB], writes=[SMALLB])
                sop(lambda e: e.activation(lnd[:, 0:8], dsC[:, 0:8], AF.Ln),
                    reads=[SMALLB], writes=[SMALLB])
                vop(lambda e: e.tensor_sub(nmld[:, 0:8], nm[:, 0:8], lnd[:, 0:8]),
                    reads=[SMALLB], writes=[SMALLB])
                for grp in range(2):
                    for hh in range(4):
                        h = grp * 4 + hh
                        trans(PBB, PB[0:1, hh * 128:(hh + 1) * 128],
                              nmld[:, h:h + 1], eye_f, [SMALLB, CONSTF],
                              start=(hh == 0), stop=(hh == 3))
                    vop(lambda e, g=grp: e.tensor_copy(
                            nmld_row[0:1, g * 512:(g + 1) * 512], PB[0:1, 0:512]),
                        reads=[PBB], writes=[STATB])
                for half in range(2):
                    mm(PBB, PB[:, 0:512], ones_f[0:1, 0:128],
                       nmld_row[0:1, half * 512:(half + 1) * 512],
                       [STATB, CONSTF], start=True, stop=True)
                    sop(lambda e, half=half, b=bcB: e.copy(
                            b[:, half * 512:(half + 1) * 512], PB[:, 0:512]),
                        reads=[PBB], writes=[GATEB])

            def pass2(tt):
                par = tt % 2
                bcB = bcastB[:, par * 1024:(par + 1) * 1024]
                nkts = (TW - tt if w > 0 else 0) + (tt + 1)

                def emit_at(h, sls):
                    hp, hpr = (h % 2) * 64, h // 2
                    for idx, (wk, j) in enumerate(kts_of(tt)):
                        kfx, kbx = (kfp, kb_prev) if wk == "p" else (kfw, kb_cur)
                        sl = slot_ctr[0] % 8
                        slot_ctr[0] += 1
                        sls.append((sl, wk, j))
                        pap, pbf = slot_ap(sl)
                        op("T", lambda e, o=pap, l1=kfx[hp:hp + 64, hpr, j * 128:(j + 1) * 128],
                           r=qfc[hp:hp + 64, hpr, tt * 128:(tt + 1) * 128]:
                           e.matmul(o, l1, r, start=True, stop=True),
                           reads=[kbx, QFCB], writes=[pbf])

                def slot_runs(sls):
                    runs = []
                    for sl, wk, j in sls:
                        if runs and runs[-1][0] + runs[-1][1] == sl and sl % 4 != 0:
                            runs[-1][1] += 1
                        else:
                            runs.append([sl, 1])
                    return runs

                def rest_a(h, sls):
                    for s0, ln_ in slot_runs(sls):
                        pbf = slot_ap(s0)[1]
                        wide = PC if s0 < 4 else PD
                        off = (s0 % 4) * 128
                        vop(lambda e, w_=wide, off=off, ln_=ln_, h=h, b=bcB:
                            e.tensor_tensor(
                                w_[:, off:off + ln_ * 128].rearrange(
                                    "p (a b) -> p a b", a=ln_),
                                w_[:, off:off + ln_ * 128].rearrange(
                                    "p (a b) -> p a b", a=ln_),
                                b[:, h * 128:(h + 1) * 128].unsqueeze(1)
                                .broadcast_to([128, ln_, 128]), ALU.add),
                            reads=[GATEB], writes=[pbf])
                    for sl, wk, j in sls:
                        pap, pbf = slot_ap(sl)
                        if wk == "p" and j == tt:
                            vop(lambda e, o=pap: e.tensor_tensor(o, o, mh_b, ALU.add),
                                reads=[CONSTB], writes=[pbf])
                        if wk == "c" and j == tt:
                            vop(lambda e, o=pap: e.tensor_tensor(o, o, ml_b, ALU.add),
                                reads=[CONSTB], writes=[pbf])
                    for s0, ln_ in slot_runs(sls):
                        pbf = slot_ap(s0)[1]
                        wide = PC if s0 < 4 else PD
                        off = (s0 % 4) * 128
                        pp = s0
                        sop(lambda e, w_=wide, off=off, ln_=ln_, pp=pp: e.activation(
                                ptile8[:, pp:pp + ln_, :].rearrange("p a b -> p (a b)"),
                                w_[:, off:off + ln_ * 128], AF.Exp),
                            reads=[pbf],
                            writes=[PTL8[s0 + k] for k in range(ln_)])
                def rest_b(h, sls):
                    hp = (h % 2) * 64
                    oc0 = (h // 2) * 128
                    for idx, (sl, wk, j) in enumerate(sls):
                        vtx, vbx = (vtp, vb_prev) if wk == "p" else (vtw, vb_cur)
                        op("T", lambda e, pp=sl, l1=vtx[:, j, h * 64:(h + 1) * 64],
                           hp=hp, oc0=oc0, st=(idx == 0), sp=(idx == nkts - 1):
                           e.matmul(PS[hp:hp + 64, oc0:oc0 + 128], l1,
                                    ptile8[:, pp, :], start=st, stop=sp),
                           reads=[vbx, PTL8[sl]], writes=[PSB])

                prev = None
                for h in range(HC):
                    sls = []
                    if prev is not None:
                        rest_a(prev[0], prev[1])
                    emit_at(h, sls)
                    if prev is not None:
                        rest_b(prev[0], prev[1])
                    prev = (h, sls)
                rest_a(prev[0], prev[1])
                rest_b(prev[0], prev[1])
                for pr in range(4):
                    vop(lambda e, pr=pr, tt=tt: e.tensor_copy(
                            ofc[:, pr, tt * 128:(tt + 1) * 128],
                            PS[:, pr * 128:(pr + 1) * 128]),
                        reads=[PSB], writes=[OFCB])

            emit_ybf(0)
            for tt in range(TW):
                ti = tt % 2
                qp, qpb = PA[:, 0:512], PAQB
                kp, kpb = PA[:, 512:1024], PAKB
                vp, vpb = PE[:, 0:512], PEB
                for kc in range(NCH):
                    lhs = ybf[:, ti, kc, 1:129] if kc < 4 else ybf[:, ti, kc, 0:128]
                    mm(qpb, qp[:, 0:512], lhs, fqkv_b[:, kc, 0:512],
                       [YBFB[ti], FQC[kc]], start=(kc == 0), stop=(kc == NCH - 1))
                    mm(kpb, kp[:, 0:512], lhs, fqkv_b[:, kc, 512:1024],
                       [YBFB[ti], FQC[kc]], start=(kc == 0), stop=(kc == NCH - 1))
                    mm(vpb, vp[:, 0:512], lhs, fqkv_b[:, kc, 1024:1536],
                       [YBFB[ti], FQC[kc]], start=(kc == 0), stop=(kc == NCH - 1))
                if tt + 1 < TW:
                    emit_ybf(tt + 1)
                sop(lambda e, tt=tt, vv=vtw, vp=vp: e.activation(vv[:, tt, :],
                                                                 vp[:, 0:512], AF.Copy),
                    reads=[vpb], writes=[vb_cur])
                rotary_pair(l, w, tt, qp, qpb, kp, kpb, kfw, kb_cur)

            pass1(0)
            nmphase(0)
            for tt in range(TW):
                if tt + 1 < TW:
                    pass1(tt + 1)
                pass2(tt)
                if tt + 1 < TW:
                    nmphase(tt + 1)

            c0 = 1 + w * WIN
            for dc in range(NCH):
                pb, pbuf = (PC, PCB) if dc % 2 == 0 else (PD, PDB)
                for kc in range(4):
                    mm(pbuf, pb[:, 0:WIN], fwo_b[:, kc, dc * 128:(dc + 1) * 128],
                       ofc[:, kc, :], [FOC[kc], OFCB], start=(kc == 0), stop=(kc == 3))
                vop(lambda e, d=dc, p=pb, c0=c0: e.tensor_tensor(
                        sF[:, d, c0:c0 + WIN], p[:, 0:WIN],
                        sF[:, d, c0:c0 + WIN], ALU.add),
                    reads=[pbuf] + wsf, writes=wsf)
            if l == 0:
                tap(f"vt0_w{w}", vtc[:, 0], [VTCB[0]])
                tap(f"kf0_w{w}", kfc[:, 0], [KFCB[0]])

    fw1r = [fw1[l].rearrange("(c p) o -> p c o", p=128) for l in range(2)]

    def load_w1(l, osn, slot):
        o0, ow = OSR[osn]
        for part, off in ((0, o0), (1, HID + o0)):
            i = _wi[0] % 2
            _wi[0] += 1
            stg = fv("WST", 1024, 1024 * i).rearrange(
                "p (c o) -> p c o", c=NCH)[:, :, 0:ow]
            dmaQ(f"WST{i}", stg, fw1r[l][:, :, off:off + ow], writes=[WST[i]])
            eng = "G" if part == 0 else "V"
            op(eng, lambda e, s=slot, ow=ow, g=stg, part=part: e.tensor_copy(
                   w1v[s][:, :, part * 128:part * 128 + ow], g),
               reads=[WST[i]], writes=[W1B[slot]])

    def load_w2(l, osn, slot):
        o0, ow = OSR[osn]
        i = _wi[0] % 2
        _wi[0] += 1
        stg = fv("WST", 1024, 1024 * i)[0:ow, :]
        dmaQ(f"WST{i}", stg, fw2[l, o0:o0 + ow, :], writes=[WST[i]])
        eng = "G" if slot % 2 == 0 else "V"
        op(eng, lambda e, s=slot, ow=ow, g=stg: e.tensor_copy(w2v[s][0:ow, :], g),
           reads=[WST[i]], writes=[W2B[slot]])

    def stage_c_mlp(l, last):
        vop(lambda e: e.tensor_copy(ybnd[:, :, 0:1], sF[:, :, N // 2:N // 2 + 1]),
            reads=sf_bufs(N // 2, N // 2 + 1), writes=[YBND])

        def emit_ln4_stats(half, g):
            ln_stats_sf(1 + half * (N // 2) + g * 512, 512, shifted=True)

        def emit_ln4_apply(half, g):
            g0 = 1 + half * (N // 2) + g * 512
            bcast_rows(512)
            for c in range(NCH):
                jwarm(2)
                off = -1 if c >= 4 else 0
                ap = sF[:, c, g0 + off: g0 + off + 512]
                rb = sf_bufs(g0 + off, g0 + off + 512)
                vop(lambda e, a=ap: e.tensor_tensor(scr[:, 0:512], a,
                                                    PC[:, 0:512], ALU.mult),
                    reads=rb + [PCB], writes=[SCRB])
                vop(lambda e: e.tensor_tensor(scr[:, 512:1024], scr[:, 0:512],
                                              PD[:, 0:512], ALU.add),
                    reads=[SCRB, PDB], writes=[SCRB])
                sop(lambda e, c=c, g=g: e.activation(
                        ln4f[:, c, g * 512:(g + 1) * 512], scr[:, 512:1024],
                        AF.Identity, bias=pvcol(f"ln_b{l}", c),
                        scale=pvcol(f"ln_g{l}", c)),
                    reads=[SCRB, CONSTF], writes=[LN4C[g * NCH + c]])

        def w1_phase(tgs):
            for osn in range(NOS):
                slot = osn % 2
                o0, ow = OSR[osn]
                for ti, tg in enumerate(tgs):
                    par = (osn * 2 + ti) % 2
                    pa_, pab = (PC, PCB) if par == 0 else (PE, PEB)
                    pg_, pgb = (PD, PDB) if par == 0 else (PB, PBB)
                    for kc in range(NCH):
                        mm(pab, pa_[0:ow, 0:512], w1v[slot][:, kc, 0:ow],
                           ln4f[:, kc, tg * 512:(tg + 1) * 512],
                           [W1B[slot], LN4C[tg * NCH + kc]],
                           start=(kc == 0), stop=(kc == NCH - 1))
                        mm(pgb, pg_[0:ow, 0:512], w1v[slot][:, kc, 128:128 + ow],
                           ln4f[:, kc, tg * 512:(tg + 1) * 512],
                           [W1B[slot], LN4C[tg * NCH + kc]],
                           start=(kc == 0), stop=(kc == NCH - 1))
                    gslot = gel[0:ow, tg * 512:(tg + 1) * 512]
                    sop(lambda e, g=gslot, ow=ow, osn=osn, p=pg_: e.activation(
                            g, p[0:ow, 0:512], AF.Gelu,
                            bias=pvcol(f"b1g_{l}", osn, rows=ow)),
                        reads=[pgb, CONSTF], writes=[GATEB])
                    vop(lambda e, g=gslot, ow=ow, osn=osn, tg=tg, p=pa_:
                        e.scalar_tensor_tensor(
                            hg_regions[osn][0:ow, tg * 512:(tg + 1) * 512],
                            p[0:ow, 0:512], pvcol(f"b1a_{l}", osn, rows=ow),
                            g, ALU.add, ALU.mult),
                        reads=[pab, GATEB, CONSTF], writes=[HGB])
                if osn + 2 < NOS:
                    load_w1(l, osn + 2, slot)

        groups = [list(range(g, min(g + 4, NOS))) for g in range(0, NOS, 4)]

        def w2_phase(half, interleave):
            hb0 = 1 + half * (N // 2)
            for k, osn in enumerate(groups[0]):
                load_w2(l, osn, k)
            for gi, grp in enumerate(groups):
                if interleave and gi == 4:
                    emit_ln4_stats(1, 1)
                for tg in range(2):
                    g0 = hb0 + tg * 512
                    gsf = sf_bufs(g0, g0 + 512)
                    for dc in range(NCH):
                        pb, pbuf = [(PC, PCB), (PD, PDB), (PE, PEB),
                                    (PB, PBB)][dc % 4]
                        for k, osn in enumerate(grp):
                            o0, ow = OSR[osn]
                            sl = (gi % 2) * 4 + k
                            mm(pbuf, pb[:, 0:512],
                               w2v[sl][0:ow, dc * 128:(dc + 1) * 128],
                               hg_regions[osn][0:ow, tg * 512:(tg + 1) * 512],
                               [W2B[sl], HGB], start=(k == 0),
                               stop=(k == len(grp) - 1))
                        if gi == 0:
                            vop(lambda e, d=dc, p=pb, g0=g0: e.scalar_tensor_tensor(
                                    sF[:, d, g0:g0 + 512], p[:, 0:512],
                                    pvcol(f"b2_{l}", d), sF[:, d, g0:g0 + 512],
                                    ALU.add, ALU.add),
                                reads=[pbuf, CONSTF] + gsf, writes=gsf)
                        else:
                            vop(lambda e, d=dc, p=pb, g0=g0: e.tensor_tensor(
                                    sF[:, d, g0:g0 + 512], p[:, 0:512],
                                    sF[:, d, g0:g0 + 512], ALU.add),
                                reads=[pbuf] + gsf, writes=gsf)
                        if last and gi == len(groups) - 1:
                            xsl = dc % 2
                            dmaG(f"X2R{xsl}", fv("TAB", 512, xsl * 512),
                                 x2s[dc * 128:(dc + 1) * 128, g0 - 1:g0 - 1 + 512],
                                 reads=[B("X2SB")], writes=[X2R[xsl]])
                            vop(lambda e, d=dc, g0=g0, xsl=xsl: e.tensor_tensor(
                                    sF[:, d, g0:g0 + 512], sF[:, d, g0:g0 + 512],
                                    fv("TAB", 512, xsl * 512), ALU.add),
                                reads=[X2R[xsl]] + gsf, writes=gsf)
                            dmaG("OUT", outT[dc * 128:(dc + 1) * 128, g0 - 1:g0 - 1 + 512],
                                 sF[:, dc, g0:g0 + 512], reads=gsf)
                    if tg == 0 and gi + 1 < len(groups):
                        for k, osn in enumerate(groups[gi + 1]):
                            load_w2(l, osn, ((gi + 1) % 2) * 4 + k)

        load_w1(l, 0, 0)
        load_w1(l, 1, 1)
        for g in range(2):
            emit_ln4_stats(0, g)
            emit_ln4_apply(0, g)
        w1_phase((0, 1))
        w2_phase(0, interleave=True)
        emit_ln4_apply(1, 1)
        load_w1(l, 0, 0)
        load_w1(l, 1, 1)
        vop(lambda e: e.tensor_copy(ybnd[:, :, 1:2],
                                    sF[:, :, N // 2:N // 2 + 1]),
            reads=sf_bufs(N // 2, N // 2 + 1), writes=[YBND])
        vop(lambda e: e.tensor_copy(sF[:, :, N // 2:N // 2 + 1],
                                    ybnd[:, :, 0:1]),
            reads=[YBND], writes=sf_bufs(N // 2, N // 2 + 1))
        emit_ln4_stats(1, 0)
        emit_ln4_apply(1, 0)
        vop(lambda e: e.tensor_copy(sF[:, :, N // 2:N // 2 + 1],
                                    ybnd[:, :, 1:2]),
            reads=[YBND], writes=sf_bufs(N // 2, N // 2 + 1))
        w1_phase((1, 0))
        w2_phase(1, interleave=False)

    for l in range(2):
        stage_c_attn(l)
        kb.barrier()
        if l == 0:
            tap_sf("ya0")
            tap("qfc0", qfc, [QFCB])
            tap("kfc0", kfc, [KFCB[0], KFCB[1]])
            tap("vtc0", vtc, [VTCB[0], VTCB[1]])
            tap("ofc0", ofc, [OFCB])
        stage_c_mlp(l, last=(l == 1))
        kb.barrier()
        if l == 0:
            tap_sf("ym0")

    kb.finalize_blocks()
    kb.tap_t = tap_t
    return kb


_BUILD_CACHE = {}

def _get_kb():
    if "kb" not in _BUILD_CACHE:
        _BUILD_CACHE["kb"] = build_program()
    return _BUILD_CACHE["kb"]


def prep_common(inputs):
    consts = _host_constants()
    pv = _pack_pvec(inputs)
    qs = np.asarray(inputs["L0_ffn_q_scale"], np.float32)
    ks = np.asarray(inputs["L0_ffn_k_scale"], np.float32)
    qsc = np.zeros((2, 4, 128, HD), np.float32)
    for l in range(2):
        qsc[l, 0] = np.broadcast_to(qs[l], (128, HD))
        qsc[l, 1] = np.broadcast_to(np.concatenate([qs[l, 32:], qs[l, :32]]), (128, HD))
        qsc[l, 2] = np.broadcast_to(ks[l], (128, HD))
        qsc[l, 3] = np.broadcast_to(np.concatenate([ks[l, 32:], ks[l, :32]]), (128, HD))

    return dict(
        wqkvT=np.ascontiguousarray(np.asarray(inputs["L0_attn_w_qkv"], np.float32).T),
        wg1T=np.ascontiguousarray(np.asarray(inputs["L0_attn_w_g1"], np.float32).T),
        wg2T=np.ascontiguousarray(np.asarray(inputs["L0_attn_w_g2"], np.float32).T),
        woutT=np.ascontiguousarray(np.asarray(inputs["L0_attn_w_out"], np.float32).T),
        fqkv=np.ascontiguousarray(np.asarray(inputs["L0_ffn_wqkv"], np.float32)),
        fwo=np.ascontiguousarray(np.asarray(inputs["L0_ffn_wo"], np.float32)),
        fw1=np.ascontiguousarray(np.asarray(inputs["L0_ffn_w1"], np.float32)),
        fw2=np.ascontiguousarray(np.asarray(inputs["L0_ffn_w2"], np.float32)),
        qscale=qsc, pvec=pv,
        tabs=consts["tabs"], tabslo=consts["tabslo"], ml=consts["ml"], mh=consts["mh"],
        maskb=consts["maskb"], eye_f=consts["eye_f"], eye_b=consts["eye_b"],
        ones_f=consts["ones_f"], ones_b=consts["ones_b"],
    )


def kernel(**inputs):
    common = prep_common(inputs)
    x = np.asarray(inputs["L0_attn_x"], np.float32)
    kb = _get_kb()
    in_maps = []
    for b in range(8):
        m = dict(common)
        m["xT"] = np.ascontiguousarray(x[b].T)
        in_maps.append(m)
    res = run_bass_kernel_spmd(kb.nc, in_maps, core_ids=list(range(8)))
    out = np.stack([np.ascontiguousarray(r["outT"].T) for r in res.results])
    return out.astype(np.float32)
